# version 04
# baseline (speedup 1.0000x reference)
import numpy as np
from contextlib import ExitStack, contextmanager
import concourse.bass as bass
import concourse.mybir as mybir
from concourse.bass_utils import run_bass_kernel_spmd
from concourse.alu_op_type import AluOpType as ALU

AF = mybir.ActivationFunctionType
F32 = mybir.dt.float32
BF16 = mybir.dt.bfloat16
I32 = mybir.dt.int32
AX = mybir.AxisListType.X

NCORES = 8
T = 1024
D = 2048
ND = D // 128
NT = T // 128
HF = 5120
NHF = HF // 128
NH = 16
CH = 64
SC = 32
NCH = T // CH
LN_EPS = 1e-5
GLA_SCALE = 128.0 ** -0.5
SWA_SCALE = 128.0 ** -0.5
SWA_SHIFT = 12.0

CONST_LIST = [LN_EPS, 1.0, -SWA_SHIFT, 0.0]



class Sem:
    __slots__ = ("h", "n")

    def __init__(self, h):
        self.h = h
        self.n = 0

    def inc(self, instr, k=16):
        instr.then_inc(self.h, k)
        self.n += k
        return self.n


class PendingInstr:
    __slots__ = ("incs",)

    def __init__(self):
        self.incs = []

    def then_inc(self, sem_h, n):
        self.incs.append((sem_h, n))
        return self


class EngRec:
    __slots__ = ("k", "name")

    def __init__(self, k, name):
        self.k = k
        self.name = name

    def __getattr__(self, meth):
        q = self.k.q[self.name]

        def call(*a, **kw):
            p = PendingInstr()
            q.append((meth, a, kw, p))
            return p
        return call


class Kern:
    ENGS = ("tensor", "vector", "scalar", "sync", "gpsimd")

    def __init__(self):
        self.nc = bass.Bass("TRN2", target_bir_lowering=False)
        self.es = ExitStack()
        self.cur_es = self.es
        self.block = None
        self._uid = 0
        self.q = {e: [] for e in self.ENGS}

    def uid(self):
        self._uid += 1
        return self._uid

    def dram(self, name, shape, dt=F32, kind="ExternalInput"):
        return self.nc.dram_tensor(name, list(shape), dt, kind=kind).ap()

    def sb(self, name, shape, dt=F32):
        return self.cur_es.enter_context(
            self.nc.sbuf_tensor(f"{name}_{self.uid()}", list(shape), dt))

    def ps(self, name, shape, dt=F32):
        return self.cur_es.enter_context(
            self.nc.psum_tensor(f"{name}_{self.uid()}", list(shape), dt))

    def sem(self, name):
        return Sem(self.es.enter_context(self.nc.semaphore(f"{name}_{self.uid()}")))

    @contextmanager
    def scope(self):
        es = ExitStack()
        old = self.cur_es
        self.cur_es = es
        try:
            yield
        finally:
            self.cur_es = old
            es.close()

    def start(self):
        self.block = self.es.enter_context(self.nc.Block())
        self.bar = self.sem("g_bar")
        self.dma = self.sem("g_dma")
        self.junk = self.sb("junk", [128, 1536], F32)

    def eng(self, name):
        return EngRec(self, name)

    def flush(self):
        for name in self.ENGS:
            queue = self.q[name]
            if not queue:
                continue

            def runner(e, queue=queue):
                for meth, a, kw, p in queue:
                    ins = getattr(e, meth)(*a, **kw)
                    for sem_h, n in p.incs:
                        ins.then_inc(sem_h, n)
            getattr(self.block, name)(runner)
            self.q[name] = []

    def barrier(self, engines=None):
        engines = engines or self.ENGS
        n = len(engines)
        base = self.bar.n
        for name in engines:
            e = self.eng(name)
            if name == "vector":
                e.memset(self.junk[:, 0:512], 0.0).then_inc(self.bar.h, 1)
            elif name == "scalar":
                e.memzero(self.junk[:, 512:1024]).then_inc(self.bar.h, 1)
            elif name == "gpsimd":
                e.memset(self.junk[:, 1024:1536], 0.0).then_inc(self.bar.h, 1)
            else:
                e.nop().then_inc(self.bar.h, 1)
        self.bar.n += n
        for name in engines:
            self.eng(name).wait_ge(self.bar.h, base + n)

    def finish(self):
        self.flush()
        self.es.close()
        return self.nc



def prep_w_fmaj(W, kblk=None):
    Din, Dout = W.shape
    nd, nf = Din // 128, Dout // 128
    return np.ascontiguousarray(
        W.reshape(nd, 128, nf, 128).transpose(2, 1, 0, 3).reshape(nf, 128, Din))


def prep_w_dmaj(W):
    Din, Dout = W.shape
    return np.ascontiguousarray(W.reshape(Din // 128, 128, Dout))


def host_common_inputs():
    import ml_dtypes
    return {
        "cpool": np.tile(np.array(CONST_LIST, np.float32), (128, 1)),
        "identF": np.eye(128, dtype=np.float32),
        "identB": np.eye(128).astype(ml_dtypes.bfloat16),
        "onesB": np.ones((128, 128), ml_dtypes.bfloat16),
        "onesF": np.ones((1, 128), np.float32),
    }


def declare_common_inputs(k):
    k.in_cpool = k.dram("cpool", [128, len(CONST_LIST)], F32)
    k.in_identF = k.dram("identF", [128, 128], F32)
    k.in_identB = k.dram("identB", [128, 128], BF16)
    k.in_onesB = k.dram("onesB", [128, 128], BF16)
    k.in_onesF = k.dram("onesF", [1, 128], F32)


def emit_common(k):
    k.cpool = k.sb("cpool", [128, len(CONST_LIST)])
    for i, v in enumerate(CONST_LIST):
        k.nc.const_aps.register(F32, float(v), k.cpool[:, i:i + 1])
    k.identF = k.sb("identF", [128, 128], F32)
    k.identB = k.sb("identB", [128, 128], BF16)
    k.ones_col = k.sb("ones_col", [128, 1], BF16)
    k.ones_row = k.sb("ones_row", [1, 128], BF16)
    k.ones_rowF = k.sb("ones_rowF", [1, 128], F32)
    s = k.eng("sync")
    k.dma.inc(s.dma_start(out=k.cpool[:], in_=k.in_cpool[:]))
    k.dma.inc(s.dma_start(out=k.identF[:], in_=k.in_identF[:]))
    k.dma.inc(s.dma_start(out=k.identB[:], in_=k.in_identB[:]))
    k.dma.inc(s.dma_start(out=k.ones_col[:], in_=k.in_onesB[0:128, 0:1],
                          allow_slow_non_contiguous=True))
    k.dma.inc(s.dma_start(out=k.ones_row[:], in_=k.in_onesB[0:1, 0:128]))
    k.dma.inc(s.dma_start(out=k.ones_rowF[:], in_=k.in_onesF[0:1, 0:128]))
    s.wait_ge(k.dma.h, k.dma.n)
    k.barrier()



def emit_in_transpose(k, x_in):
    with k.scope():
        xin = k.sb("xin_stage", [128, NT * D], F32)
        s = k.eng("sync")
        for tt in range(NT):
            k.dma.inc(s.dma_start(out=xin[:, tt * D:(tt + 1) * D],
                                  in_=x_in[tt * 128:(tt + 1) * 128, :]))
        dma_done = k.dma.n

        te = k.eng("tensor")
        v = k.eng("vector")
        te.wait_ge(k.dma.h, dma_done)
        tp = [k.ps("tp_in", [128, 512], F32) for _ in range(2)]
        s_tp = k.sem("s_tp_in")
        s_ev = k.sem("s_ev_in")
        grp = 0
        for dc in range(ND):
            for tg in range(2):
                pb = tp[grp % 2]
                if grp >= 2:
                    te.wait_ge(s_ev.h, grp - 1)
                for j in range(4):
                    tt = tg * 4 + j
                    ins = te.transpose(
                        pb[:, j * 128:(j + 1) * 128],
                        xin[:, tt * D + dc * 128: tt * D + (dc + 1) * 128],
                        k.identF[:])
                s_tp.inc(ins, 1)
                v.wait_ge(s_tp.h, grp + 1)
                ins = v.tensor_copy(
                    k.xT[:, dc * T + tg * 512: dc * T + (tg + 1) * 512], pb[:])
                s_ev.inc(ins, 1)
                grp += 1
        k.barrier()


def emit_out_transpose(k, x_out):
    with k.scope():
        xout = k.sb("xout_stage", [128, NT * D], F32)
        te = k.eng("tensor")
        v = k.eng("vector")
        tp = [k.ps("tp_out", [128, 512], F32) for _ in range(2)]
        s_tp = k.sem("s_tp_out")
        s_ev = k.sem("s_ev_out")
        grp = 0
        for tt in range(NT):
            for dg in range(4):
                pb = tp[grp % 2]
                if grp >= 2:
                    te.wait_ge(s_ev.h, grp - 1)
                for j in range(4):
                    dc = dg * 4 + j
                    ins = te.transpose(
                        pb[:, j * 128:(j + 1) * 128],
                        k.xT[:, dc * T + tt * 128: dc * T + (tt + 1) * 128],
                        k.identF[:])
                s_tp.inc(ins, 1)
                v.wait_ge(s_tp.h, grp + 1)
                ins = v.tensor_copy(
                    xout[:, tt * D + dg * 512: tt * D + (dg + 1) * 512], pb[:])
                s_ev.inc(ins, 1)
                grp += 1
        s = k.eng("sync")
        s.wait_ge(s_ev.h, grp)
        for tt in range(NT):
            k.dma.inc(s.dma_start(out=x_out[tt * 128:(tt + 1) * 128, :],
                                  in_=xout[:, tt * D:(tt + 1) * D]))
        s.wait_ge(k.dma.h, k.dma.n)
        k.barrier()



def emit_ln(k, xn):
    with k.scope():
        xbc = [k.sb("ln_xbc", [128, T], BF16) for _ in range(2)]
        sqc = [k.sb("ln_sqc", [128, T], BF16) for _ in range(2)]
        v = k.eng("vector")
        a = k.eng("scalar")
        te = k.eng("tensor")
        p_s1 = k.ps("ln_ps1", [1, T], F32)
        p_s2 = k.ps("ln_ps2", [1, T], F32)
        s_sq = k.sem("ln_s_sq")
        s_mm = k.sem("ln_s_mm")
        for dc in range(ND):
            if dc >= 2:
                v.wait_ge(s_mm.h, dc - 1)
                a.wait_ge(s_mm.h, dc - 1)
            ins = v.tensor_copy(xbc[dc % 2][:], k.xT[:, dc * T:(dc + 1) * T])
            s_sq.inc(ins, 1)
            ins = a.activation(sqc[dc % 2][:], k.xT[:, dc * T:(dc + 1) * T],
                               AF.Square)
            s_sq.inc(ins, 1)
            te.wait_ge(s_sq.h, 2 * (dc + 1))
            for half in range(2):
                hs = slice(half * 512, (half + 1) * 512)
                te.matmul(p_s1[:, hs], k.ones_col[:], xbc[dc % 2][:, hs],
                          start=(dc == 0), stop=(dc == ND - 1))
                ins = te.matmul(p_s2[:, hs], k.ones_col[:], sqc[dc % 2][:, hs],
                                start=(dc == 0), stop=(dc == ND - 1))
            s_mm.inc(ins, 1)

        m_row = k.sb("ln_m", [1, T], F32)
        var_row = k.sb("ln_var", [1, T], F32)
        msq_row = k.sb("ln_msq", [1, T], F32)
        r_row = k.sb("ln_r", [1, T], F32)
        mr_row = k.sb("ln_mr", [1, T], F32)
        s_st = k.sem("ln_s_st")
        v.wait_ge(s_mm.h, s_mm.n)
        v.tensor_scalar_mul(m_row[:], p_s1[:], 1.0 / D)
        v.tensor_scalar_mul(var_row[:], p_s2[:], 1.0 / D)
        v.tensor_tensor(msq_row[:], m_row[:], m_row[:], ALU.mult)
        ins = v.tensor_tensor(var_row[:], var_row[:], msq_row[:], ALU.subtract)
        s_st.inc(ins, 1)
        a.wait_ge(s_st.h, s_st.n)
        a.activation(r_row[:], var_row[:], AF.Ln, bias=LN_EPS)
        ins = a.activation(r_row[:], r_row[:], AF.Exp, scale=-0.5)
        s_st.inc(ins, 1)
        v.wait_ge(s_st.h, s_st.n)
        ins = v.tensor_tensor(mr_row[:], m_row[:], r_row[:], ALU.mult)
        s_st.inc(ins, 1)

        p_rb = k.ps("ln_prb", [128, 512], F32)
        p_mrb = k.ps("ln_pmrb", [128, 512], F32)
        rb = k.sb("ln_rb", [128, T], F32)
        mrb = k.sb("ln_mrb", [128, T], F32)
        s_bc = k.sem("ln_s_bc")
        te.wait_ge(s_st.h, s_st.n)
        for half in range(2):
            hs = slice(half * 512, (half + 1) * 512)
            te.matmul(p_rb[:], k.ones_rowF[:], r_row[:, hs], start=True, stop=True)
            ins = te.matmul(p_mrb[:], k.ones_rowF[:], mr_row[:, hs],
                            start=True, stop=True)
            s_bc.inc(ins, 1)
            v.wait_ge(s_bc.h, s_bc.n)
            v.tensor_copy(rb[:, hs], p_rb[:])
            ins = v.tensor_copy(mrb[:, hs], p_mrb[:])
            s_bc.inc(ins, 1)
            te.wait_ge(s_bc.h, s_bc.n)
        tmp = k.sb("ln_tmp", [128, T], F32)
        for dc in range(ND):
            v.tensor_tensor(tmp[:], k.xT[:, dc * T:(dc + 1) * T], rb[:],
                            ALU.mult)
            v.tensor_tensor(xn[:, dc * T:(dc + 1) * T], tmp[:], mrb[:],
                            ALU.subtract)
        k.barrier()



def emit_absmean(k, wfct, wprojt, am_out):
    k.thr_fc = k.sb("thr_fc", [128, 1], F32)
    k.thr_proj = k.sb("thr_proj", [128, 1], F32)
    k.ffn_oscale = k.sb("ffn_oscale", [128, 1], F32)
    sc_dram = k.dram(f"am_scratch_{k.uid()}", [1, 4], F32, kind="Internal")
    with k.scope():
        wbuf = [k.sb("am_w", [128, D], F32) for _ in range(2)]
        wb2 = [k.sb("am_w2", [128, HF // 2], F32) for _ in range(2)]
        reds1 = k.sb("am_reds1", [128, NHF], F32)
        reds2 = k.sb("am_reds2", [128, 2 * ND], F32)
        acc = k.sb("am_acc", [128, 2], F32)
        p_t = k.ps("am_pt", [2, 128], F32)
        tot = k.sb("am_tot", [2, 1], F32)
        s = k.eng("sync")
        v = k.eng("vector")
        s_r = k.sem("am_s_r")
        dma0 = k.dma.n
        for fb in range(NHF):
            k.dma.inc(s.dma_start(out=wbuf[fb % 2][:], in_=wfct[fb]))
            if fb >= 1:
                s.wait_ge(s_r.h, fb)
        for fb in range(NHF):
            v.wait_ge(k.dma.h, dma0 + 16 * (fb + 1))
            ins = v.reduce_sum(reds1[:, fb:fb + 1], wbuf[fb % 2][:], axis=AX,
                               apply_absolute_value=True)
            s_r.inc(ins, 1)
        dma1 = k.dma.n
        for fb in range(2 * ND):
            k.dma.inc(s.dma_start(
                out=wb2[fb % 2][:],
                in_=wprojt[fb // 2][:, (fb % 2) * (HF // 2):(fb % 2 + 1) * (HF // 2)]))
            if fb >= 1:
                s.wait_ge(s_r.h, NHF + fb)
        for fb in range(2 * ND):
            v.wait_ge(k.dma.h, dma1 + 16 * (fb + 1))
            ins = v.reduce_sum(reds2[:, fb:fb + 1], wb2[fb % 2][:], axis=AX,
                               apply_absolute_value=True)
            s_r.inc(ins, 1)
        v.wait_ge(s_r.h, s_r.n)
        v.reduce_sum(acc[:, 0:1], reds1[:], axis=AX)
        ins = v.reduce_sum(acc[:, 1:2], reds2[:], axis=AX)
        s_r.inc(ins, 1)
        te = k.eng("tensor")
        s_mm = k.sem("am_s_mm")
        te.wait_ge(s_r.h, s_r.n)
        ins = te.transpose(p_t[:], acc[:], k.identF[:])
        s_mm.inc(ins, 1)
        v.wait_ge(s_mm.h, 1)
        ins = v.reduce_sum(tot[:], p_t[:], axis=AX)
        s_mm.inc(ins, 1)
        v.wait_ge(s_mm.h, 2)
        thr2w = k.sb("am_thr2w", [2, 128], F32)
        am2w = k.sb("am_am2w", [2, 128], F32)
        v.tensor_copy(k.junk[:, 0:512], k.junk[:, 0:512])
        v.tensor_scalar_mul(thr2w[:], tot[:].broadcast_to([2, 128]),
                            0.7 / (HF * D))
        ins = v.tensor_scalar_mul(am2w[:], tot[:].broadcast_to([2, 128]),
                                  1.0 / (HF * D))
        s_mm.inc(ins, 1)
        s = k.eng("sync")
        s.wait_ge(s_mm.h, 3)
        k.dma.inc(s.dma_start(out=sc_dram[0:1, 0:1], in_=thr2w[0:1, 0:1]))
        k.dma.inc(s.dma_start(out=sc_dram[0:1, 1:2], in_=thr2w[1:2, 0:1]))
        k.dma.inc(s.dma_start(out=sc_dram[0:1, 2:3], in_=am2w[0:1, 0:1]))
        k.dma.inc(s.dma_start(out=sc_dram[0:1, 3:4], in_=am2w[1:2, 0:1]))
        k.dma.inc(s.dma_start(out=am_out[0:1, 0:1], in_=am2w[0:1, 0:1]))
        k.dma.inc(s.dma_start(out=am_out[0:1, 1:2], in_=am2w[1:2, 0:1]))
        s.wait_ge(k.dma.h, k.dma.n)
        amrow = k.sb("am_row", [1, 4], F32)
        oscw = k.sb("am_oscw", [1, 128], F32)
        k.dma.inc(s.dma_start(out=amrow[:], in_=sc_dram[:]))
        s.wait_ge(k.dma.h, k.dma.n)
        s_f = k.sem("am_s_f")
        v.wait_ge(k.dma.h, k.dma.n)
        v.scalar_tensor_tensor(oscw[:], amrow[:, 2:3].broadcast_to([1, 128]),
                               0.5, amrow[:, 3:4].broadcast_to([1, 128]),
                               ALU.mult, ALU.mult)
        ins = v.tensor_tensor(oscw[:], oscw[:],
                              amrow[:, 2:3].broadcast_to([1, 128]), ALU.mult)
        s_f.inc(ins, 1)
        s.wait_ge(s_f.h, 1)
        k.dma.inc(s.dma_start(out=sc_dram[0:1, 2:3], in_=oscw[0:1, 0:1]))
        s.wait_ge(k.dma.h, k.dma.n)
        bsrc = sc_dram.broadcast_to([128, 4])
        k.dma.inc(s.dma_start(out=k.thr_fc[:], in_=bsrc[:, 0:1],
                              allow_slow_non_contiguous=True))
        k.dma.inc(s.dma_start(out=k.thr_proj[:], in_=bsrc[:, 1:2],
                              allow_slow_non_contiguous=True))
        k.dma.inc(s.dma_start(out=k.ffn_oscale[:], in_=bsrc[:, 2:3],
                              allow_slow_non_contiguous=True))
        s.wait_ge(k.dma.h, k.dma.n)
        k.barrier()


def emit_ffn0(k, wfct, wprojt, hdram, qfc_out=None, qproj_out=None):
    with k.scope():
        emit_ffn0_fc_only(k, wfct, hdram, qfc_out=qfc_out)
        emit_ffn0_proj_only(k, wprojt, hdram, qproj_out=qproj_out)


def emit_ffn0_fc_only(k, wfct, hdram, xn_spill=None, qfc_out=None):
    if True:
        with k.scope():
            xn = k.sb("ffn_xn", [128, ND * T], BF16)
            emit_ln(k, xn)
            if xn_spill is not None:
                sdbg = k.eng("sync")
                k.dma.inc(sdbg.dma_start(out=xn_spill[:], in_=xn[:]))
                sdbg.wait_ge(k.dma.h, k.dma.n)
                k.barrier()
            wraw = [k.sb("ffn_wraw", [128, D], F32) for _ in range(2)]
            wq = [k.sb("ffn_wq", [128, D], BF16) for _ in range(2)]
            sc_abs = k.sb("ffn_scabs", [128, D], F32)
            sc_m = k.sb("ffn_scm", [128, D], BF16)
            sc_sgn = k.sb("ffn_scsgn", [128, D], BF16)
            relu_f32 = k.sb("ffn_relu", [128, T], F32)
            hstage = [k.sb("ffn_hstage", [128, 512], BF16) for _ in range(2)]
            p_h = [k.ps("ffn_ph", [128, 512], F32) for _ in range(4)]
            s_q = k.sem("ffn_s_q")
            s_sgn = k.sem("ffn_s_sgn")
            s_mm = k.sem("ffn_s_mm")
            s_h = k.sem("ffn_s_h")
            s_hd = k.sem("ffn_s_hd")
            s_qs = k.sem("ffn_s_qs")
            s = k.eng("sync")
            g = k.eng("gpsimd")
            v = k.eng("vector")
            a = k.eng("scalar")
            te = k.eng("tensor")
            dma0 = k.dma.n
            for fb in range(NHF):
                k.dma.inc(s.dma_start(out=wraw[fb % 2][:], in_=wfct[fb]))
                if fb >= 1:
                    s.wait_ge(s_q.h, fb)
            for fb in range(NHF):
                wr, wqt = wraw[fb % 2], wq[fb % 2]
                v.wait_ge(k.dma.h, dma0 + 16 * (fb + 1))
                if fb >= 2:
                    v.wait_ge(s_mm.h, 2 * (fb - 1))
                    if qfc_out is not None:
                        v.wait_ge(s_qs.h, 16 * (fb - 1))
                a.wait_ge(k.dma.h, dma0 + 16 * (fb + 1))
                if fb >= 1:
                    a.wait_ge(s_q.h, fb)
                a.activation(sc_abs[:], wr[:], AF.Abs)
                ins = a.activation(sc_sgn[:], wr[:], AF.Sign)
                s_sgn.inc(ins, 1)
                v.wait_ge(s_sgn.h, s_sgn.n)
                v.tensor_scalar(sc_m[:], sc_abs[:], k.thr_fc[:], None, ALU.is_gt)
                v.tensor_tensor(wqt[:], sc_m[:], sc_sgn[:], ALU.mult)
                ins = v.tensor_copy(k.junk[:, 0:512], k.junk[:, 0:512])
                s_q.inc(ins, 1)
                if qfc_out is not None:
                    g.wait_ge(s_q.h, fb + 1)
                    s_qs.inc(g.dma_start(out=qfc_out[fb], in_=wqt[:]))

                te.wait_ge(s_q.h, fb + 1)
                for half in range(2):
                    idx = 2 * fb + half
                    pb = p_h[idx % 4]
                    if idx >= 4:
                        te.wait_ge(s_h.h, idx - 3)
                    for dc in range(ND):
                        ins = te.matmul(
                            pb[:], wqt[:, dc * 128:(dc + 1) * 128],
                            xn[:, dc * T + half * 512: dc * T + (half + 1) * 512],
                            start=(dc == 0), stop=(dc == ND - 1))
                    s_mm.inc(ins, 1)
                    a.wait_ge(s_mm.h, idx + 1)
                    if idx >= 2:
                        a.wait_ge(s_h.h, idx - 1)
                    ins = a.activation(relu_f32[:, half * 512:(half + 1) * 512],
                                       pb[:], AF.Relu)
                    s_sgn.inc(ins, 1)
                    v.wait_ge(s_sgn.h, s_sgn.n)
                    if idx >= 2:
                        v.wait_ge(s_hd.h, 16 * (idx - 1))
                    ins = v.tensor_tensor(
                        hstage[idx % 2][:],
                        relu_f32[:, half * 512:(half + 1) * 512],
                        relu_f32[:, half * 512:(half + 1) * 512], ALU.mult)
                    s_h.inc(ins, 1)
                    g.wait_ge(s_h.h, idx + 1)
                    s_hd.inc(g.dma_start(
                        out=hdram[fb][:, half * 512:(half + 1) * 512],
                        in_=hstage[idx % 2][:]))
            g.wait_ge(s_hd.h, s_hd.n)
            k.barrier()


def emit_ffn0_proj_only(k, wprojt, hdram, qproj_out=None):
    if True:
        with k.scope():
            GRP = 4
            NG = ND // GRP
            HH = HF // 2
            wraw2 = [k.sb("ffn_w2raw", [128, HH], F32) for _ in range(2)]
            wqg = [[k.sb("ffn_w2q", [128, HF], BF16) for _ in range(GRP)]
                   for _ in range(2)]
            sc2_abs = k.sb("ffn_sc2abs", [128, HH], F32)
            sc2_m = k.sb("ffn_sc2m", [128, HH], BF16)
            sc2_sgn = k.sb("ffn_sc2sgn", [128, HH], BF16)
            o_f32 = [k.sb("ffn_of32", [128, 512], F32) for _ in range(2)]
            htile = [k.sb("ffn_htile", [128, T], BF16) for _ in range(2)]
            s_add = k.sem("ffn_s_add")
            s_q2 = k.sem("ffn_s_q2")
            s_sgn2 = k.sem("ffn_s_sgn2")
            s_mm2 = k.sem("ffn_s_mm2")
            s_hin = k.sem("ffn_s_hin")
            s_qs2 = k.sem("ffn_s_qs2")
            s = k.eng("sync")
            g = k.eng("gpsimd")
            v = k.eng("vector")
            a = k.eng("scalar")
            te = k.eng("tensor")
            wdma0 = k.dma.n
            for hb in range(2 * ND):
                k.dma.inc(s.dma_start(
                    out=wraw2[hb % 2][:],
                    in_=wprojt[hb // 2][:, (hb % 2) * HH:(hb % 2 + 1) * HH]))
                if hb >= 1:
                    s.wait_ge(s_q2.h, hb)
            for gi in range(NG):
                for hc in range(NHF):
                    j = gi * NHF + hc
                    if j >= 2:
                        g.wait_ge(s_mm2.h, j - 1)
                    ins = g.dma_start(out=htile[j % 2][:], in_=hdram[hc][:])
                    s_hin.inc(ins, 16)
            def emit_tern(gi):
                for fi in range(GRP):
                    fb = gi * GRP + fi
                    for piece in range(2):
                        hb = 2 * fb + piece
                        wr = wraw2[hb % 2]
                        wqt = wqg[gi % 2][fi]
                        v.wait_ge(k.dma.h, wdma0 + 16 * (hb + 1))
                        if gi >= 2 and fi == 0 and piece == 0:
                            v.wait_ge(s_mm2.h, (gi - 1) * NHF)
                        if qproj_out is not None and fb >= GRP:
                            v.wait_ge(s_qs2.h, 16 * (fb - GRP + 1))
                        a.wait_ge(k.dma.h, wdma0 + 16 * (hb + 1))
                        if hb >= 1:
                            a.wait_ge(s_q2.h, hb)
                        a.activation(sc2_abs[:], wr[:], AF.Abs)
                        ins = a.activation(sc2_sgn[:], wr[:], AF.Sign)
                        s_sgn2.inc(ins, 1)
                        v.wait_ge(s_sgn2.h, s_sgn2.n)
                        v.tensor_scalar(sc2_m[:], sc2_abs[:], k.thr_proj[:],
                                        None, ALU.is_gt)
                        v.tensor_tensor(wqt[:, piece * HH:(piece + 1) * HH],
                                        sc2_m[:], sc2_sgn[:], ALU.mult)
                        ins = v.tensor_copy(k.junk[:, 0:512], k.junk[:, 0:512])
                        s_q2.inc(ins, 1)
                    if qproj_out is not None:
                        a.wait_ge(s_q2.h, 2 * fb + 2)
                        s_qs2.inc(a.dma_start(out=qproj_out[fb],
                                              in_=wqg[gi % 2][fi][:]))
            emit_tern(0)
            if NG > 1:
                emit_tern(1)
            for gi in range(NG):
                te.wait_ge(s_q2.h, (gi + 1) * GRP * 2)
                if gi >= 1:
                    te.wait_ge(s_add.h, gi * 2 * GRP)
                with k.scope():
                    p_o = [k.ps("ffn_po", [128, T], F32) for _ in range(GRP)]
                    for hc in range(NHF):
                        j = gi * NHF + hc
                        te.wait_ge(s_hin.h, 16 * (j + 1))
                        for fi in range(GRP):
                            for half in range(2):
                                ins = te.matmul(
                                    p_o[fi][:, half * 512:(half + 1) * 512],
                                    wqg[gi % 2][fi][:, hc * 128:(hc + 1) * 128],
                                    htile[j % 2][:, half * 512:(half + 1) * 512],
                                    start=(hc == 0), stop=(hc == NHF - 1))
                        s_mm2.inc(ins, 1)
                    for fi in range(GRP):
                        fb = gi * GRP + fi
                        for half in range(2):
                            hi = 2 * (gi * GRP + fi) + half
                            ob = o_f32[hi % 2]
                            a.wait_ge(s_mm2.h, (gi + 1) * NHF)
                            if hi >= 2:
                                a.wait_ge(s_add.h, hi - 1)
                            ins = a.activation(
                                ob[:],
                                p_o[fi][:, half * 512:(half + 1) * 512],
                                AF.Copy, scale=k.ffn_oscale[:])
                            s_sgn2.inc(ins, 1)
                            v.wait_ge(s_sgn2.h, s_sgn2.n)
                            xsl = slice(fb * T + half * 512,
                                        fb * T + (half + 1) * 512)
                            ins = v.tensor_tensor(k.xT[:, xsl], k.xT[:, xsl],
                                                  ob[:], ALU.add)
                            s_add.inc(ins, 1)
                if gi + 2 < NG:
                    emit_tern(gi + 2)
            k.barrier()


def emit_ffn_scales(k, am_in):
    k.ffn_oscale = k.sb("ffn_oscale2", [128, 1], F32)
    sc_dram = k.dram(f"fsc_scratch_{k.uid()}", [1, 1], F32, kind="Internal")
    with k.scope():
        amrow = k.sb("fsc_amrow", [1, 2], F32)
        oscw = k.sb("fsc_oscw", [1, 128], F32)
        s = k.eng("sync")
        v = k.eng("vector")
        k.dma.inc(s.dma_start(out=amrow[:], in_=am_in[:]))
        s.wait_ge(k.dma.h, k.dma.n)
        s_f = k.sem("fsc_s_f")
        v.wait_ge(k.dma.h, k.dma.n)
        v.scalar_tensor_tensor(oscw[:], amrow[:, 0:1].broadcast_to([1, 128]),
                               0.5, amrow[:, 1:2].broadcast_to([1, 128]),
                               ALU.mult, ALU.mult)
        ins = v.tensor_tensor(oscw[:], oscw[:],
                              amrow[:, 0:1].broadcast_to([1, 128]), ALU.mult)
        s_f.inc(ins, 1)
        s.wait_ge(s_f.h, 1)
        k.dma.inc(s.dma_start(out=sc_dram[0:1, 0:1], in_=oscw[0:1, 0:1]))
        s.wait_ge(k.dma.h, k.dma.n)
        k.dma.inc(s.dma_start(out=k.ffn_oscale[:],
                              in_=sc_dram.broadcast_to([128, 1]),
                              allow_slow_non_contiguous=True))
        s.wait_ge(k.dma.h, k.dma.n)
        k.barrier()


def emit_ffn0_pre(k, qfc_in, qproj_in, hdram):
    with k.scope():
        with k.scope():
            xn = k.sb("ffp_xn", [128, ND * T], BF16)
            emit_ln(k, xn)
            wq = [k.sb("ffp_wq", [128, D], BF16) for _ in range(2)]
            relu_f32 = k.sb("ffp_relu", [128, T], F32)
            hstage = [k.sb("ffp_hstage", [128, T], BF16) for _ in range(2)]
            p_h = [k.ps("ffp_ph", [128, 512], F32) for _ in range(4)]
            s_mm = k.sem("ffp_s_mm")
            s_h = k.sem("ffp_s_h")
            s_hd = k.sem("ffp_s_hd")
            s = k.eng("sync")
            g = k.eng("gpsimd")
            v = k.eng("vector")
            a = k.eng("scalar")
            te = k.eng("tensor")
            dma0 = k.dma.n
            for fb in range(NHF):
                k.dma.inc(s.dma_start(out=wq[fb % 2][:], in_=qfc_in[fb]))
                if fb >= 1:
                    s.wait_ge(s_mm.h, 2 * fb)
            for fb in range(NHF):
                te.wait_ge(k.dma.h, dma0 + 16 * (fb + 1))
                for half in range(2):
                    idx = 2 * fb + half
                    pb = p_h[idx % 4]
                    if idx >= 4:
                        te.wait_ge(s_h.h, idx - 3)
                    for dc in range(ND):
                        ins = te.matmul(
                            pb[:], wq[fb % 2][:, dc * 128:(dc + 1) * 128],
                            xn[:, dc * T + half * 512: dc * T + (half + 1) * 512],
                            start=(dc == 0), stop=(dc == ND - 1))
                    s_mm.inc(ins, 1)
                    a.wait_ge(s_mm.h, idx + 1)
                    if idx >= 2:
                        a.wait_ge(s_h.h, idx - 1)
                    ins = a.activation(relu_f32[:, half * 512:(half + 1) * 512],
                                       pb[:], AF.Relu)
                    k.sq3 = getattr(k, "sq3", None) or k.sem("ffp_s_sq3")
                    k.sq3.inc(ins, 1)
                    v.wait_ge(k.sq3.h, k.sq3.n)
                    if fb >= 2 and half == 0:
                        v.wait_ge(s_hd.h, 16 * (fb - 1))
                    ins = v.tensor_tensor(
                        hstage[fb % 2][:, half * 512:(half + 1) * 512],
                        relu_f32[:, half * 512:(half + 1) * 512],
                        relu_f32[:, half * 512:(half + 1) * 512], ALU.mult)
                    s_h.inc(ins, 1)
                g.wait_ge(s_h.h, 2 * fb + 2)
                s_hd.inc(g.dma_start(out=hdram[fb][:], in_=hstage[fb % 2][:]))
            g.wait_ge(s_hd.h, s_hd.n)
            k.barrier()

        with k.scope():
            GRP = 4
            NG = ND // GRP
            HB = 4
            wqg = [[k.sb("ffp_w2q", [128, HF], BF16) for _ in range(GRP)]
                   for _ in range(2)]
            o_f32 = [k.sb("ffp_of32", [128, 512], F32) for _ in range(2)]
            htile = [k.sb("ffp_htile", [128, HB * T], BF16) for _ in range(2)]
            s_add = k.sem("ffp_s_add")
            s_mm2 = k.sem("ffp_s_mm2")
            s_ev = k.sem("ffp_s_ev")
            s_hin = k.sem("ffp_s_hin")
            s = k.eng("sync")
            g = k.eng("gpsimd")
            v = k.eng("vector")
            a = k.eng("scalar")
            te = k.eng("tensor")
            NB = NHF // HB
            for gi in range(NG):
                for fi in range(GRP):
                    fb = gi * GRP + fi
                    if gi >= 2 and fi == 0:
                        s.wait_ge(s_mm2.h, (gi - 1) * NB)
                    k.dma.inc(s.dma_start(out=wqg[gi % 2][fi][:],
                                          in_=qproj_in[fb]))
            wdma0 = k.dma.n - 16 * ND
            for gi in range(NG):
                for hb in range(NB):
                    j = gi * NB + hb
                    if j >= 2:
                        g.wait_ge(s_mm2.h, j - 1)
                    ins = g.dma_start(
                        out=htile[j % 2][:].rearrange("p (c t) -> p c t", c=HB),
                        in_=hdram[hb * HB:(hb + 1) * HB].transpose([1, 0, 2]))
                    s_hin.inc(ins)
            for gi in range(NG):
                te.wait_ge(k.dma.h, wdma0 + 16 * (gi + 1) * GRP)
                if gi >= 1:
                    te.wait_ge(s_add.h, gi * 2 * GRP)
                with k.scope():
                    p_o = [k.ps("ffp_po", [128, T], F32) for _ in range(GRP)]
                    for hb in range(NB):
                        j = gi * NB + hb
                        te.wait_ge(s_hin.h, 16 * (j + 1))
                        for hc4 in range(HB):
                            hc = hb * HB + hc4
                            for fi in range(GRP):
                                for half in range(2):
                                    ins = te.matmul(
                                        p_o[fi][:, half * 512:(half + 1) * 512],
                                        wqg[gi % 2][fi][:, hc * 128:(hc + 1) * 128],
                                        htile[j % 2][:, hc4 * T + half * 512:
                                                     hc4 * T + (half + 1) * 512],
                                        start=(hc == 0), stop=(hc == NHF - 1))
                        s_mm2.inc(ins, 1)
                    for fi in range(GRP):
                        fb = gi * GRP + fi
                        ai = gi * GRP + fi
                        for half in range(2):
                            hi = 2 * ai + half
                            ob = o_f32[hi % 2]
                            a.wait_ge(s_mm2.h, (gi + 1) * NB)
                            if hi >= 2:
                                a.wait_ge(s_add.h, hi - 1)
                            ins = a.activation(ob[:],
                                               p_o[fi][:, half * 512:(half + 1) * 512],
                                               AF.Copy, scale=k.ffn_oscale[:])
                            s_ev.inc(ins, 1)
                            v.wait_ge(s_ev.h, s_ev.n)
                            xsl = slice(fb * T + half * 512,
                                        fb * T + (half + 1) * 512)
                            ins = v.tensor_tensor(k.xT[:, xsl], k.xT[:, xsl],
                                                  ob[:], ALU.add)
                            s_add.inc(ins, 1)
            k.barrier()



def chain_step(k, sem, engname, fn):
    e = k.eng(engname)
    e.wait_ge(sem.h, sem.n)
    ins = fn(e)
    sem.inc(ins, 1)
    return sem.n


def emit_gla_prep_pass1(k, wq_f, wk_f, wi_d, qt_d, kt_d, kh_d, vt_d, fac_d,
                        td_out, x_in):
    s = k.eng("sync")
    g = k.eng("gpsimd")
    v = k.eng("vector")
    a = k.eng("scalar")
    te = k.eng("tensor")

    xn = k.sb("gla_xn", [128, ND * T], BF16)
    with k.scope():
        k.xT = k.sb("xT_gla", [128, ND * T], F32)
        emit_in_transpose(k, x_in)
        emit_ln(k, xn)
    k.kh_sb = k.sb("kh_sb", [128, NH * T], BF16)
    k.vt_sb = k.sb("vt_sb", [128, NT * D], BF16)
    k.dfp_sb = k.sb("dfp_sb", [128, NH * NCH], F32)
    dcs_sb = k.sb("gla_dcs", [128, NH], F32)

    with k.scope():
        wqb = [k.sb("gla_wqb", [128, D], F32) for _ in range(2)]
        wkb = [k.sb("gla_wkb", [128, D], F32) for _ in range(2)]
        wqc = k.sb("gla_wqc", [128, D], BF16)
        wkc = k.sb("gla_wkc", [128, D], BF16)
        kraw = k.sb("gla_kraw", [128, T], F32)
        sp = k.sb("gla_sp", [128, T], F32)
        t1 = k.sb("gla_t1", [128, T], F32)
        t2 = k.sb("gla_t2", [128, T], F32)
        ap_ = k.sb("gla_ap", [128, T], F32)
        ep = k.sb("gla_ep", [128, T], F32)
        em = k.sb("gla_em", [128, T], F32)
        qrel = k.sb("gla_qrel", [128, T], F32)
        qt_t = [k.sb("gla_qt", [128, T], BF16) for _ in range(2)]
        kt_t = [k.sb("gla_kt", [128, T], BF16) for _ in range(2)]
        kh_t = [k.sb("gla_kh", [128, T], BF16) for _ in range(2)]
        facd = [k.sb("gla_facd", [128, 3 * NCH], F32) for _ in range(2)]
        s_tp = k.sem("gla_s_tp")
        s_ev = k.sem("gla_s_ev")
        p_kq = k.ps("gla_pkq", [128, 1024], F32)
        p_q = k.ps("gla_pq", [128, 1024], F32)
        p_tp = k.ps("gla_ptp", [128, 1024], BF16)
        s_w = k.sem("gla_s_w")
        s_mm = k.sem("gla_s_mm")
        s_ch = k.sem("gla_s_ch")
        s_st = k.sem("gla_s_st")

        k.gla_evac_done = {}
        k.gla_chain_done = {}
        dma0 = k.dma.n
        for h in range(NH):
            k.dma.inc(s.dma_start(out=wqb[h % 2][:], in_=wq_f[h]))
            k.dma.inc(s.dma_start(out=wkb[h % 2][:], in_=wk_f[h]))
            if h >= 1:
                s.wait_ge(s_w.h, h)

        def emit_khtp(h):
            te2 = k.eng("tensor")
            te2.wait_ge(s_ch.h, k.gla_chain_done[h])
            if h >= 1:
                te2.wait_ge(s_ev.h, h)
            for tg in range(2):
                for j in range(4):
                    tt = tg * 4 + j
                    ins = te2.transpose(
                        p_tp[:, tg * 512 + j * 128: tg * 512 + (j + 1) * 128],
                        kh_t[h % 2][:, tt * 128:(tt + 1) * 128], k.identB[:])
            s_tp.inc(ins, 1)
            v2 = k.eng("vector")
            v2.wait_ge(s_tp.h, h + 1)
            ins = v2.tensor_copy(k.kh_sb[:, h * T:(h + 1) * T], p_tp[:])
            s_ev.inc(ins, 1)
            g2 = k.eng("gpsimd")
            g2.wait_ge(s_ev.h, h + 1)
            s_st.inc(g2.dma_start(out=qt_d[h], in_=qt_t[h % 2][:]))
            s_st.inc(g2.dma_start(out=kt_d[h], in_=kt_t[h % 2][:]))
            s_st.inc(g2.dma_start(out=kh_d[h],
                                  in_=k.kh_sb[:, h * T:(h + 1) * T]))
            s_st.inc(g2.dma_start(out=fac_d[h], in_=facd[h % 2][:]))

        for h in range(NH):
            v.wait_ge(k.dma.h, dma0 + 32 * (h + 1))
            if h >= 1:
                v.wait_ge(s_mm.h, 2 * h)
            v.tensor_copy(wqc[:], wqb[h % 2][:])
            ins = v.tensor_copy(wkc[:], wkb[h % 2][:])
            s_w.inc(ins, 1)
            if h >= 1:
                emit_khtp(h - 1)
            te.wait_ge(s_w.h, h + 1)
            if h >= 1:
                te.wait_ge(s_ch.h, k.gla_evac_done[h - 1])
            for half in range(2):
                for dc in range(ND):
                    ins = te.matmul(
                        p_kq[:, half * 512:(half + 1) * 512],
                        wkc[:, dc * 128:(dc + 1) * 128],
                        xn[:, dc * T + half * 512: dc * T + (half + 1) * 512],
                        start=(dc == 0), stop=(dc == ND - 1))
            s_mm.inc(ins, 1)
            for half in range(2):
                for dc in range(ND):
                    ins = te.matmul(
                        p_q[:, half * 512:(half + 1) * 512],
                        wqc[:, dc * 128:(dc + 1) * 128],
                        xn[:, dc * T + half * 512: dc * T + (half + 1) * 512],
                        start=(dc == 0), stop=(dc == ND - 1))
            s_mm.inc(ins, 1)

            C = lambda eng, fn: chain_step(k, s_ch, eng, fn)
            if h >= 2:
                k.eng("vector").wait_ge(s_st.h, 64 * (h - 1))
            k.eng("scalar").wait_ge(s_mm.h, 2 * h + 1)
            C("scalar", lambda e: e.activation(kraw[:], p_kq[:], AF.Copy))
            k.eng("scalar").wait_ge(s_mm.h, 2 * h + 2)
            C("scalar", lambda e: e.activation(qrel[:], p_q[:], AF.Relu,
                                               scale=GLA_SCALE))
            k.gla_evac_done[h] = s_ch.n
            C("scalar", lambda e: e.activation(t1[:], kraw[:], AF.Abs))
            C("scalar", lambda e: e.activation(t2[:], t1[:], AF.Exp, scale=-1.0))
            C("scalar", lambda e: e.activation(t2[:], t2[:], AF.Ln, bias=1.0))
            C("vector", lambda e: e.tensor_scalar(t1[:], kraw[:], 0.0, None,
                                                  ALU.max))
            C("vector", lambda e: e.tensor_tensor(sp[:], t1[:], t2[:], ALU.add))
            C("vector", lambda e: e.tensor_tensor(t1[:], kraw[:], sp[:],
                                                  ALU.subtract))
            C("scalar", lambda e: e.activation(t2[:], t1[:], AF.Exp))
            for ci in range(NCH):
                sl = slice(ci * CH, (ci + 1) * CH)
                C("vector", lambda e, sl=sl: e.tensor_tensor_scan(
                    ap_[:, sl], sp[:, sl], sp[:, sl], 0.0, ALU.add, ALU.bypass))
            C("scalar", lambda e: e.activation(ep[:], ap_[:], AF.Exp, scale=-1.0))
            C("scalar", lambda e: e.activation(em[:], ap_[:], AF.Exp))
            C("vector", lambda e: e.tensor_copy(
                k.dfp_sb[:, h * NCH:(h + 1) * NCH],
                ep[:].rearrange("p (c w) -> p c w", w=CH)[:, :, CH - 1]))
            C("vector", lambda e: e.reduce_sum(
                dcs_sb[:, h:h + 1],
                ap_[:].rearrange("p (c w) -> p c w", w=CH)[:, :, CH - 1],
                axis=AX))
            C("vector", lambda e: e.tensor_copy(
                facd[h % 2][:, 0:NCH],
                em[:].rearrange("p (c w) -> p c w", w=CH)[:, :, SC - 1]))
            C("vector", lambda e: e.tensor_copy(
                facd[h % 2][:, NCH:2 * NCH],
                ep[:].rearrange("p (c w) -> p c w", w=CH)[:, :, SC - 1]))
            C("vector", lambda e: e.tensor_copy(
                facd[h % 2][:, 2 * NCH:3 * NCH],
                k.dfp_sb[:, h * NCH:(h + 1) * NCH]))
            C("vector", lambda e: e.tensor_tensor(qt_t[h % 2][:], qrel[:], ep[:],
                                                  ALU.mult))
            C("vector", lambda e: e.tensor_tensor(kt_t[h % 2][:], t2[:], em[:],
                                                  ALU.mult))
            C("vector", lambda e: e.tensor_tensor(
                kh_t[h % 2][:].rearrange("p (c w) -> p c w", w=CH),
                kt_t[h % 2][:].rearrange("p (c w) -> p c w", w=CH),
                k.dfp_sb[:, h * NCH:(h + 1) * NCH].unsqueeze(2
                    ).broadcast_to([128, NCH, CH]),
                ALU.mult))
            C("vector", lambda e: e.tensor_tensor(
                kt_t[h % 2][:].rearrange("p (c w) -> p c w", w=CH)[:, :, SC:CH],
                kt_t[h % 2][:].rearrange("p (c w) -> p c w", w=CH)[:, :, SC:CH],
                facd[h % 2][:, NCH:2 * NCH].unsqueeze(2
                    ).broadcast_to([128, NCH, SC]),
                ALU.mult))
            k.gla_chain_done[h] = s_ch.n
        emit_khtp(NH - 1)
        g.wait_ge(s_st.h, s_st.n)
        dcore = k.sb("gla_dcore", [128, NH], F32)
        C2 = lambda eng, fn: chain_step(k, s_ch, eng, fn)
        C2("scalar", lambda e: e.activation(dcore[:], dcs_sb[:], AF.Exp,
                                            scale=-1.0))
        C2("scalar", lambda e: e.activation(k.junk[:, 512:1024],
                                            k.junk[:, 512:1024], AF.Copy))
        s.wait_ge(s_ch.h, s_ch.n)
        k.dma.inc(s.dma_start(out=td_out[:, D:D + NH], in_=dcore[:]))
        s.wait_ge(k.dma.h, k.dma.n)
        k.barrier()

    with k.scope():
        wib = [k.sb("gla_wib", [128, D], F32) for _ in range(2)]
        wic = [k.sb("gla_wic", [128, D], BF16) for _ in range(2)]
        p_v = [k.ps("gla_pv", [128, D], F32) for _ in range(2)]
        s_wv = k.sem("gla_s_wv")
        s_mmv = k.sem("gla_s_mmv")
        s_evv = k.sem("gla_s_evv")
        dma0 = k.dma.n
        for tp in range(4):
            for dc in range(ND):
                k.dma.inc(k.eng("sync").dma_start(out=wib[dc % 2][:],
                                                  in_=wi_d[dc]))
                if dc >= 1 or tp >= 1:
                    k.eng("sync").wait_ge(s_wv.h, tp * ND + dc)
            for dc in range(ND):
                v2 = k.eng("vector")
                v2.wait_ge(k.dma.h, dma0 + 16 * (tp * ND + dc + 1))
                if tp * ND + dc >= 2:
                    v2.wait_ge(s_mmv.h, tp * ND + dc - 1)
                ins = v2.tensor_copy(wic[dc % 2][:], wib[dc % 2][:])
                s_wv.inc(ins, 1)
                te2 = k.eng("tensor")
                te2.wait_ge(s_wv.h, tp * ND + dc + 1)
                if tp >= 1 and dc == 0:
                    te2.wait_ge(s_evv.h, 2 * tp)
                for ti in range(2):
                    tt = tp * 2 + ti
                    for half in range(4):
                        ins = te2.matmul(
                            p_v[ti][:, half * 512:(half + 1) * 512],
                            xn[:, dc * T + tt * 128:dc * T + (tt + 1) * 128],
                            wic[dc % 2][:, half * 512:(half + 1) * 512],
                            start=(dc == 0), stop=(dc == ND - 1))
                s_mmv.inc(ins, 1)
            for ti in range(2):
                tt = tp * 2 + ti
                v2 = k.eng("vector")
                v2.wait_ge(s_mmv.h, (tp + 1) * ND)
                ins = v2.tensor_copy(k.vt_sb[:, tt * D:(tt + 1) * D],
                                     p_v[ti][:])
                s_evv.inc(ins, 1)
        s2 = k.eng("sync")
        s2.wait_ge(s_evv.h, s_evv.n)
        k.dma.inc(s2.dma_start(out=vt_d[:], in_=k.vt_sb[:]))
        s2.wait_ge(k.dma.h, k.dma.n)
        k.barrier()

    with k.scope():
        S_all = k.sb("gla_S", [128, D], F32)
        p_i = [k.ps("gla_pi", [128, 1024], F32) for _ in range(4)]
        s_i = k.sem("gla_s_i")
        s_u = k.sem("gla_s_u")
        gz = k.sem("gla_gz")
        gz.inc(k.eng("gpsimd").memset(S_all[:], 0.0), 1)
        k.eng("vector").wait_ge(gz.h, 1)
        for ci in range(NCH):
            te3 = k.eng("tensor")
            v3 = k.eng("vector")
            rbase = (ci % 2) * 64
            tt = ci // 2
            for hg in range(2):
                pb = p_i[(2 * ci + hg) % 4]
                if 2 * ci + hg >= 4:
                    te3.wait_ge(s_u.h, 2 * ci + hg - 3)
                for hh in range(8):
                    h = hg * 8 + hh
                    ins = te3.matmul(
                        pb[:, hh * 128:(hh + 1) * 128],
                        k.kh_sb[rbase:rbase + 64,
                                h * T + tt * 128: h * T + (tt + 1) * 128],
                        k.vt_sb[rbase:rbase + 64,
                                tt * D + h * 128: tt * D + (h + 1) * 128],
                        start=True, stop=True)
                s_i.inc(ins, 1)
                v3.wait_ge(s_i.h, 2 * ci + hg + 1)
                for hh in range(8):
                    h = hg * 8 + hh
                    ins = v3.scalar_tensor_tensor(
                        S_all[:, h * 128:(h + 1) * 128],
                        S_all[:, h * 128:(h + 1) * 128],
                        k.dfp_sb[:, h * NCH + ci: h * NCH + ci + 1],
                        pb[:, hh * 128:(hh + 1) * 128],
                        ALU.mult, ALU.add)
                s_u.inc(ins, 1)
        s4 = k.eng("sync")
        s4.wait_ge(s_u.h, s_u.n)
        k.dma.inc(s4.dma_start(out=td_out[:, 0:D], in_=S_all[:]))
        s4.wait_ge(k.dma.h, k.dma.n)
        k.barrier()


def build_launch_gla1():
    k = Kern()
    declare_common_inputs(k)
    x_in = k.dram("x_in", [T, D], F32)
    wq_f = k.dram("wq_f", [NH, 128, D], F32)
    wk_f = k.dram("wk_f", [NH, 128, D], F32)
    wi_d = k.dram("wi_d", [ND, 128, D], F32)
    qt_d = k.dram("qt_d", [NH, 128, T], BF16, kind="ExternalOutput")
    kt_d = k.dram("kt_d", [NH, 128, T], BF16, kind="ExternalOutput")
    kh_d = k.dram("kh_d", [NH, 128, T], BF16, kind="ExternalOutput")
    vt_d = k.dram("vt_d", [128, NT * D], BF16, kind="ExternalOutput")
    fac_d = k.dram("fac_d", [NH, 128, 3 * NCH], F32, kind="ExternalOutput")
    td_out = k.dram("td_out", [128, D + NH], F32, kind="ExternalOutput")
    k.start()
    emit_common(k)
    emit_gla_prep_pass1(k, wq_f, wk_f, wi_d, qt_d, kt_d, kh_d, vt_d, fac_d,
                        td_out, x_in)
    return k.finish()



def emit_gla_pass2(k, qt_d, kt_d, kh_d, vt_d, fac_d, td_all, gsel_in, gw_in,
                   wo_f, x_in, x_out, o_scr, dbg=None):
    s = k.eng("sync")
    g = k.eng("gpsimd")
    v = k.eng("vector")
    a = k.eng("scalar")
    te = k.eng("tensor")

    oT = k.sb("g2_oT", [128, ND * T], BF16)
    with k.scope():
        qt_sb = k.sb("g2_qt", [128, NH * T], BF16)
        kt_sb = k.sb("g2_kt", [128, NH * T], BF16)
        kh_sb = k.sb("g2_kh", [128, NH * T], BF16)
        vt_sb = k.sb("g2_vt", [128, NT * D], BF16)
        fac_sb = k.sb("g2_fac", [128, NH * 3 * NCH], F32)
        gsel = k.sb("g2_gsel", [128, 2 * 8], F32)
        S_all = k.sb("g2_S", [128, D], F32)
        S_bf = k.sb("g2_Sbf", [128, D], BF16)
        ostg = [k.sb("g2_ostg", [64, D], BF16) for _ in range(2)]
        for h in range(NH):
            k.dma.inc(s.dma_start(out=qt_sb[:, h * T:(h + 1) * T], in_=qt_d[h]))
            k.dma.inc(s.dma_start(out=kt_sb[:, h * T:(h + 1) * T], in_=kt_d[h]))
            k.dma.inc(s.dma_start(out=kh_sb[:, h * T:(h + 1) * T], in_=kh_d[h]))
            k.dma.inc(s.dma_start(out=fac_sb[:, h * 3 * NCH:(h + 1) * 3 * NCH],
                                  in_=fac_d[h]))
        k.dma.inc(s.dma_start(out=vt_sb[:], in_=vt_d[:]))
        k.dma.inc(s.dma_start(out=gsel[:], in_=gsel_in[:]))
        s.wait_ge(k.dma.h, k.dma.n)
        k.barrier()

        def fac(h, which, ci=None):
            base = h * 3 * NCH + which * NCH
            if ci is None:
                return fac_sb[:, base:base + NCH]
            return fac_sb[:, base + ci:base + ci + 1]

        sc = k.sem("g2_sc")
        gz = k.sem("g2_gz")
        gz.inc(g.memset(S_all[:], 0.0), 1)
        v.wait_ge(gz.h, 1)
        pfx_scope = k.scope()
        pfx_scope.__enter__()
        td_j = k.sb("g2_td", [128, D + NH], F32)
        tju = k.sb("g2_tju", [128, D], F32)
        dp_j = k.sb("g2_dp", [128, NH], F32)
        for j in range(NCORES - 1):
            d0 = k.dma.n
            k.dma.inc(s.dma_start(out=td_j[:],
                                  in_=td_all[j * 128:(j + 1) * 128, :]))
            v.wait_ge(k.dma.h, d0 + 16)
            v.tensor_scalar(dp_j[:], td_j[:, D:D + NH], gsel[:, j:j + 1],
                            gsel[:, 8 + j:8 + j + 1], ALU.mult, ALU.add)
            ins = v.tensor_scalar(tju[:], td_j[:, 0:D], gsel[:, j:j + 1], None,
                                  ALU.mult)
            sc.inc(ins, 1)
            for h in range(NH):
                ins = v.scalar_tensor_tensor(
                    S_all[:, h * 128:(h + 1) * 128],
                    S_all[:, h * 128:(h + 1) * 128],
                    dp_j[:, h:h + 1],
                    tju[:, h * 128:(h + 1) * 128], ALU.mult, ALU.add)
            sc.inc(ins, 1)
            s.wait_ge(sc.h, sc.n)
        ins = v.tensor_copy(S_bf[:], S_all[:])
        sc.inc(ins, 1)
        k.barrier()
        pfx_scope.__exit__(None, None, None)

        p_P = k.ps("g2_pP", [64, 1024], F32)
        p_o = k.ps("g2_po", [64, 2048], F32)
        p_i = k.ps("g2_pi", [128, 1024], F32)
        P_sb = k.sb("g2_Psb", [128, 1024], BF16)
        qty = k.sb("g2_qty", [128, NH * SC], BF16)
        s_P = k.sem("g2_s_P")
        s_Pe = k.sem("g2_s_Pe")
        s_om = k.sem("g2_s_om")
        s_oe = k.sem("g2_s_oe")
        s_im = k.sem("g2_s_im")
        s_su = k.sem("g2_s_su")
        s_sb2 = k.sem("g2_s_sb2")
        s_od = k.sem("g2_s_od")
        s_y = k.sem("g2_s_y")
        gz.inc(v.memset(p_P[:], 0.0), 1)
        te.wait_ge(gz.h, 2)
        for ci in range(NCH):
            rbase = (ci % 2) * 64
            tt = ci // 2
            Xs = slice(ci * CH, ci * CH + SC)
            Ys = slice(ci * CH + SC, (ci + 1) * CH)
            if ci >= 1:
                te.wait_ge(s_Pe.h, ci)
            for h in range(NH):
                if ci > 0 and h == 0:
                    v.wait_ge(s_P.h, ci)
                ins = v.tensor_scalar(qty[:, h * SC:(h + 1) * SC],
                                      qt_sb[:, h * T:(h + 1) * T][:, Ys],
                                      fac(h, 0, ci), None, ALU.mult)
                s_y.inc(ins, 1)
                te.wait_ge(s_y.h, s_y.n)
                co = h * 64
                te.matmul(p_P[0:32, co:co + 32],
                          kt_sb[:, h * T:(h + 1) * T][:, Xs],
                          qt_sb[:, h * T:(h + 1) * T][:, Xs],
                          start=True, stop=True)
                te.matmul(p_P[0:32, co + 32:co + 64],
                          kt_sb[:, h * T:(h + 1) * T][:, Xs],
                          qt_sb[:, h * T:(h + 1) * T][:, Ys],
                          start=True, stop=True)
                ins = te.matmul(p_P[32:64, co + 32:co + 64],
                                kt_sb[:, h * T:(h + 1) * T][:, Ys],
                                qty[:, h * SC:(h + 1) * SC],
                                start=True, stop=True)
            s_P.inc(ins, 1)
            v.wait_ge(s_P.h, ci + 1)
            ins = v.tensor_tensor(
                P_sb[rbase:rbase + 64, :].rearrange("p (h w) -> p h w", w=64),
                p_P[:].rearrange("p (h w) -> p h w", w=64),
                k.trimask[:].unsqueeze(1).broadcast_to([64, NH, 64]),
                ALU.mult)
            s_Pe.inc(ins, 1)
            te.wait_ge(s_Pe.h, ci + 1)
            if ci >= 1:
                te.wait_ge(s_oe.h, ci)
                te.wait_ge(s_sb2.h, ci)
            for h in range(NH):
                te.matmul(p_o[:, h * 128:(h + 1) * 128],
                          qt_sb[:, h * T + ci * CH: h * T + (ci + 1) * CH],
                          S_bf[:, h * 128:(h + 1) * 128],
                          start=True, stop=False)
                ins = te.matmul(p_o[:, h * 128:(h + 1) * 128],
                                P_sb[rbase:rbase + 64, h * 64:(h + 1) * 64],
                                vt_sb[rbase:rbase + 64,
                                      tt * D + h * 128: tt * D + (h + 1) * 128],
                                start=False, stop=True)
            s_om.inc(ins, 1)
            v.wait_ge(s_om.h, ci + 1)
            if ci >= 2:
                v.wait_ge(s_od.h, 16 * (ci - 1))
            ins = v.tensor_copy(ostg[ci % 2][:], p_o[:])
            s_oe.inc(ins, 1)
            g.wait_ge(s_oe.h, ci + 1)
            s_od.inc(g.dma_start(out=o_scr[ci], in_=ostg[ci % 2][:]))
            for hg in range(2):
                if 2 * ci + hg >= 1:
                    te.wait_ge(s_su.h, 2 * ci + hg)
                for hh in range(8):
                    h = hg * 8 + hh
                    ins = te.matmul(
                        p_i[:, hh * 128:(hh + 1) * 128],
                        kh_sb[rbase:rbase + 64,
                              h * T + tt * 128: h * T + (tt + 1) * 128],
                        vt_sb[rbase:rbase + 64,
                              tt * D + h * 128: tt * D + (h + 1) * 128],
                        start=True, stop=True)
                s_im.inc(ins, 1)
                v.wait_ge(s_im.h, 2 * ci + hg + 1)
                if hg == 0:
                    v.wait_ge(s_om.h, ci + 1)
                for hh in range(8):
                    h = hg * 8 + hh
                    ins = v.scalar_tensor_tensor(
                        S_all[:, h * 128:(h + 1) * 128],
                        S_all[:, h * 128:(h + 1) * 128],
                        fac(h, 2, ci),
                        p_i[:, hh * 128:(hh + 1) * 128],
                        ALU.mult, ALU.add)
                s_su.inc(ins, 1)
            ins = v.tensor_copy(S_bf[:], S_all[:])
            s_sb2.inc(ins, 1)
        g.wait_ge(s_od.h, s_od.n)
        k.barrier()

    with k.scope():
        otile = [k.sb("g2_otile", [128, D], BF16) for _ in range(2)]
        tp = [k.ps("g2_tp", [128, 512], BF16) for _ in range(2)]
        s_tp = k.sem("g2_s_tp")
        s_ev = k.sem("g2_s_ev")
        odma0 = k.dma.n
        for tt in range(NT):
            if tt >= 2:
                s.wait_ge(s_ev.h, 4 * (tt - 1))
            k.dma.inc(s.dma_start(out=otile[tt % 2][0:64, :], in_=o_scr[2 * tt]))
            k.dma.inc(s.dma_start(out=otile[tt % 2][64:128, :],
                                  in_=o_scr[2 * tt + 1]))
        grp = 0
        for tt in range(NT):
            te.wait_ge(k.dma.h, odma0 + 32 * (tt + 1))
            for dg in range(4):
                pb = tp[grp % 2]
                if grp >= 2:
                    te.wait_ge(s_ev.h, grp - 1)
                for j in range(4):
                    dc = dg * 4 + j
                    ins = te.transpose(
                        pb[:, j * 128:(j + 1) * 128],
                        otile[tt % 2][:, dc * 128:(dc + 1) * 128],
                        k.identB[:])
                s_tp.inc(ins, 1)
                v.wait_ge(s_tp.h, grp + 1)
                for j in range(4):
                    dc = dg * 4 + j
                    ins = v.tensor_copy(
                        oT[:, dc * T + tt * 128: dc * T + (tt + 1) * 128],
                        pb[:, j * 128:(j + 1) * 128])
                s_ev.inc(ins, 1)
                grp += 1
        k.barrier()

    with k.scope():
        sqc = [k.sb("g2_sqc", [128, T], BF16) for _ in range(2)]
        p_s2 = k.ps("g2_ps2", [1, T], F32)
        s_sq = k.sem("g2_s_sq")
        s_mm = k.sem("g2_s_mm")
        for dc in range(ND):
            if dc >= 2:
                a.wait_ge(s_mm.h, dc - 1)
            ins = a.activation(sqc[dc % 2][:], oT[:, dc * T:(dc + 1) * T],
                               AF.Square)
            s_sq.inc(ins, 1)
            te.wait_ge(s_sq.h, dc + 1)
            for half in range(2):
                hs = slice(half * 512, (half + 1) * 512)
                ins = te.matmul(p_s2[:, hs], k.ones_col[:], sqc[dc % 2][:, hs],
                                start=(dc == 0), stop=(dc == ND - 1))
            s_mm.inc(ins, 1)
        ms_row = k.sb("g2_ms", [1, T], F32)
        r_row = k.sb("g2_r", [1, T], F32)
        s_st = k.sem("g2_s_st")
        v.wait_ge(s_mm.h, s_mm.n)
        ins = v.tensor_scalar_mul(ms_row[:], p_s2[:], 1.0 / D)
        s_st.inc(ins, 1)
        a.wait_ge(s_st.h, s_st.n)
        a.activation(r_row[:], ms_row[:], AF.Ln, bias=LN_EPS)
        ins = a.activation(r_row[:], r_row[:], AF.Exp, scale=-0.5)
        s_st.inc(ins, 1)
        p_rb = k.ps("g2_prb", [128, 512], F32)
        rb = k.sb("g2_rb", [128, T], F32)
        s_bc = k.sem("g2_s_bc")
        te.wait_ge(s_st.h, s_st.n)
        for half in range(2):
            hs = slice(half * 512, (half + 1) * 512)
            ins = te.matmul(p_rb[:], k.ones_rowF[:], r_row[:, hs],
                            start=True, stop=True)
            s_bc.inc(ins, 1)
            v.wait_ge(s_bc.h, s_bc.n)
            ins = v.tensor_copy(rb[:, hs], p_rb[:])
            s_bc.inc(ins, 1)
            te.wait_ge(s_bc.h, s_bc.n)
        gw = k.sb("g2_gw", [128, ND], F32)
        k.dma.inc(s.dma_start(out=gw[:], in_=gw_in[:]))
        s.wait_ge(k.dma.h, k.dma.n)
        nrm = k.sb("g2_nrm", [128, T], F32)
        s_nm = k.sem("g2_s_nm")
        k.barrier()
        for dc in range(ND):
            v.wait_ge(s_nm.h, s_nm.n)
            ins = v.tensor_tensor(nrm[:], oT[:, dc * T:(dc + 1) * T], rb[:],
                                  ALU.mult)
            s_nm.inc(ins, 1)
            a.wait_ge(s_nm.h, s_nm.n)
            ins = a.activation(oT[:, dc * T:(dc + 1) * T], nrm[:], AF.Copy,
                               scale=gw[:, dc:dc + 1])
            s_nm.inc(ins, 1)
        k.barrier()
        if dbg is not None:
            k.dma.inc(s.dma_start(out=dbg[:], in_=oT[:]))
            s.wait_ge(k.dma.h, k.dma.n)
            k.barrier()

        k.xT = k.sb("xT_g2", [128, ND * T], F32)
        emit_in_transpose(k, x_in)
        wo_scope = k.scope()
        wo_scope.__enter__()
        wob = [k.sb("g2_wob", [128, D], F32) for _ in range(2)]
        woc = [k.sb("g2_woc", [128, D], BF16) for _ in range(2)]
        o_f32 = [k.sb("g2_of32", [128, 512], F32) for _ in range(2)]
        p_w = [k.ps("g2_pw", [128, 512], F32) for _ in range(4)]
        s_wc = k.sem("g2_s_wc")
        s_wm = k.sem("g2_s_wm")
        s_ev2 = k.sem("g2_s_ev2")
        s_add = k.sem("g2_s_add")
        dma0 = k.dma.n
        for fb in range(ND):
            k.dma.inc(s.dma_start(out=wob[fb % 2][:], in_=wo_f[fb]))
            if fb >= 1:
                s.wait_ge(s_wc.h, fb)
        for fb in range(ND):
            v.wait_ge(k.dma.h, dma0 + 16 * (fb + 1))
            if fb >= 2:
                v.wait_ge(s_wm.h, 2 * (fb - 1))
            ins = v.tensor_copy(woc[fb % 2][:], wob[fb % 2][:])
            s_wc.inc(ins, 1)
            te.wait_ge(s_wc.h, fb + 1)
            for half in range(2):
                idx = 2 * fb + half
                pb = p_w[idx % 4]
                if idx >= 4:
                    te.wait_ge(s_add.h, idx - 3)
                for dc in range(ND):
                    ins = te.matmul(
                        pb[:], woc[fb % 2][:, dc * 128:(dc + 1) * 128],
                        oT[:, dc * T + half * 512: dc * T + (half + 1) * 512],
                        start=(dc == 0), stop=(dc == ND - 1))
                s_wm.inc(ins, 1)
                a.wait_ge(s_wm.h, idx + 1)
                if idx >= 2:
                    a.wait_ge(s_add.h, idx - 1)
                ins = a.activation(o_f32[idx % 2][:, 0:512], pb[:], AF.Copy)
                s_ev2.inc(ins, 1)
                v.wait_ge(s_ev2.h, s_ev2.n)
                xsl = slice(fb * T + half * 512, fb * T + (half + 1) * 512)
                ins = v.tensor_tensor(k.xT[:, xsl], k.xT[:, xsl],
                                      o_f32[idx % 2][:, 0:512], ALU.add)
                s_add.inc(ins, 1)
        k.barrier()
        wo_scope.__exit__(None, None, None)
        emit_out_transpose(k, x_out)


def build_launch_gla2():
    k = Kern()
    declare_common_inputs(k)
    x_in = k.dram("x_in", [T, D], F32)
    qt_d = k.dram("qt_d", [NH, 128, T], BF16)
    kt_d = k.dram("kt_d", [NH, 128, T], BF16)
    kh_d = k.dram("kh_d", [NH, 128, T], BF16)
    vt_d = k.dram("vt_d", [128, NT * D], BF16)
    fac_d = k.dram("fac_d", [NH, 128, 3 * NCH], F32)
    td_all = k.dram("td_all", [NCORES * 128, D + NH], F32)
    gsel_in = k.dram("gsel_in", [128, 16], F32)
    gw_in = k.dram("gw_in", [128, ND], F32)
    wo_f = k.dram("wo_f", [ND, 128, D], F32)
    trimask_in = k.dram("trimask_in", [64, 64], BF16)
    x_out = k.dram("x_out", [T, D], F32, kind="ExternalOutput")
    o_scr = k.dram("o_scr", [NCH, 64, D], BF16, kind="Internal")
    dbg = None
    k.start()
    emit_common(k)
    k.trimask = k.sb("trimask", [64, 64], BF16)
    s = k.eng("sync")
    k.dma.inc(s.dma_start(out=k.trimask[:], in_=trimask_in[:]))
    s.wait_ge(k.dma.h, k.dma.n)
    k.barrier()
    emit_gla_pass2(k, qt_d, kt_d, kh_d, vt_d, fac_d, td_all, gsel_in, gw_in,
                   wo_f, x_in, x_out, o_scr, dbg=dbg)
    return k.finish()




def emit_l1ffn(k, wbase_f, rw_in, a_f, bw_f, bb_in, e16_in):
    s = k.eng("sync")
    v = k.eng("vector")
    a = k.eng("scalar")
    te = k.eng("tensor")
    with k.scope():
        xn = k.sb("l1f_xn", [128, ND * T], BF16)
        emit_ln(k, xn)
        rw = k.sb("l1f_rw", [128, ND * 8], BF16)
        rw_f32 = k.sb("l1f_rwf", [128, ND * 8], F32)
        aw = k.sb("l1f_aw", [128, D], BF16)
        aw_f32 = k.sb("l1f_awf", [128, D], F32)
        e16 = k.sb("l1f_e16", [8, 128], BF16)
        bb = k.sb("l1f_bb", [128, ND], F32)
        bbh = k.sb("l1f_bbh", [128, ND], F32)
        d0 = k.dma.n
        k.dma.inc(s.dma_start(out=rw_f32[:], in_=rw_in[:]))
        k.dma.inc(s.dma_start(out=aw_f32[:], in_=a_f[:]))
        k.dma.inc(s.dma_start(out=e16[:], in_=e16_in[:]))
        k.dma.inc(s.dma_start(out=bb[:], in_=bb_in[:]))
        sprep = k.sem("l1f_sprep")
        v.wait_ge(k.dma.h, k.dma.n)
        v.tensor_copy(rw[:], rw_f32[:])
        v.tensor_copy(aw[:], aw_f32[:])
        ins = v.tensor_scalar_mul(bbh[:], bb[:], 0.5)
        sprep.inc(ins, 1)

        hm = k.sb("l1f_hm", [128, 1024], BF16)
        rt_scope = k.scope()
        rt_scope.__enter__()
        p_r = k.ps("l1f_pr", [128, 64], F32)
        logit = k.sb("l1f_lg", [128, 64], F32)
        rmax = k.sb("l1f_rmax", [128, 1], F32)
        mask = k.sb("l1f_mask", [128, 64], BF16)
        p_mt = k.ps("l1f_pmt", [8, 1024], BF16)
        maskT = k.sb("l1f_maskT", [8, 1024], BF16)
        p_me = k.ps("l1f_pme", [128, 1024], F32)
        mexp = k.sb("l1f_mexp", [128, 1024], BF16)
        s_r = k.sem("l1f_s_r")
        te.wait_ge(sprep.h, 1)
        for tt in range(NT):
            for dc in range(ND):
                ins = te.matmul(p_r[:, tt * 8:(tt + 1) * 8],
                                xn[:, dc * T + tt * 128: dc * T + (tt + 1) * 128],
                                rw[:, dc * 8:(dc + 1) * 8],
                                start=(dc == 0), stop=(dc == ND - 1))
        s_r.inc(ins, 1)
        v.wait_ge(s_r.h, 1)
        ins = v.tensor_copy(logit[:], p_r[:])
        s_r.inc(ins, 1)
        v.wait_ge(s_r.h, 2)
        for tt in range(NT):
            v.reduce_max(rmax[:], logit[:, tt * 8:(tt + 1) * 8], axis=AX)
            v.tensor_copy(k.junk[:, 0:512], k.junk[:, 0:512])
            ins = v.tensor_scalar(mask[:, tt * 8:(tt + 1) * 8],
                                  logit[:, tt * 8:(tt + 1) * 8],
                                  rmax[:], None, ALU.is_ge)
        s_r.inc(ins, 1)
        te.wait_ge(s_r.h, 3)
        for tt in range(NT):
            ins = te.transpose(p_mt[:, tt * 128:(tt + 1) * 128],
                               mask[:, tt * 8:(tt + 1) * 8], k.identB[:])
        s_r.inc(ins, 1)
        v.wait_ge(s_r.h, 4)
        ins = v.tensor_copy(maskT[:], p_mt[:])
        s_r.inc(ins, 1)
        te.wait_ge(s_r.h, 5)
        for half in range(2):
            hs = slice(half * 512, (half + 1) * 512)
            ins = te.matmul(p_me[:, hs], e16[:], maskT[:, hs],
                            start=True, stop=True)
        s_r.inc(ins, 1)
        v.wait_ge(s_r.h, 6)
        ins = v.tensor_copy(mexp[:], p_me[:])
        s_r.inc(ins, 1)

        p_h = k.ps("l1f_ph", [128, 1024], F32)
        te.wait_ge(s_r.h, 7)
        for half in range(2):
            hs = slice(half * 512, (half + 1) * 512)
            for dc in range(ND):
                ins = te.matmul(p_h[:, hs], aw[:, dc * 128:(dc + 1) * 128],
                                xn[:, dc * T + half * 512: dc * T + (half + 1) * 512],
                                start=(dc == 0), stop=(dc == ND - 1))
        s_r.inc(ins, 1)
        v.wait_ge(s_r.h, 8)
        ins = v.tensor_tensor(hm[:], p_h[:], mexp[:], ALU.mult)
        s_r.inc(ins, 1)
        k.barrier()
        rt_scope.__exit__(None, None, None)

        wbb = [k.sb("l1f_wbb", [128, D], F32) for _ in range(2)]
        wbc = [k.sb("l1f_wbc", [128, D], BF16) for _ in range(2)]
        bwb = k.sb("l1f_bwb", [128, ND * 128], F32)
        bwc = k.sb("l1f_bwc", [128, ND * 128], BF16)
        o_f32 = [k.sb("l1f_of32", [128, 512], F32) for _ in range(2)]
        p_w = [k.ps("l1f_pw", [128, 512], F32) for _ in range(4)]
        s_wc = k.sem("l1f_s_wc")
        s_wm = k.sem("l1f_s_wm")
        s_ev = k.sem("l1f_s_ev")
        s_add = k.sem("l1f_s_add")
        d1 = k.dma.n
        for fb in range(ND):
            k.dma.inc(s.dma_start(out=bwb[:, fb * 128:(fb + 1) * 128],
                                  in_=bw_f[fb]))
        v.wait_ge(k.dma.h, k.dma.n)
        ins = v.tensor_copy(bwc[:], bwb[:])
        s_wc.inc(ins, 1)
        dma0 = k.dma.n
        for fb in range(ND):
            k.dma.inc(s.dma_start(out=wbb[fb % 2][:], in_=wbase_f[fb]))
            if fb >= 1:
                s.wait_ge(s_wc.h, fb + 1)
        for fb in range(ND):
            v.wait_ge(k.dma.h, dma0 + 16 * (fb + 1))
            if fb >= 2:
                v.wait_ge(s_wm.h, 2 * (fb - 1))
            ins = v.tensor_copy(wbc[fb % 2][:], wbb[fb % 2][:])
            s_wc.inc(ins, 1)
            te.wait_ge(s_wc.h, fb + 2)
            te.wait_ge(s_r.h, 9)
            for half in range(2):
                idx = 2 * fb + half
                pb = p_w[idx % 4]
                if idx >= 4:
                    te.wait_ge(s_add.h, idx - 3)
                for dc in range(ND):
                    te.matmul(
                        pb[:], wbc[fb % 2][:, dc * 128:(dc + 1) * 128],
                        xn[:, dc * T + half * 512: dc * T + (half + 1) * 512],
                        start=(dc == 0), stop=False)
                ins = te.matmul(
                    pb[:], bwc[:, fb * 128:(fb + 1) * 128],
                    hm[:, half * 512:(half + 1) * 512],
                    start=False, stop=True)
                s_wm.inc(ins, 1)
                a.wait_ge(s_wm.h, idx + 1)
                if idx >= 2:
                    a.wait_ge(s_add.h, idx - 1)
                ins = a.activation(o_f32[idx % 2][:], pb[:], AF.Identity,
                                   bias=bbh[:, fb:fb + 1], scale=0.5)
                s_ev.inc(ins, 1)
                v.wait_ge(s_ev.h, s_ev.n)
                xsl = slice(fb * T + half * 512, fb * T + (half + 1) * 512)
                ins = v.tensor_tensor(k.xT[:, xsl], k.xT[:, xsl],
                                      o_f32[idx % 2][:], ALU.add)
                s_add.inc(ins, 1)
        k.barrier()



NKV = 4
SWA_W = 1024


def emit_swa_proj(k, wq_f, wk_f, wv_d, qt_out, kv_out):
    s = k.eng("sync")
    g = k.eng("gpsimd")
    v = k.eng("vector")
    a = k.eng("scalar")
    te = k.eng("tensor")
    with k.scope():
        xn = k.sb("swp_xn", [128, ND * T], BF16)
        emit_ln(k, xn)
        wb = [k.sb("swp_wb", [128, D], F32) for _ in range(2)]
        wc = [k.sb("swp_wc", [128, D], BF16) for _ in range(2)]
        ot = k.sb("swp_ot", [128, T], BF16)
        p_o = k.ps("swp_po", [128, 1024], F32)
        s_wc = k.sem("swp_s_wc")
        s_mm = k.sem("swp_s_mm")
        s_ev = k.sem("swp_s_ev")
        s_st = k.sem("swp_s_st")
        blocks = [("q", h) for h in range(NH)] + [("k", c) for c in range(NKV)]
        dma0 = k.dma.n
        for bi, (kind, h) in enumerate(blocks):
            src_t = wq_f if kind == "q" else wk_f
            k.dma.inc(s.dma_start(out=wb[bi % 2][:], in_=src_t[h]))
            if bi >= 1:
                s.wait_ge(s_wc.h, bi)
        for bi, (kind, h) in enumerate(blocks):
            v.wait_ge(k.dma.h, dma0 + 16 * (bi + 1))
            if bi >= 2:
                v.wait_ge(s_mm.h, bi - 1)
            ins = v.tensor_copy(wc[bi % 2][:], wb[bi % 2][:])
            s_wc.inc(ins, 1)
            te.wait_ge(s_wc.h, bi + 1)
            if bi >= 1:
                te.wait_ge(s_ev.h, bi)
            for half in range(2):
                for dc in range(ND):
                    ins = te.matmul(
                        p_o[:, half * 512:(half + 1) * 512],
                        wc[bi % 2][:, dc * 128:(dc + 1) * 128],
                        xn[:, dc * T + half * 512: dc * T + (half + 1) * 512],
                        start=(dc == 0), stop=(dc == ND - 1))
            s_mm.inc(ins, 1)
            v.wait_ge(s_mm.h, bi + 1)
            if bi >= 1:
                v.wait_ge(s_st.h, 16 * bi)
            ins = v.tensor_copy(ot[:], p_o[:])
            s_ev.inc(ins, 1)
            g.wait_ge(s_ev.h, bi + 1)
            if kind == "q":
                ins = g.dma_start(out=qt_out[h], in_=ot[:])
            else:
                ins = g.dma_start(out=kv_out[:, h * T:(h + 1) * T], in_=ot[:])
            s_st.inc(ins)
        wvb = [k.sb("swp_wvb", [128, NKV * 128], F32) for _ in range(2)]
        wvc = k.sb("swp_wvc", [128, ND * NKV * 128], BF16)
        p_v = [k.ps("swp_pv", [128, 512], F32) for _ in range(2)]
        vstg = [k.sb("swp_vstg", [128, 512], BF16) for _ in range(2)]
        s_vw = k.sem("swp_s_vw")
        s_vm = k.sem("swp_s_vm")
        s_vev = k.sem("swp_s_vev")
        s_vst = k.sem("swp_s_vst")
        dma1 = k.dma.n
        for dc in range(ND):
            k.dma.inc(s.dma_start(out=wvb[dc % 2][:], in_=wv_d[dc]))
            if dc >= 1:
                s.wait_ge(s_vw.h, dc)
        for dc in range(ND):
            v.wait_ge(k.dma.h, dma1 + 16 * (dc + 1))
            ins = v.tensor_copy(wvc[:, dc * 512:(dc + 1) * 512], wvb[dc % 2][:])
            s_vw.inc(ins, 1)
        te.wait_ge(s_vw.h, ND)
        for tt in range(NT):
            pb = p_v[tt % 2]
            if tt >= 2:
                te.wait_ge(s_vev.h, tt - 1)
            for dc in range(ND):
                ins = te.matmul(pb[:],
                                xn[:, dc * T + tt * 128: dc * T + (tt + 1) * 128],
                                wvc[:, dc * 512:(dc + 1) * 512],
                                start=(dc == 0), stop=(dc == ND - 1))
            s_vm.inc(ins, 1)
            v.wait_ge(s_vm.h, tt + 1)
            if tt >= 2:
                v.wait_ge(s_vst.h, 16 * (tt - 1))
            ins = v.tensor_copy(vstg[tt % 2][:], pb[:])
            s_vev.inc(ins, 1)
            g.wait_ge(s_vev.h, tt + 1)
            ins = g.dma_start(out=kv_out[:, NKV * T + tt * 512:
                                        NKV * T + (tt + 1) * 512],
                              in_=vstg[tt % 2][:])
            s_vst.inc(ins)
        g.wait_ge(s_vst.h, s_vst.n)
        k.barrier()


def emit_swa_core(k, qt_d, kctx_in, vctx_in, ksink_in, vsink_in, dmask_in,
                  fmask_in, smask_in, nsp_in, wo_f, x_in, x_out):
    s = k.eng("sync")
    g = k.eng("gpsimd")
    v = k.eng("vector")
    a = k.eng("scalar")
    te = k.eng("tensor")
    NCT = 16
    oatt = k.sb("swa_o", [128, NH * T], BF16)
    with k.scope():
        qt_sb = k.sb("swa_qt", [128, NH * T], BF16)
        kctx = k.sb("swa_kctx", [128, NKV * 2 * T], BF16)
        vctx = k.sb("swa_vctx", [128, 2 * NT * 512], BF16)
        ksink = k.sb("swa_ksink", [128, NKV * 4], BF16)
        vsink = k.sb("swa_vsink", [4, 512], BF16)
        dmask = k.sb("swa_dmask", [128, 128], BF16)
        fmask = k.sb("swa_fmask", [128, 128], BF16)
        smask = k.sb("swa_smask", [4, T], BF16)
        nsp = k.sb("swa_nsp", [1, T], F32)
        ehat = k.sb("swa_ehat", [1, 128], BF16)
        ehatf = k.sb("swa_ehatf", [1, 128], F32)
        zrow = k.sb("swa_z", [1, T], F32)
        rzb = k.sb("swa_rzb", [128, T], F32)
        P_sb = [k.sb("swa_P", [128, T], BF16) for _ in range(2)]
        Ps_sb = k.sb("swa_Ps", [4, T], BF16)
        for h in range(NH):
            k.dma.inc(s.dma_start(out=qt_sb[:, h * T:(h + 1) * T], in_=qt_d[h]))
        k.dma.inc(s.dma_start(out=kctx[:], in_=kctx_in[:]))
        k.dma.inc(s.dma_start(out=vctx[:], in_=vctx_in[:]))
        k.dma.inc(s.dma_start(out=ksink[:], in_=ksink_in[:]))
        k.dma.inc(s.dma_start(out=vsink[:], in_=vsink_in[:]))
        k.dma.inc(s.dma_start(out=dmask[:], in_=dmask_in[:]))
        k.dma.inc(s.dma_start(out=fmask[:], in_=fmask_in[:]))
        k.dma.inc(s.dma_start(out=smask[:], in_=smask_in[:]))
        k.dma.inc(s.dma_start(out=nsp[:], in_=nsp_in[:]))
        s.wait_ge(k.dma.h, k.dma.n)
        k.barrier()
        sc0 = k.sem("swa_sc0")
        gz = k.sem("swa_gz")
        gz.inc(v.memset(ehat[:], 0.0), 1)
        a.wait_ge(gz.h, 1)
        a.activation(ehat[:], ehat[:], AF.Exp, bias=-SWA_SHIFT,
                     scale=SWA_SCALE)
        v.wait_ge(gz.h, 1)
        gz.inc(v.memset(k.junk[:, 0:512], 0.0), 1)
        ins = a.activation(k.junk[0:1, 512:640], k.junk[0:1, 640:768], AF.Copy)
        sc0.inc(ins, 1)
        v.wait_ge(sc0.h, 1)
        ins = v.tensor_copy(ehatf[:], ehat[:])
        sc0.inc(ins, 1)

        p_s = [k.ps("swa_ps", [128, 1024], F32) for _ in range(2)]
        p_o = k.ps("swa_po", [128, 1024], F32)
        p_z = k.ps("swa_pz", [1, 1024], F32)
        p_rb = p_s[0][:, 0:512]
        s_sm = k.sem("swa_s_sm")
        s_pe = k.sem("swa_s_pe")
        s_mk = k.sem("swa_s_mk")
        s_pv = k.sem("swa_s_pv")
        s_on = k.sem("swa_s_on")
        s_zz = k.sem("swa_s_zz")
        for qh in range(NH):
            kvh = qh // 4
            gs = qh * (NCT + 1)
            if qh >= 1:
                te.wait_ge(s_pe.h, gs)
                te.wait_ge(s_on.h, qh)
                te.wait_ge(s_zz.h, 5 * qh)
            for off in range(0, T, 512):
                ins = te.matmul(p_s[0][0:4, off:off + 512],
                                ksink[:, kvh * 4:(kvh + 1) * 4],
                                qt_sb[:, qh * T + off: qh * T + off + 512],
                                start=True, stop=True)
            s_sm.inc(ins, 1)
            a.wait_ge(s_sm.h, gs + 1)
            if qh >= 1:
                a.wait_ge(s_pv.h, gs)
            ins = a.activation(Ps_sb[:], p_s[0][0:4, :], AF.Exp,
                               bias=-SWA_SHIFT, scale=SWA_SCALE)
            s_pe.inc(ins, 1)
            v.wait_ge(s_pe.h, gs + 1)
            ins = v.tensor_tensor(Ps_sb[:], Ps_sb[:], smask[:], ALU.mult)
            s_mk.inc(ins, 1)
            te.wait_ge(s_mk.h, s_mk.n)
            for off in range(0, T, 512):
                te.matmul(p_o[:, off:off + 512],
                          vsink[:, kvh * 128:(kvh + 1) * 128],
                          Ps_sb[:, off:off + 512], start=True, stop=False)
                ins = te.matmul(p_z[:, off:off + 512], k.ones_col[0:4, :],
                                Ps_sb[:, off:off + 512], start=True, stop=False)
            s_pv.inc(ins, 1)
            for ktl in range(NCT):
                qt0 = max(0, ktl - 8)
                qt1 = min(7, ktl)
                qlo, qhi = qt0 * 128, qt1 * 128 + 128
                n = qhi - qlo
                gi = qh * (NCT + 1) + 1 + ktl
                par = ktl % 2
                ps_b = p_s[par]
                pb_b = P_sb[par]
                if ktl == 0:
                    te.wait_ge(s_pe.h, gs + 1)
                    te.wait_ge(s_zz.h, 5 * qh)
                elif ktl == 1:
                    te.wait_ge(s_pe.h, gs)
                else:
                    te.wait_ge(s_pe.h, gi - 1)
                for seg in range(qlo // 512, (qhi + 511) // 512):
                    lo = max(qlo, seg * 512)
                    hi = min(qhi, (seg + 1) * 512)
                    ins = te.matmul(
                        ps_b[:, lo:hi],
                        kctx[:, kvh * 2 * T + ktl * 128:
                             kvh * 2 * T + (ktl + 1) * 128],
                        qt_sb[:, qh * T + lo: qh * T + hi],
                        start=True, stop=True)
                s_sm.inc(ins, 1)
                a.wait_ge(s_sm.h, gi + 1)
                a.wait_ge(s_pv.h, max(gi - 1, qh * (NCT + 1)))
                ins = a.activation(pb_b[:, qlo:qhi], ps_b[:, qlo:qhi], AF.Exp,
                                   bias=-SWA_SHIFT, scale=SWA_SCALE)
                s_pe.inc(ins, 1)
                if ktl >= 8:
                    v.wait_ge(s_pe.h, gi + 1)
                    ins = v.tensor_tensor(
                        pb_b[:, (ktl - 8) * 128:(ktl - 7) * 128],
                        pb_b[:, (ktl - 8) * 128:(ktl - 7) * 128],
                        dmask[:], ALU.mult)
                    s_mk.inc(ins, 1)
                if ktl <= 7:
                    v.wait_ge(s_pe.h, gi + 1)
                    ins = v.tensor_tensor(
                        pb_b[:, ktl * 128:(ktl + 1) * 128],
                        pb_b[:, ktl * 128:(ktl + 1) * 128],
                        fmask[:], ALU.mult)
                    s_mk.inc(ins, 1)
                te.wait_ge(s_pe.h, gi + 1)
                te.wait_ge(s_mk.h, s_mk.n)
                last = (ktl == NCT - 1)
                for seg in range(qlo // 512, (qhi + 511) // 512):
                    lo = max(qlo, seg * 512)
                    hi = min(qhi, (seg + 1) * 512)
                    te.matmul(p_o[:, lo:hi],
                              vctx[:, ktl * 512 + kvh * 128:
                                   ktl * 512 + (kvh + 1) * 128],
                              pb_b[:, lo:hi], start=False, stop=last)
                    ins = te.matmul(p_z[:, lo:hi], k.ones_col[:],
                                    pb_b[:, lo:hi], start=False, stop=last)
                s_pv.inc(ins, 1)
            v.wait_ge(s_pv.h, s_pv.n)
            v.wait_ge(sc0.h, 2)
            v.tensor_scalar(zrow[:], nsp[:], ehatf[0:1, 0:1], None, ALU.mult)
            v.tensor_copy(k.junk[:, 0:512], k.junk[:, 0:512])
            v.tensor_tensor(zrow[:], p_z[:], zrow[:], ALU.subtract)
            v.tensor_copy(k.junk[:, 0:512], k.junk[:, 0:512])
            ins = v.reciprocal(zrow[:], zrow[:])
            s_zz.inc(ins, 1)
            te.wait_ge(s_zz.h, s_zz.n)
            for half in range(2):
                hs = slice(half * 512, (half + 1) * 512)
                ins = te.matmul(p_rb[:], k.ones_rowF[:], zrow[:, hs],
                                start=True, stop=True)
                s_zz.inc(ins, 1)
                v.wait_ge(s_zz.h, s_zz.n)
                ins = v.tensor_copy(rzb[:, hs], p_rb[:])
                s_zz.inc(ins, 1)
                te.wait_ge(s_zz.h, s_zz.n)
            ins = v.tensor_tensor(oatt[:, qh * T:(qh + 1) * T], p_o[:],
                                  rzb[:], ALU.mult)
            s_on.inc(ins, 1)
            if getattr(k, "swa_zdbg", None) is not None:
                s.wait_ge(s_on.h, qh + 1)
                k.dma.inc(s.dma_start(out=k.swa_zdbg[qh], in_=zrow[:]))
            te.wait_ge(s_on.h, qh + 1)
        k.barrier()
        if getattr(k, "swa_dbg", None) is not None:
            s.wait_ge(s_on.h, s_on.n)
            k.dma.inc(s.dma_start(out=k.swa_dbg[:], in_=oatt[:]))
            s.wait_ge(k.dma.h, k.dma.n)
            k.barrier()

    with k.scope():
        k.xT = k.sb("xT_swa", [128, ND * T], F32)
        emit_in_transpose(k, x_in)
        wo_scope = k.scope()
        wo_scope.__enter__()
        wob = [k.sb("swa_wob", [128, D], F32) for _ in range(2)]
        woc = [k.sb("swa_woc", [128, D], BF16) for _ in range(2)]
        o_f32 = [k.sb("swa_of32", [128, 512], F32) for _ in range(2)]
        p_w = [k.ps("swa_pw", [128, 512], F32) for _ in range(4)]
        s_wc = k.sem("swa_s_wc")
        s_wm = k.sem("swa_s_wm")
        s_ev2 = k.sem("swa_s_ev2")
        s_add = k.sem("swa_s_add")
        dma0 = k.dma.n
        for fb in range(ND):
            k.dma.inc(s.dma_start(out=wob[fb % 2][:], in_=wo_f[fb]))
            if fb >= 1:
                s.wait_ge(s_wc.h, fb)
        for fb in range(ND):
            v.wait_ge(k.dma.h, dma0 + 16 * (fb + 1))
            if fb >= 2:
                v.wait_ge(s_wm.h, 2 * (fb - 1))
            ins = v.tensor_copy(woc[fb % 2][:], wob[fb % 2][:])
            s_wc.inc(ins, 1)
            te.wait_ge(s_wc.h, fb + 1)
            for half in range(2):
                idx = 2 * fb + half
                pb = p_w[idx % 4]
                if idx >= 4:
                    te.wait_ge(s_add.h, idx - 3)
                for dc in range(ND):
                    ins = te.matmul(
                        pb[:], woc[fb % 2][:, dc * 128:(dc + 1) * 128],
                        oatt[:, dc * T + half * 512: dc * T + (half + 1) * 512],
                        start=(dc == 0), stop=(dc == ND - 1))
                s_wm.inc(ins, 1)
                a.wait_ge(s_wm.h, idx + 1)
                if idx >= 2:
                    a.wait_ge(s_add.h, idx - 1)
                ins = a.activation(o_f32[idx % 2][:], pb[:], AF.Copy)
                s_ev2.inc(ins, 1)
                v.wait_ge(s_ev2.h, s_ev2.n)
                xsl = slice(fb * T + half * 512, fb * T + (half + 1) * 512)
                ins = v.tensor_tensor(k.xT[:, xsl], k.xT[:, xsl],
                                      o_f32[idx % 2][:], ALU.add)
                s_add.inc(ins, 1)
        k.barrier()
        wo_scope.__exit__(None, None, None)
        emit_out_transpose(k, x_out)


def build_launch_ffn0():
    k = Kern()
    declare_common_inputs(k)
    x_in = k.dram("x_in", [T, D], F32)
    wfct = k.dram("wfct", [NHF, 128, D], F32)
    wprojt = k.dram("wprojt", [ND, 128, HF], F32)
    x_out = k.dram("x_out", [T, D], F32, kind="ExternalOutput")
    am_out = k.dram("am_out", [1, 2], F32, kind="ExternalOutput")
    qfc_out = k.dram("qfc_out", [NHF, 128, D], BF16, kind="ExternalOutput")
    qproj_out = k.dram("qproj_out", [ND, 128, HF], BF16, kind="ExternalOutput")
    k.start()
    emit_common(k)
    k.xT = k.sb("xT", [128, ND * T], F32)
    hdram = k.dram("hdram", [NHF, 128, T], BF16, kind="Internal")
    emit_absmean(k, wfct, wprojt, am_out)
    emit_in_transpose(k, x_in)
    emit_ffn0(k, wfct, wprojt, hdram, qfc_out=qfc_out, qproj_out=qproj_out)
    emit_out_transpose(k, x_out)
    return k.finish()


def build_launch_ffn0b():
    k = Kern()
    declare_common_inputs(k)
    x_in = k.dram("x_in", [T, D], F32)
    qfc_in = k.dram("qfc_in", [NHF, 128, D], BF16)
    qproj_in = k.dram("qproj_in", [ND, 128, HF], BF16)
    am_in = k.dram("am_in", [1, 2], F32)
    x_out = k.dram("x_out", [T, D], F32, kind="ExternalOutput")
    k.start()
    emit_common(k)
    k.xT = k.sb("xT_b", [128, ND * T], F32)
    hdram = k.dram("hdram_b", [NHF, 128, T], BF16, kind="Internal")
    emit_ffn_scales(k, am_in)
    emit_in_transpose(k, x_in)
    emit_ffn0_pre(k, qfc_in, qproj_in, hdram)
    emit_out_transpose(k, x_out)
    return k.finish()


def build_launch_gla1():
    k = Kern()
    declare_common_inputs(k)
    x_in = k.dram("x_in", [T, D], F32)
    wq_f = k.dram("wq_f", [NH, 128, D], F32)
    wk_f = k.dram("wk_f", [NH, 128, D], F32)
    wi_d = k.dram("wi_d", [ND, 128, D], F32)
    qt_d = k.dram("qt_d", [NH, 128, T], BF16, kind="ExternalOutput")
    kt_d = k.dram("kt_d", [NH, 128, T], BF16, kind="ExternalOutput")
    kh_d = k.dram("kh_d", [NH, 128, T], BF16, kind="ExternalOutput")
    vt_d = k.dram("vt_d", [128, NT * D], BF16, kind="ExternalOutput")
    fac_d = k.dram("fac_d", [NH, 128, 3 * NCH], F32, kind="ExternalOutput")
    td_out = k.dram("td_out", [128, D + NH], F32, kind="ExternalOutput")
    k.start()
    emit_common(k)
    emit_gla_prep_pass1(k, wq_f, wk_f, wi_d, qt_d, kt_d, kh_d, vt_d, fac_d,
                        td_out, x_in)
    return k.finish()


def build_launch_gla2():
    k = Kern()
    declare_common_inputs(k)
    x_in = k.dram("x_in", [T, D], F32)
    qt_d = k.dram("qt_d", [NH, 128, T], BF16)
    kt_d = k.dram("kt_d", [NH, 128, T], BF16)
    kh_d = k.dram("kh_d", [NH, 128, T], BF16)
    vt_d = k.dram("vt_d", [128, NT * D], BF16)
    fac_d = k.dram("fac_d", [NH, 128, 3 * NCH], F32)
    td_all = k.dram("td_all", [NCORES * 128, D + NH], F32)
    gsel_in = k.dram("gsel_in", [128, 16], F32)
    gw_in = k.dram("gw_in", [128, ND], F32)
    wo_f = k.dram("wo_f", [ND, 128, D], F32)
    trimask_in = k.dram("trimask_in", [64, 64], BF16)
    x_out = k.dram("x_out", [T, D], F32, kind="ExternalOutput")
    o_scr = k.dram("o_scr", [NCH, 64, D], BF16, kind="Internal")
    k.start()
    emit_common(k)
    k.trimask = k.sb("trimask", [64, 64], BF16)
    s = k.eng("sync")
    k.dma.inc(s.dma_start(out=k.trimask[:], in_=trimask_in[:]))
    s.wait_ge(k.dma.h, k.dma.n)
    k.barrier()
    emit_gla_pass2(k, qt_d, kt_d, kh_d, vt_d, fac_d, td_all, gsel_in, gw_in,
                   wo_f, x_in, x_out, o_scr)
    return k.finish()


def build_launch_l1ffn():
    k = Kern()
    declare_common_inputs(k)
    x_in = k.dram("x_in", [T, D], F32)
    wbase_f = k.dram("wbase_f", [ND, 128, D], F32)
    rw_in = k.dram("rw_in", [128, ND * 8], F32)
    a_f = k.dram("a_f", [128, D], F32)
    bw_f = k.dram("bw_f", [ND, 128, 128], F32)
    bb_in = k.dram("bb_in", [128, ND], F32)
    e16_in = k.dram("e16_in", [8, 128], BF16)
    x_out = k.dram("x_out", [T, D], F32, kind="ExternalOutput")
    k.start()
    emit_common(k)
    k.xT = k.sb("xT_l1f", [128, ND * T], F32)
    emit_in_transpose(k, x_in)
    emit_l1ffn(k, wbase_f, rw_in, a_f, bw_f, bb_in, e16_in)
    emit_out_transpose(k, x_out)
    return k.finish()


def build_launch_swaproj():
    k = Kern()
    declare_common_inputs(k)
    x_in = k.dram("x_in", [T, D], F32)
    wq_f = k.dram("wq_f", [NH, 128, D], F32)
    wk_f = k.dram("wk_f", [NKV, 128, D], F32)
    wv_d = k.dram("wv_d", [ND, 128, NKV * 128], F32)
    qt_out = k.dram("qt_out", [NH, 128, T], BF16, kind="ExternalOutput")
    kv_out = k.dram("kv_out", [128, NKV * T + NT * 512], BF16,
                    kind="ExternalOutput")
    k.start()
    emit_common(k)
    k.xT = k.sb("xT_swp", [128, ND * T], F32)
    emit_in_transpose(k, x_in)
    emit_swa_proj(k, wq_f, wk_f, wv_d, qt_out, kv_out)
    return k.finish()


def build_launch_swacore():
    k = Kern()
    declare_common_inputs(k)
    x_in = k.dram("x_in", [T, D], F32)
    qt_d = k.dram("qt_d", [NH, 128, T], BF16)
    kctx_in = k.dram("kctx_in", [128, NKV * 2 * T], BF16)
    vctx_in = k.dram("vctx_in", [128, 2 * NT * 512], BF16)
    ksink_in = k.dram("ksink_in", [128, NKV * 4], BF16)
    vsink_in = k.dram("vsink_in", [4, 512], BF16)
    dmask_in = k.dram("dmask_in", [128, 128], BF16)
    fmask_in = k.dram("fmask_in", [128, 128], BF16)
    smask_in = k.dram("smask_in", [4, T], BF16)
    nsp_in = k.dram("nsp_in", [1, T], F32)
    wo_f = k.dram("wo_f", [ND, 128, D], F32)
    x_out = k.dram("x_out", [T, D], F32, kind="ExternalOutput")
    k.start()
    emit_common(k)
    k.swa_dbg = None
    k.swa_zdbg = None
    emit_swa_core(k, qt_d, kctx_in, vctx_in, ksink_in, vsink_in, dmask_in,
                  fmask_in, smask_in, nsp_in, wo_f, x_in, x_out)
    return k.finish()



_BUILD_CACHE = {}


def _get(name, builder):
    if name not in _BUILD_CACHE:
        _BUILD_CACHE[name] = builder()
    return _BUILD_CACHE[name]


def _run(nc, in_maps):
    res = run_bass_kernel_spmd(nc, in_maps, core_ids=list(range(NCORES)))
    return res.results


def kernel(L0_attn_hidden_states, L0_attn_Wq, L0_attn_Wk, L0_attn_Wi,
           L0_attn_Wo, L0_attn_g_weight, L0_ffn_w_fc, L0_ffn_w_proj,
           L1_attn_Wq, L1_attn_Wk, L1_attn_Wv, L1_attn_Wo, L1_ffn_W_base,
           L1_ffn_b_base, L1_ffn_router_W, L1_ffn_A, L1_ffn_Bw):
    import ml_dtypes
    common = host_common_inputs()
    BD = ml_dtypes.bfloat16
    C = np.ascontiguousarray

    x = np.asarray(L0_attn_hidden_states)[0]
    xs = [C(x[c * T:(c + 1) * T]) for c in range(NCORES)]

    nc_ffn0 = _get("ffn0", build_launch_ffn0)
    wfct = prep_w_fmaj(C(np.asarray(L0_ffn_w_fc).T))
    wprojt = prep_w_fmaj(C(np.asarray(L0_ffn_w_proj).T))
    maps = [dict(common, x_in=xs[c], wfct=wfct, wprojt=wprojt)
            for c in range(NCORES)]
    r0f = _run(nc_ffn0, maps)
    xs = [r0f[c]["x_out"] for c in range(NCORES)]

    nc_g1 = _get("gla1", build_launch_gla1)
    wq_f = prep_w_fmaj(np.asarray(L0_attn_Wq))
    wk_f = prep_w_fmaj(np.asarray(L0_attn_Wk))
    wi_d = prep_w_dmaj(np.asarray(L0_attn_Wi))
    maps = [dict(common, x_in=xs[c], wq_f=wq_f, wk_f=wk_f, wi_d=wi_d)
            for c in range(NCORES)]
    r1 = _run(nc_g1, maps)

    nc_g2 = _get("gla2", build_launch_gla2)
    td_all = np.concatenate([r1[c]["td_out"] for c in range(NCORES)], axis=0)
    wo_f = prep_w_fmaj(np.asarray(L0_attn_Wo))
    gw_in = C(np.asarray(L0_attn_g_weight).reshape(ND, 128).T)
    trimask = np.triu(np.ones((64, 64))).astype(BD)
    maps = []
    for c in range(NCORES):
        gsel = np.zeros((128, 16), np.float32)
        gsel[:, 0:8] = (np.arange(8)[None, :] < c)
        gsel[:, 8:16] = 1.0 - gsel[:, 0:8]
        maps.append(dict(common, x_in=xs[c], qt_d=r1[c]["qt_d"],
                         kt_d=r1[c]["kt_d"], kh_d=r1[c]["kh_d"],
                         vt_d=r1[c]["vt_d"], fac_d=r1[c]["fac_d"],
                         td_all=td_all, gsel_in=gsel, gw_in=gw_in,
                         wo_f=wo_f, trimask_in=trimask))
    r = _run(nc_g2, maps)
    xs = [r[c]["x_out"] for c in range(NCORES)]

    nc_ffn0b = _get("ffn0b", build_launch_ffn0b)
    maps = [dict(common, x_in=xs[c], qfc_in=r0f[c]["qfc_out"],
                 qproj_in=r0f[c]["qproj_out"], am_in=r0f[c]["am_out"])
            for c in range(NCORES)]
    r = _run(nc_ffn0b, maps)
    xs = [r[c]["x_out"] for c in range(NCORES)]

    nc_l1f = _get("l1ffn", build_launch_l1ffn)
    wbase_f = prep_w_fmaj(C(np.asarray(L1_ffn_W_base).T))
    rw_in = C(np.asarray(L1_ffn_router_W).T.reshape(ND, 128, 8)
              .transpose(1, 0, 2).reshape(128, ND * 8))
    A2d = np.asarray(L1_ffn_A).reshape(128, D)
    a_f = prep_w_fmaj(C(A2d.T))[0]
    BwT = C(np.asarray(L1_ffn_Bw).transpose(0, 2, 1).reshape(128, D))
    bw_f = prep_w_fmaj(BwT)
    bb_in = C(np.asarray(L1_ffn_b_base).reshape(ND, 128).T)
    e16 = np.zeros((8, 128), np.float32)
    for e in range(8):
        e16[e, e * 16:(e + 1) * 16] = 2.0
    e16 = e16.astype(BD)
    l1f_w = dict(wbase_f=wbase_f, rw_in=rw_in, a_f=a_f, bw_f=bw_f,
                 bb_in=bb_in, e16_in=e16)
    maps = [dict(common, x_in=xs[c], **l1f_w) for c in range(NCORES)]
    r = _run(nc_l1f, maps)
    xs = [r[c]["x_out"] for c in range(NCORES)]

    nc_sp = _get("swaproj", build_launch_swaproj)
    swq_f = prep_w_fmaj(C(np.asarray(L1_attn_Wq).T))
    swk_f = prep_w_fmaj(C(np.asarray(L1_attn_Wk).T))
    swv_d = prep_w_dmaj(C(np.asarray(L1_attn_Wv).T))
    maps = [dict(common, x_in=xs[c], wq_f=swq_f, wk_f=swk_f, wv_d=swv_d)
            for c in range(NCORES)]
    rp = _run(nc_sp, maps)

    nc_sc = _get("swacore", build_launch_swacore)
    swo_f = prep_w_fmaj(C(np.asarray(L1_attn_Wo).T))
    dmask = np.triu(np.ones((128, 128))).astype(BD)
    fmask = np.tril(np.ones((128, 128)), -1).astype(BD)
    kv0 = rp[0]["kv_out"]
    ksink = C(kv0[:, [kvh * T + j for kvh in range(NKV) for j in range(4)]])
    vsink = C(kv0[0:4, NKV * T:NKV * T + 512])
    zkv = np.zeros_like(kv0)
    maps = []
    for c in range(NCORES):
        prev = rp[c - 1]["kv_out"] if c >= 1 else zkv
        own = rp[c]["kv_out"]
        kctx = np.concatenate(
            [np.concatenate([prev[:, kvh * T:(kvh + 1) * T],
                             own[:, kvh * T:(kvh + 1) * T]], axis=1)
             for kvh in range(NKV)], axis=1)
        vctx = np.concatenate([prev[:, NKV * T:], own[:, NKV * T:]], axis=1)
        smask = np.zeros((4, T), np.float32)
        if c == 1:
            smask[:, :] = (np.arange(T)[None, :] >= np.arange(4)[:, None])
        elif c >= 2:
            smask[:, :] = 1.0
        nsp = np.zeros((1, T), np.float32)
        if c == 0:
            nsp[0, :] = 1023.0 - np.arange(T)
        maps.append(dict(common, x_in=xs[c], qt_d=rp[c]["qt_out"],
                         kctx_in=C(kctx), vctx_in=C(vctx), ksink_in=ksink,
                         vsink_in=vsink, dmask_in=dmask, fmask_in=fmask,
                         smask_in=smask.astype(BD), nsp_in=nsp, wo_f=swo_f))
    r = _run(nc_sc, maps)
    xs = [r[c]["x_out"] for c in range(NCORES)]

    maps = [dict(common, x_in=xs[c], **l1f_w) for c in range(NCORES)]
    r = _run(nc_l1f, maps)
    xs = [r[c]["x_out"] for c in range(NCORES)]

    return np.concatenate(xs, axis=0)[None, :, :]


# version 07 (final)
# speedup vs baseline: 1.0640x; 1.0640x over previous
import numpy as np
from contextlib import ExitStack, contextmanager
import concourse.bass as bass
import concourse.mybir as mybir
from concourse.bass_utils import run_bass_kernel_spmd
from concourse.alu_op_type import AluOpType as ALU

AF = mybir.ActivationFunctionType
F32 = mybir.dt.float32
BF16 = mybir.dt.bfloat16
I32 = mybir.dt.int32
AX = mybir.AxisListType.X

NCORES = 8
T = 1024
D = 2048
ND = D // 128
NT = T // 128
HF = 5120
NHF = HF // 128
NH = 16
CH = 64
SC = 32
NCH = T // CH
LN_EPS = 1e-5
GLA_SCALE = 128.0 ** -0.5
SWA_SCALE = 128.0 ** -0.5
SWA_SHIFT = 12.0

CONST_LIST = [LN_EPS, 1.0, -SWA_SHIFT, 0.0]



class Sem:
    __slots__ = ("h", "n")

    def __init__(self, h):
        self.h = h
        self.n = 0

    def inc(self, instr, k=16):
        instr.then_inc(self.h, k)
        self.n += k
        return self.n


class PendingInstr:
    __slots__ = ("incs",)

    def __init__(self):
        self.incs = []

    def then_inc(self, sem_h, n):
        self.incs.append((sem_h, n))
        return self


class EngRec:
    __slots__ = ("k", "name")

    def __init__(self, k, name):
        self.k = k
        self.name = name

    def __getattr__(self, meth):
        q = self.k.q[self.name]

        def call(*a, **kw):
            p = PendingInstr()
            q.append((meth, a, kw, p))
            return p
        return call


class Kern:
    ENGS = ("tensor", "vector", "scalar", "sync", "gpsimd")

    def __init__(self):
        self.nc = bass.Bass("TRN2", target_bir_lowering=False)
        self.es = ExitStack()
        self.cur_es = self.es
        self.block = None
        self._uid = 0
        self.q = {e: [] for e in self.ENGS}

    def uid(self):
        self._uid += 1
        return self._uid

    def dram(self, name, shape, dt=F32, kind="ExternalInput"):
        return self.nc.dram_tensor(name, list(shape), dt, kind=kind).ap()

    def sb(self, name, shape, dt=F32):
        return self.cur_es.enter_context(
            self.nc.sbuf_tensor(f"{name}_{self.uid()}", list(shape), dt))

    def ps(self, name, shape, dt=F32):
        return self.cur_es.enter_context(
            self.nc.psum_tensor(f"{name}_{self.uid()}", list(shape), dt))

    def sem(self, name):
        return Sem(self.es.enter_context(self.nc.semaphore(f"{name}_{self.uid()}")))

    @contextmanager
    def scope(self):
        es = ExitStack()
        old = self.cur_es
        self.cur_es = es
        try:
            yield
        finally:
            self.cur_es = old
            es.close()

    def start(self):
        self.block = self.es.enter_context(self.nc.Block())
        self.bar = self.sem("g_bar")
        self.dma = self.sem("g_dma")
        self.junk = self.sb("junk", [128, 1536], F32)

    def eng(self, name):
        return EngRec(self, name)

    def flush(self):
        for name in self.ENGS:
            queue = self.q[name]
            if not queue:
                continue

            def runner(e, queue=queue):
                for meth, a, kw, p in queue:
                    ins = getattr(e, meth)(*a, **kw)
                    for sem_h, n in p.incs:
                        ins.then_inc(sem_h, n)
            getattr(self.block, name)(runner)
            self.q[name] = []

    def barrier(self, engines=None):
        engines = engines or self.ENGS
        n = len(engines)
        base = self.bar.n
        for name in engines:
            e = self.eng(name)
            if name == "vector":
                e.memset(self.junk[:, 0:512], 0.0).then_inc(self.bar.h, 1)
            elif name == "scalar":
                e.memzero(self.junk[:, 512:1024]).then_inc(self.bar.h, 1)
            elif name == "gpsimd":
                e.memset(self.junk[:, 1024:1536], 0.0).then_inc(self.bar.h, 1)
            else:
                e.nop().then_inc(self.bar.h, 1)
        self.bar.n += n
        for name in engines:
            self.eng(name).wait_ge(self.bar.h, base + n)

    def finish(self):
        self.flush()
        self.es.close()
        return self.nc



def prep_w_fmaj(W, kblk=None):
    Din, Dout = W.shape
    nd, nf = Din // 128, Dout // 128
    return np.ascontiguousarray(
        W.reshape(nd, 128, nf, 128).transpose(2, 1, 0, 3).reshape(nf, 128, Din))


def prep_w_dmaj(W):
    Din, Dout = W.shape
    return np.ascontiguousarray(W.reshape(Din // 128, 128, Dout))


def host_common_inputs():
    import ml_dtypes
    return {
        "cpool": np.tile(np.array(CONST_LIST, np.float32), (128, 1)),
        "identF": np.eye(128, dtype=np.float32),
        "identB": np.eye(128).astype(ml_dtypes.bfloat16),
        "onesB": np.ones((128, 128), ml_dtypes.bfloat16),
        "onesF": np.ones((1, 128), np.float32),
    }


def declare_common_inputs(k):
    k.in_cpool = k.dram("cpool", [128, len(CONST_LIST)], F32)
    k.in_identF = k.dram("identF", [128, 128], F32)
    k.in_identB = k.dram("identB", [128, 128], BF16)
    k.in_onesB = k.dram("onesB", [128, 128], BF16)
    k.in_onesF = k.dram("onesF", [1, 128], F32)


def emit_common(k):
    k.cpool = k.sb("cpool", [128, len(CONST_LIST)])
    for i, v in enumerate(CONST_LIST):
        k.nc.const_aps.register(F32, float(v), k.cpool[:, i:i + 1])
    k.identF = k.sb("identF", [128, 128], F32)
    k.identB = k.sb("identB", [128, 128], BF16)
    k.ones_col = k.sb("ones_col", [128, 1], BF16)
    k.ones_row = k.sb("ones_row", [1, 128], BF16)
    k.ones_rowF = k.sb("ones_rowF", [1, 128], F32)
    s = k.eng("sync")
    k.dma.inc(s.dma_start(out=k.cpool[:], in_=k.in_cpool[:]))
    k.dma.inc(s.dma_start(out=k.identF[:], in_=k.in_identF[:]))
    k.dma.inc(s.dma_start(out=k.identB[:], in_=k.in_identB[:]))
    k.dma.inc(s.dma_start(out=k.ones_col[:], in_=k.in_onesB[0:128, 0:1],
                          allow_slow_non_contiguous=True))
    k.dma.inc(s.dma_start(out=k.ones_row[:], in_=k.in_onesB[0:1, 0:128]))
    k.dma.inc(s.dma_start(out=k.ones_rowF[:], in_=k.in_onesF[0:1, 0:128]))
    s.wait_ge(k.dma.h, k.dma.n)
    k.barrier()



def emit_in_transpose(k, x_in):
    with k.scope():
        xin = k.sb("xin_stage", [128, NT * D], F32)
        s = k.eng("sync")
        for tt in range(NT):
            k.dma.inc(s.dma_start(out=xin[:, tt * D:(tt + 1) * D],
                                  in_=x_in[tt * 128:(tt + 1) * 128, :]))
        dma_done = k.dma.n

        te = k.eng("tensor")
        v = k.eng("vector")
        te.wait_ge(k.dma.h, dma_done)
        tp = [k.ps("tp_in", [128, 512], F32) for _ in range(2)]
        s_tp = k.sem("s_tp_in")
        s_ev = k.sem("s_ev_in")
        grp = 0
        for dc in range(ND):
            for tg in range(2):
                pb = tp[grp % 2]
                if grp >= 2:
                    te.wait_ge(s_ev.h, grp - 1)
                for j in range(4):
                    tt = tg * 4 + j
                    ins = te.transpose(
                        pb[:, j * 128:(j + 1) * 128],
                        xin[:, tt * D + dc * 128: tt * D + (dc + 1) * 128],
                        k.identF[:])
                s_tp.inc(ins, 1)
                v.wait_ge(s_tp.h, grp + 1)
                ins = v.tensor_copy(
                    k.xT[:, dc * T + tg * 512: dc * T + (tg + 1) * 512], pb[:])
                s_ev.inc(ins, 1)
                grp += 1
        k.barrier()


def emit_out_transpose(k, x_out):
    with k.scope():
        xout = k.sb("xout_stage", [128, NT * D], F32)
        te = k.eng("tensor")
        v = k.eng("vector")
        tp = [k.ps("tp_out", [128, 512], F32) for _ in range(2)]
        s_tp = k.sem("s_tp_out")
        s_ev = k.sem("s_ev_out")
        grp = 0
        for tt in range(NT):
            for dg in range(4):
                pb = tp[grp % 2]
                if grp >= 2:
                    te.wait_ge(s_ev.h, grp - 1)
                for j in range(4):
                    dc = dg * 4 + j
                    ins = te.transpose(
                        pb[:, j * 128:(j + 1) * 128],
                        k.xT[:, dc * T + tt * 128: dc * T + (tt + 1) * 128],
                        k.identF[:])
                s_tp.inc(ins, 1)
                v.wait_ge(s_tp.h, grp + 1)
                ins = v.tensor_copy(
                    xout[:, tt * D + dg * 512: tt * D + (dg + 1) * 512], pb[:])
                s_ev.inc(ins, 1)
                grp += 1
        s = k.eng("sync")
        s.wait_ge(s_ev.h, grp)
        for tt in range(NT):
            k.dma.inc(s.dma_start(out=x_out[tt * 128:(tt + 1) * 128, :],
                                  in_=xout[:, tt * D:(tt + 1) * D]))
        s.wait_ge(k.dma.h, k.dma.n)
        k.barrier()



def emit_ln(k, xn):
    with k.scope():
        xbc = [k.sb("ln_xbc", [128, T], BF16) for _ in range(2)]
        sqc = [k.sb("ln_sqc", [128, T], BF16) for _ in range(2)]
        v = k.eng("vector")
        a = k.eng("scalar")
        te = k.eng("tensor")
        p_s1 = k.ps("ln_ps1", [1, T], F32)
        p_s2 = k.ps("ln_ps2", [1, T], F32)
        s_sq = k.sem("ln_s_sq")
        s_mm = k.sem("ln_s_mm")
        for dc in range(ND):
            if dc >= 2:
                v.wait_ge(s_mm.h, dc - 1)
                a.wait_ge(s_mm.h, dc - 1)
            ins = v.tensor_copy(xbc[dc % 2][:], k.xT[:, dc * T:(dc + 1) * T])
            s_sq.inc(ins, 1)
            ins = a.activation(sqc[dc % 2][:], k.xT[:, dc * T:(dc + 1) * T],
                               AF.Square)
            s_sq.inc(ins, 1)
            te.wait_ge(s_sq.h, 2 * (dc + 1))
            for half in range(2):
                hs = slice(half * 512, (half + 1) * 512)
                te.matmul(p_s1[:, hs], k.ones_col[:], xbc[dc % 2][:, hs],
                          start=(dc == 0), stop=(dc == ND - 1))
                ins = te.matmul(p_s2[:, hs], k.ones_col[:], sqc[dc % 2][:, hs],
                                start=(dc == 0), stop=(dc == ND - 1))
            s_mm.inc(ins, 1)

        m_row = k.sb("ln_m", [1, T], F32)
        var_row = k.sb("ln_var", [1, T], F32)
        msq_row = k.sb("ln_msq", [1, T], F32)
        r_row = k.sb("ln_r", [1, T], F32)
        mr_row = k.sb("ln_mr", [1, T], F32)
        s_st = k.sem("ln_s_st")
        v.wait_ge(s_mm.h, s_mm.n)
        v.tensor_scalar_mul(m_row[:], p_s1[:], 1.0 / D)
        v.tensor_scalar_mul(var_row[:], p_s2[:], 1.0 / D)
        v.tensor_tensor(msq_row[:], m_row[:], m_row[:], ALU.mult)
        ins = v.tensor_tensor(var_row[:], var_row[:], msq_row[:], ALU.subtract)
        s_st.inc(ins, 1)
        a.wait_ge(s_st.h, s_st.n)
        a.activation(r_row[:], var_row[:], AF.Ln, bias=LN_EPS)
        ins = a.activation(r_row[:], r_row[:], AF.Exp, scale=-0.5)
        s_st.inc(ins, 1)
        v.wait_ge(s_st.h, s_st.n)
        ins = v.tensor_tensor(mr_row[:], m_row[:], r_row[:], ALU.mult)
        s_st.inc(ins, 1)

        p_rb = k.ps("ln_prb", [128, 512], F32)
        p_mrb = k.ps("ln_pmrb", [128, 512], F32)
        rb = k.sb("ln_rb", [128, T], F32)
        mrb = k.sb("ln_mrb", [128, T], F32)
        s_bc = k.sem("ln_s_bc")
        te.wait_ge(s_st.h, s_st.n)
        for half in range(2):
            hs = slice(half * 512, (half + 1) * 512)
            te.matmul(p_rb[:], k.ones_rowF[:], r_row[:, hs], start=True, stop=True)
            ins = te.matmul(p_mrb[:], k.ones_rowF[:], mr_row[:, hs],
                            start=True, stop=True)
            s_bc.inc(ins, 1)
            v.wait_ge(s_bc.h, s_bc.n)
            v.tensor_copy(rb[:, hs], p_rb[:])
            ins = v.tensor_copy(mrb[:, hs], p_mrb[:])
            s_bc.inc(ins, 1)
            te.wait_ge(s_bc.h, s_bc.n)
        tmp = k.sb("ln_tmp", [128, T], F32)
        for dc in range(ND):
            v.tensor_tensor(tmp[:], k.xT[:, dc * T:(dc + 1) * T], rb[:],
                            ALU.mult)
            v.tensor_tensor(xn[:, dc * T:(dc + 1) * T], tmp[:], mrb[:],
                            ALU.subtract)
        k.barrier()



def emit_absmean(k, wfct, wprojt, am_out):
    k.thr_fc = k.sb("thr_fc", [128, 1], F32)
    k.thr_proj = k.sb("thr_proj", [128, 1], F32)
    k.ffn_oscale = k.sb("ffn_oscale", [128, 1], F32)
    sc_dram = k.dram(f"am_scratch_{k.uid()}", [1, 4], F32, kind="Internal")
    with k.scope():
        wbuf = [k.sb("am_w", [128, D], F32) for _ in range(2)]
        wb2 = [k.sb("am_w2", [128, HF // 2], F32) for _ in range(2)]
        reds1 = k.sb("am_reds1", [128, NHF], F32)
        reds2 = k.sb("am_reds2", [128, 2 * ND], F32)
        acc = k.sb("am_acc", [128, 2], F32)
        p_t = k.ps("am_pt", [2, 128], F32)
        tot = k.sb("am_tot", [2, 1], F32)
        s = k.eng("sync")
        v = k.eng("vector")
        s_r = k.sem("am_s_r")
        dma0 = k.dma.n
        for fb in range(NHF):
            k.dma.inc(s.dma_start(out=wbuf[fb % 2][:], in_=wfct[fb]))
            if fb >= 1:
                s.wait_ge(s_r.h, fb)
        for fb in range(NHF):
            v.wait_ge(k.dma.h, dma0 + 16 * (fb + 1))
            ins = v.reduce_sum(reds1[:, fb:fb + 1], wbuf[fb % 2][:], axis=AX,
                               apply_absolute_value=True)
            s_r.inc(ins, 1)
        dma1 = k.dma.n
        for fb in range(2 * ND):
            k.dma.inc(s.dma_start(
                out=wb2[fb % 2][:],
                in_=wprojt[fb // 2][:, (fb % 2) * (HF // 2):(fb % 2 + 1) * (HF // 2)]))
            if fb >= 1:
                s.wait_ge(s_r.h, NHF + fb)
        for fb in range(2 * ND):
            v.wait_ge(k.dma.h, dma1 + 16 * (fb + 1))
            ins = v.reduce_sum(reds2[:, fb:fb + 1], wb2[fb % 2][:], axis=AX,
                               apply_absolute_value=True)
            s_r.inc(ins, 1)
        v.wait_ge(s_r.h, s_r.n)
        v.reduce_sum(acc[:, 0:1], reds1[:], axis=AX)
        ins = v.reduce_sum(acc[:, 1:2], reds2[:], axis=AX)
        s_r.inc(ins, 1)
        te = k.eng("tensor")
        s_mm = k.sem("am_s_mm")
        te.wait_ge(s_r.h, s_r.n)
        ins = te.transpose(p_t[:], acc[:], k.identF[:])
        s_mm.inc(ins, 1)
        v.wait_ge(s_mm.h, 1)
        ins = v.reduce_sum(tot[:], p_t[:], axis=AX)
        s_mm.inc(ins, 1)
        v.wait_ge(s_mm.h, 2)
        thr2w = k.sb("am_thr2w", [2, 128], F32)
        am2w = k.sb("am_am2w", [2, 128], F32)
        v.tensor_copy(k.junk[:, 0:512], k.junk[:, 0:512])
        v.tensor_scalar_mul(thr2w[:], tot[:].broadcast_to([2, 128]),
                            0.7 / (HF * D))
        ins = v.tensor_scalar_mul(am2w[:], tot[:].broadcast_to([2, 128]),
                                  1.0 / (HF * D))
        s_mm.inc(ins, 1)
        s = k.eng("sync")
        s.wait_ge(s_mm.h, 3)
        k.dma.inc(s.dma_start(out=sc_dram[0:1, 0:1], in_=thr2w[0:1, 0:1]))
        k.dma.inc(s.dma_start(out=sc_dram[0:1, 1:2], in_=thr2w[1:2, 0:1]))
        k.dma.inc(s.dma_start(out=sc_dram[0:1, 2:3], in_=am2w[0:1, 0:1]))
        k.dma.inc(s.dma_start(out=sc_dram[0:1, 3:4], in_=am2w[1:2, 0:1]))
        k.dma.inc(s.dma_start(out=am_out[0:1, 0:1], in_=am2w[0:1, 0:1]))
        k.dma.inc(s.dma_start(out=am_out[0:1, 1:2], in_=am2w[1:2, 0:1]))
        s.wait_ge(k.dma.h, k.dma.n)
        amrow = k.sb("am_row", [1, 4], F32)
        oscw = k.sb("am_oscw", [1, 128], F32)
        k.dma.inc(s.dma_start(out=amrow[:], in_=sc_dram[:]))
        s.wait_ge(k.dma.h, k.dma.n)
        s_f = k.sem("am_s_f")
        v.wait_ge(k.dma.h, k.dma.n)
        v.scalar_tensor_tensor(oscw[:], amrow[:, 2:3].broadcast_to([1, 128]),
                               0.5, amrow[:, 3:4].broadcast_to([1, 128]),
                               ALU.mult, ALU.mult)
        ins = v.tensor_tensor(oscw[:], oscw[:],
                              amrow[:, 2:3].broadcast_to([1, 128]), ALU.mult)
        s_f.inc(ins, 1)
        s.wait_ge(s_f.h, 1)
        k.dma.inc(s.dma_start(out=sc_dram[0:1, 2:3], in_=oscw[0:1, 0:1]))
        s.wait_ge(k.dma.h, k.dma.n)
        bsrc = sc_dram.broadcast_to([128, 4])
        k.dma.inc(s.dma_start(out=k.thr_fc[:], in_=bsrc[:, 0:1],
                              allow_slow_non_contiguous=True))
        k.dma.inc(s.dma_start(out=k.thr_proj[:], in_=bsrc[:, 1:2],
                              allow_slow_non_contiguous=True))
        k.dma.inc(s.dma_start(out=k.ffn_oscale[:], in_=bsrc[:, 2:3],
                              allow_slow_non_contiguous=True))
        s.wait_ge(k.dma.h, k.dma.n)
        k.barrier()


def emit_ffn0(k, wfct, wprojt, hdram, qfc_out=None, qproj_out=None):
    with k.scope():
        emit_ffn0_fc_only(k, wfct, hdram, qfc_out=qfc_out)
        emit_ffn0_proj_only(k, wprojt, hdram, qproj_out=qproj_out)


def emit_ffn0_fc_only(k, wfct, hdram, xn_spill=None, qfc_out=None):
    if True:
        with k.scope():
            xn = k.sb("ffn_xn", [128, ND * T], BF16)
            emit_ln(k, xn)
            if xn_spill is not None:
                sdbg = k.eng("sync")
                k.dma.inc(sdbg.dma_start(out=xn_spill[:], in_=xn[:]))
                sdbg.wait_ge(k.dma.h, k.dma.n)
                k.barrier()
            wraw = [k.sb("ffn_wraw", [128, D], F32) for _ in range(2)]
            wq = [k.sb("ffn_wq", [128, D], BF16) for _ in range(2)]
            sc_abs = k.sb("ffn_scabs", [128, D], F32)
            sc_m = k.sb("ffn_scm", [128, D], BF16)
            sc_sgn = k.sb("ffn_scsgn", [128, D], BF16)
            relu_f32 = k.sb("ffn_relu", [128, T], F32)
            hstage = [k.sb("ffn_hstage", [128, 512], BF16) for _ in range(2)]
            p_h = [k.ps("ffn_ph", [128, 512], F32) for _ in range(4)]
            s_q = k.sem("ffn_s_q")
            s_sgn = k.sem("ffn_s_sgn")
            s_mm = k.sem("ffn_s_mm")
            s_h = k.sem("ffn_s_h")
            s_hd = k.sem("ffn_s_hd")
            s_qs = k.sem("ffn_s_qs")
            s = k.eng("sync")
            g = k.eng("gpsimd")
            v = k.eng("vector")
            a = k.eng("scalar")
            te = k.eng("tensor")
            dma0 = k.dma.n
            for fb in range(NHF):
                k.dma.inc(s.dma_start(out=wraw[fb % 2][:], in_=wfct[fb]))
                if fb >= 1:
                    s.wait_ge(s_q.h, fb)
            for fb in range(NHF):
                wr, wqt = wraw[fb % 2], wq[fb % 2]
                v.wait_ge(k.dma.h, dma0 + 16 * (fb + 1))
                if fb >= 2:
                    v.wait_ge(s_mm.h, 2 * (fb - 1))
                    if qfc_out is not None:
                        v.wait_ge(s_qs.h, 16 * (fb - 1))
                a.wait_ge(k.dma.h, dma0 + 16 * (fb + 1))
                if fb >= 1:
                    a.wait_ge(s_q.h, fb)
                a.activation(sc_abs[:], wr[:], AF.Abs)
                ins = a.activation(sc_sgn[:], wr[:], AF.Sign)
                s_sgn.inc(ins, 1)
                v.wait_ge(s_sgn.h, s_sgn.n)
                v.tensor_scalar(sc_m[:], sc_abs[:], k.thr_fc[:], None, ALU.is_gt)
                v.tensor_tensor(wqt[:], sc_m[:], sc_sgn[:], ALU.mult)
                ins = v.tensor_copy(k.junk[:, 0:512], k.junk[:, 0:512])
                s_q.inc(ins, 1)
                if qfc_out is not None:
                    g.wait_ge(s_q.h, fb + 1)
                    s_qs.inc(g.dma_start(out=qfc_out[fb], in_=wqt[:]))

                te.wait_ge(s_q.h, fb + 1)
                for half in range(2):
                    idx = 2 * fb + half
                    pb = p_h[idx % 4]
                    if idx >= 4:
                        te.wait_ge(s_h.h, idx - 3)
                    for dc in range(ND):
                        ins = te.matmul(
                            pb[:], wqt[:, dc * 128:(dc + 1) * 128],
                            xn[:, dc * T + half * 512: dc * T + (half + 1) * 512],
                            start=(dc == 0), stop=(dc == ND - 1))
                    s_mm.inc(ins, 1)
                    a.wait_ge(s_mm.h, idx + 1)
                    if idx >= 2:
                        a.wait_ge(s_h.h, idx - 1)
                    ins = a.activation(relu_f32[:, half * 512:(half + 1) * 512],
                                       pb[:], AF.Relu)
                    s_sgn.inc(ins, 1)
                    v.wait_ge(s_sgn.h, s_sgn.n)
                    if idx >= 2:
                        v.wait_ge(s_hd.h, 16 * (idx - 1))
                    ins = v.tensor_tensor(
                        hstage[idx % 2][:],
                        relu_f32[:, half * 512:(half + 1) * 512],
                        relu_f32[:, half * 512:(half + 1) * 512], ALU.mult)
                    s_h.inc(ins, 1)
                    g.wait_ge(s_h.h, idx + 1)
                    s_hd.inc(g.dma_start(
                        out=hdram[fb][:, half * 512:(half + 1) * 512],
                        in_=hstage[idx % 2][:]))
            g.wait_ge(s_hd.h, s_hd.n)
            k.barrier()


def emit_ffn0_proj_only(k, wprojt, hdram, qproj_out=None):
    if True:
        with k.scope():
            GRP = 4
            NG = ND // GRP
            HH = HF // 2
            wraw2 = [k.sb("ffn_w2raw", [128, HH], F32) for _ in range(2)]
            wqg = [[k.sb("ffn_w2q", [128, HF], BF16) for _ in range(GRP)]
                   for _ in range(2)]
            sc2_abs = k.sb("ffn_sc2abs", [128, HH], F32)
            sc2_m = k.sb("ffn_sc2m", [128, HH], BF16)
            sc2_sgn = k.sb("ffn_sc2sgn", [128, HH], BF16)
            o_f32 = [k.sb("ffn_of32", [128, 512], F32) for _ in range(2)]
            htile = [k.sb("ffn_htile", [128, T], BF16) for _ in range(2)]
            s_add = k.sem("ffn_s_add")
            s_q2 = k.sem("ffn_s_q2")
            s_sgn2 = k.sem("ffn_s_sgn2")
            s_mm2 = k.sem("ffn_s_mm2")
            s_hin = k.sem("ffn_s_hin")
            s_qs2 = k.sem("ffn_s_qs2")
            s = k.eng("sync")
            g = k.eng("gpsimd")
            v = k.eng("vector")
            a = k.eng("scalar")
            te = k.eng("tensor")
            wdma0 = k.dma.n
            for hb in range(2 * ND):
                k.dma.inc(s.dma_start(
                    out=wraw2[hb % 2][:],
                    in_=wprojt[hb // 2][:, (hb % 2) * HH:(hb % 2 + 1) * HH]))
                if hb >= 1:
                    s.wait_ge(s_q2.h, hb)
            for gi in range(NG):
                for hc in range(NHF):
                    j = gi * NHF + hc
                    if j >= 2:
                        g.wait_ge(s_mm2.h, j - 1)
                    ins = g.dma_start(out=htile[j % 2][:], in_=hdram[hc][:])
                    s_hin.inc(ins, 16)
            def emit_tern(gi):
                for fi in range(GRP):
                    fb = gi * GRP + fi
                    for piece in range(2):
                        hb = 2 * fb + piece
                        wr = wraw2[hb % 2]
                        wqt = wqg[gi % 2][fi]
                        v.wait_ge(k.dma.h, wdma0 + 16 * (hb + 1))
                        if gi >= 2 and fi == 0 and piece == 0:
                            v.wait_ge(s_mm2.h, (gi - 1) * NHF)
                        if qproj_out is not None and fb >= GRP:
                            v.wait_ge(s_qs2.h, 16 * (fb - GRP + 1))
                        a.wait_ge(k.dma.h, wdma0 + 16 * (hb + 1))
                        if hb >= 1:
                            a.wait_ge(s_q2.h, hb)
                        a.activation(sc2_abs[:], wr[:], AF.Abs)
                        ins = a.activation(sc2_sgn[:], wr[:], AF.Sign)
                        s_sgn2.inc(ins, 1)
                        v.wait_ge(s_sgn2.h, s_sgn2.n)
                        v.tensor_scalar(sc2_m[:], sc2_abs[:], k.thr_proj[:],
                                        None, ALU.is_gt)
                        v.tensor_tensor(wqt[:, piece * HH:(piece + 1) * HH],
                                        sc2_m[:], sc2_sgn[:], ALU.mult)
                        ins = v.tensor_copy(k.junk[:, 0:512], k.junk[:, 0:512])
                        s_q2.inc(ins, 1)
                    if qproj_out is not None:
                        a.wait_ge(s_q2.h, 2 * fb + 2)
                        s_qs2.inc(a.dma_start(out=qproj_out[fb],
                                              in_=wqg[gi % 2][fi][:]))
            emit_tern(0)
            if NG > 1:
                emit_tern(1)
            for gi in range(NG):
                te.wait_ge(s_q2.h, (gi + 1) * GRP * 2)
                if gi >= 1:
                    te.wait_ge(s_add.h, gi * 2 * GRP)
                with k.scope():
                    p_o = [k.ps("ffn_po", [128, T], F32) for _ in range(GRP)]
                    for hc in range(NHF):
                        j = gi * NHF + hc
                        te.wait_ge(s_hin.h, 16 * (j + 1))
                        for fi in range(GRP):
                            for half in range(2):
                                ins = te.matmul(
                                    p_o[fi][:, half * 512:(half + 1) * 512],
                                    wqg[gi % 2][fi][:, hc * 128:(hc + 1) * 128],
                                    htile[j % 2][:, half * 512:(half + 1) * 512],
                                    start=(hc == 0), stop=(hc == NHF - 1))
                        s_mm2.inc(ins, 1)
                    for fi in range(GRP):
                        fb = gi * GRP + fi
                        for half in range(2):
                            hi = 2 * (gi * GRP + fi) + half
                            ob = o_f32[hi % 2]
                            a.wait_ge(s_mm2.h, (gi + 1) * NHF)
                            if hi >= 2:
                                a.wait_ge(s_add.h, hi - 1)
                            ins = a.activation(
                                ob[:],
                                p_o[fi][:, half * 512:(half + 1) * 512],
                                AF.Copy, scale=k.ffn_oscale[:])
                            s_sgn2.inc(ins, 1)
                            v.wait_ge(s_sgn2.h, s_sgn2.n)
                            xsl = slice(fb * T + half * 512,
                                        fb * T + (half + 1) * 512)
                            ins = v.tensor_tensor(k.xT[:, xsl], k.xT[:, xsl],
                                                  ob[:], ALU.add)
                            s_add.inc(ins, 1)
                if gi + 2 < NG:
                    emit_tern(gi + 2)
            k.barrier()


def emit_ffn_scales(k, am_in):
    k.ffn_oscale = k.sb("ffn_oscale2", [128, 1], F32)
    sc_dram = k.dram(f"fsc_scratch_{k.uid()}", [1, 1], F32, kind="Internal")
    with k.scope():
        amrow = k.sb("fsc_amrow", [1, 2], F32)
        oscw = k.sb("fsc_oscw", [1, 128], F32)
        s = k.eng("sync")
        v = k.eng("vector")
        k.dma.inc(s.dma_start(out=amrow[:], in_=am_in[:]))
        s.wait_ge(k.dma.h, k.dma.n)
        s_f = k.sem("fsc_s_f")
        v.wait_ge(k.dma.h, k.dma.n)
        v.scalar_tensor_tensor(oscw[:], amrow[:, 0:1].broadcast_to([1, 128]),
                               0.5, amrow[:, 1:2].broadcast_to([1, 128]),
                               ALU.mult, ALU.mult)
        ins = v.tensor_tensor(oscw[:], oscw[:],
                              amrow[:, 0:1].broadcast_to([1, 128]), ALU.mult)
        s_f.inc(ins, 1)
        s.wait_ge(s_f.h, 1)
        k.dma.inc(s.dma_start(out=sc_dram[0:1, 0:1], in_=oscw[0:1, 0:1]))
        s.wait_ge(k.dma.h, k.dma.n)
        k.dma.inc(s.dma_start(out=k.ffn_oscale[:],
                              in_=sc_dram.broadcast_to([128, 1]),
                              allow_slow_non_contiguous=True))
        s.wait_ge(k.dma.h, k.dma.n)
        k.barrier()


def emit_ffn0_pre(k, qfc_in, qproj_in, hdram):
    with k.scope():
        with k.scope():
            xn = k.sb("ffp_xn", [128, ND * T], BF16)
            emit_ln(k, xn)
            wq = [k.sb("ffp_wq", [128, D], BF16) for _ in range(2)]
            relu_f32 = k.sb("ffp_relu", [128, T], F32)
            hstage = [k.sb("ffp_hstage", [128, T], BF16) for _ in range(2)]
            p_h = [k.ps("ffp_ph", [128, 512], F32) for _ in range(4)]
            s_mm = k.sem("ffp_s_mm")
            s_h = k.sem("ffp_s_h")
            s_hd = k.sem("ffp_s_hd")
            s = k.eng("sync")
            g = k.eng("gpsimd")
            v = k.eng("vector")
            a = k.eng("scalar")
            te = k.eng("tensor")
            dma0 = k.dma.n
            for fb in range(NHF):
                k.dma.inc(s.dma_start(out=wq[fb % 2][:], in_=qfc_in[fb]))
                if fb >= 1:
                    s.wait_ge(s_mm.h, 2 * fb)
            for fb in range(NHF):
                te.wait_ge(k.dma.h, dma0 + 16 * (fb + 1))
                for half in range(2):
                    idx = 2 * fb + half
                    pb = p_h[idx % 4]
                    if idx >= 4:
                        te.wait_ge(s_h.h, idx - 3)
                    for dc in range(ND):
                        ins = te.matmul(
                            pb[:], wq[fb % 2][:, dc * 128:(dc + 1) * 128],
                            xn[:, dc * T + half * 512: dc * T + (half + 1) * 512],
                            start=(dc == 0), stop=(dc == ND - 1))
                    s_mm.inc(ins, 1)
                    a.wait_ge(s_mm.h, idx + 1)
                    if idx >= 2:
                        a.wait_ge(s_h.h, idx - 1)
                    ins = a.activation(relu_f32[:, half * 512:(half + 1) * 512],
                                       pb[:], AF.Relu)
                    k.sq3 = getattr(k, "sq3", None) or k.sem("ffp_s_sq3")
                    k.sq3.inc(ins, 1)
                    v.wait_ge(k.sq3.h, k.sq3.n)
                    if fb >= 2 and half == 0:
                        v.wait_ge(s_hd.h, 16 * (fb - 1))
                    ins = v.tensor_tensor(
                        hstage[fb % 2][:, half * 512:(half + 1) * 512],
                        relu_f32[:, half * 512:(half + 1) * 512],
                        relu_f32[:, half * 512:(half + 1) * 512], ALU.mult)
                    s_h.inc(ins, 1)
                g.wait_ge(s_h.h, 2 * fb + 2)
                s_hd.inc(g.dma_start(out=hdram[fb][:], in_=hstage[fb % 2][:]))
            g.wait_ge(s_hd.h, s_hd.n)
            k.barrier()

        with k.scope():
            GRP = 4
            NG = ND // GRP
            HB = 4
            wqg = [[k.sb("ffp_w2q", [128, HF], BF16) for _ in range(GRP)]
                   for _ in range(2)]
            o_f32 = [k.sb("ffp_of32", [128, 512], F32) for _ in range(2)]
            htile = [k.sb("ffp_htile", [128, HB * T], BF16) for _ in range(2)]
            s_add = k.sem("ffp_s_add")
            s_mm2 = k.sem("ffp_s_mm2")
            s_ev = k.sem("ffp_s_ev")
            s_hin = k.sem("ffp_s_hin")
            s = k.eng("sync")
            g = k.eng("gpsimd")
            v = k.eng("vector")
            a = k.eng("scalar")
            te = k.eng("tensor")
            NB = NHF // HB
            for gi in range(NG):
                for fi in range(GRP):
                    fb = gi * GRP + fi
                    if gi >= 2 and fi == 0:
                        s.wait_ge(s_mm2.h, (gi - 1) * NB)
                    k.dma.inc(s.dma_start(out=wqg[gi % 2][fi][:],
                                          in_=qproj_in[fb]))
            wdma0 = k.dma.n - 16 * ND
            for gi in range(NG):
                for hb in range(NB):
                    j = gi * NB + hb
                    if j >= 2:
                        g.wait_ge(s_mm2.h, j - 1)
                    ins = g.dma_start(
                        out=htile[j % 2][:].rearrange("p (c t) -> p c t", c=HB),
                        in_=hdram[hb * HB:(hb + 1) * HB].transpose([1, 0, 2]))
                    s_hin.inc(ins)
            for gi in range(NG):
                te.wait_ge(k.dma.h, wdma0 + 16 * (gi + 1) * GRP)
                if gi >= 1:
                    te.wait_ge(s_add.h, gi * 2 * GRP)
                with k.scope():
                    p_o = [k.ps("ffp_po", [128, T], F32) for _ in range(GRP)]
                    for hb in range(NB):
                        j = gi * NB + hb
                        te.wait_ge(s_hin.h, 16 * (j + 1))
                        for hc4 in range(HB):
                            hc = hb * HB + hc4
                            for fi in range(GRP):
                                for half in range(2):
                                    ins = te.matmul(
                                        p_o[fi][:, half * 512:(half + 1) * 512],
                                        wqg[gi % 2][fi][:, hc * 128:(hc + 1) * 128],
                                        htile[j % 2][:, hc4 * T + half * 512:
                                                     hc4 * T + (half + 1) * 512],
                                        start=(hc == 0), stop=(hc == NHF - 1))
                        s_mm2.inc(ins, 1)
                    for fi in range(GRP):
                        fb = gi * GRP + fi
                        ai = gi * GRP + fi
                        for half in range(2):
                            hi = 2 * ai + half
                            ob = o_f32[hi % 2]
                            a.wait_ge(s_mm2.h, (gi + 1) * NB)
                            if hi >= 2:
                                a.wait_ge(s_add.h, hi - 1)
                            ins = a.activation(ob[:],
                                               p_o[fi][:, half * 512:(half + 1) * 512],
                                               AF.Copy, scale=k.ffn_oscale[:])
                            s_ev.inc(ins, 1)
                            v.wait_ge(s_ev.h, s_ev.n)
                            xsl = slice(fb * T + half * 512,
                                        fb * T + (half + 1) * 512)
                            ins = v.tensor_tensor(k.xT[:, xsl], k.xT[:, xsl],
                                                  ob[:], ALU.add)
                            s_add.inc(ins, 1)
            k.barrier()



def chain_step(k, sem, engname, fn):
    e = k.eng(engname)
    e.wait_ge(sem.h, sem.n)
    ins = fn(e)
    sem.inc(ins, 1)
    return sem.n


def emit_gla_prep_pass1(k, wq_f, wk_f, wi_d, qt_d, kt_d, kh_d, vt_d, fac_d,
                        td_out, x_in):
    s = k.eng("sync")
    g = k.eng("gpsimd")
    v = k.eng("vector")
    a = k.eng("scalar")
    te = k.eng("tensor")

    xn = k.sb("gla_xn", [128, ND * T], BF16)
    with k.scope():
        k.xT = k.sb("xT_gla", [128, ND * T], F32)
        emit_in_transpose(k, x_in)
        emit_ln(k, xn)
    k.kh_sb = k.sb("kh_sb", [128, NH * T], BF16)
    k.vt_sb = k.sb("vt_sb", [128, NT * D], BF16)
    k.dfp_sb = k.sb("dfp_sb", [128, NH * NCH], F32)
    dcs_sb = k.sb("gla_dcs", [128, NH], F32)

    with k.scope():
        wqb = [k.sb("gla_wqb", [128, D], F32) for _ in range(2)]
        wkb = [k.sb("gla_wkb", [128, D], F32) for _ in range(2)]
        wqc = k.sb("gla_wqc", [128, D], BF16)
        wkc = k.sb("gla_wkc", [128, D], BF16)
        kraw = k.sb("gla_kraw", [128, T], F32)
        sp = k.sb("gla_sp", [128, T], F32)
        t1 = k.sb("gla_t1", [128, T], F32)
        t2 = k.sb("gla_t2", [128, T], F32)
        ap_ = k.sb("gla_ap", [128, T], F32)
        ep = k.sb("gla_ep", [128, T], F32)
        em = k.sb("gla_em", [128, T], F32)
        qrel = k.sb("gla_qrel", [128, T], F32)
        qt_t = [k.sb("gla_qt", [128, T], BF16) for _ in range(2)]
        kt_t = [k.sb("gla_kt", [128, T], BF16) for _ in range(2)]
        kh_t = [k.sb("gla_kh", [128, T], BF16) for _ in range(2)]
        facd = [k.sb("gla_facd", [128, 3 * NCH], F32) for _ in range(2)]
        s_tp = k.sem("gla_s_tp")
        s_ev = k.sem("gla_s_ev")
        p_kq = k.ps("gla_pkq", [128, 1024], F32)
        p_q = k.ps("gla_pq", [128, 1024], F32)
        p_tp = k.ps("gla_ptp", [128, 1024], BF16)
        s_w = k.sem("gla_s_w")
        s_mm = k.sem("gla_s_mm")
        s_ch = k.sem("gla_s_ch")
        s_st = k.sem("gla_s_st")

        k.gla_evac_done = {}
        k.gla_chain_done = {}
        dma0 = k.dma.n
        for h in range(NH):
            k.dma.inc(s.dma_start(out=wqb[h % 2][:], in_=wq_f[h]))
            k.dma.inc(s.dma_start(out=wkb[h % 2][:], in_=wk_f[h]))
            if h >= 1:
                s.wait_ge(s_w.h, h)

        def emit_khtp(h):
            te2 = k.eng("tensor")
            te2.wait_ge(s_ch.h, k.gla_chain_done[h])
            if h >= 1:
                te2.wait_ge(s_ev.h, h)
            for tg in range(2):
                for j in range(4):
                    tt = tg * 4 + j
                    ins = te2.transpose(
                        p_tp[:, tg * 512 + j * 128: tg * 512 + (j + 1) * 128],
                        kh_t[h % 2][:, tt * 128:(tt + 1) * 128], k.identB[:])
            s_tp.inc(ins, 1)
            v2 = k.eng("vector")
            v2.wait_ge(s_tp.h, h + 1)
            ins = v2.tensor_copy(k.kh_sb[:, h * T:(h + 1) * T], p_tp[:])
            s_ev.inc(ins, 1)
            g2 = k.eng("gpsimd")
            g2.wait_ge(s_ev.h, h + 1)
            s_st.inc(g2.dma_start(out=qt_d[h], in_=qt_t[h % 2][:]))
            s_st.inc(g2.dma_start(out=kt_d[h], in_=kt_t[h % 2][:]))
            s_st.inc(g2.dma_start(out=kh_d[h],
                                  in_=k.kh_sb[:, h * T:(h + 1) * T]))
            s_st.inc(g2.dma_start(out=fac_d[h], in_=facd[h % 2][:]))

        for h in range(NH):
            v.wait_ge(k.dma.h, dma0 + 32 * (h + 1))
            if h >= 1:
                v.wait_ge(s_mm.h, 2 * h)
            v.tensor_copy(wqc[:], wqb[h % 2][:])
            ins = v.tensor_copy(wkc[:], wkb[h % 2][:])
            s_w.inc(ins, 1)
            if h >= 1:
                emit_khtp(h - 1)
            te.wait_ge(s_w.h, h + 1)
            if h >= 1:
                te.wait_ge(s_ch.h, k.gla_evac_done[h - 1])
            for half in range(2):
                for dc in range(ND):
                    ins = te.matmul(
                        p_kq[:, half * 512:(half + 1) * 512],
                        wkc[:, dc * 128:(dc + 1) * 128],
                        xn[:, dc * T + half * 512: dc * T + (half + 1) * 512],
                        start=(dc == 0), stop=(dc == ND - 1))
            s_mm.inc(ins, 1)
            for half in range(2):
                for dc in range(ND):
                    ins = te.matmul(
                        p_q[:, half * 512:(half + 1) * 512],
                        wqc[:, dc * 128:(dc + 1) * 128],
                        xn[:, dc * T + half * 512: dc * T + (half + 1) * 512],
                        start=(dc == 0), stop=(dc == ND - 1))
            s_mm.inc(ins, 1)

            C = lambda eng, fn: chain_step(k, s_ch, eng, fn)
            if h >= 2:
                k.eng("vector").wait_ge(s_st.h, 64 * (h - 1))
            k.eng("scalar").wait_ge(s_mm.h, 2 * h + 1)
            C("scalar", lambda e: e.activation(kraw[:], p_kq[:], AF.Copy))
            k.eng("scalar").wait_ge(s_mm.h, 2 * h + 2)
            C("scalar", lambda e: e.activation(qrel[:], p_q[:], AF.Relu,
                                               scale=GLA_SCALE))
            k.gla_evac_done[h] = s_ch.n
            C("scalar", lambda e: e.activation(t1[:], kraw[:], AF.Abs))
            C("scalar", lambda e: e.activation(t2[:], t1[:], AF.Exp, scale=-1.0))
            C("scalar", lambda e: e.activation(t2[:], t2[:], AF.Ln, bias=1.0))
            C("vector", lambda e: e.tensor_scalar(t1[:], kraw[:], 0.0, None,
                                                  ALU.max))
            C("vector", lambda e: e.tensor_tensor(sp[:], t1[:], t2[:], ALU.add))
            C("vector", lambda e: e.tensor_tensor(t1[:], kraw[:], sp[:],
                                                  ALU.subtract))
            C("scalar", lambda e: e.activation(t2[:], t1[:], AF.Exp))
            for ci in range(NCH):
                sl = slice(ci * CH, (ci + 1) * CH)
                C("vector", lambda e, sl=sl: e.tensor_tensor_scan(
                    ap_[:, sl], sp[:, sl], sp[:, sl], 0.0, ALU.add, ALU.bypass))
            C("scalar", lambda e: e.activation(ep[:], ap_[:], AF.Exp, scale=-1.0))
            C("scalar", lambda e: e.activation(em[:], ap_[:], AF.Exp))
            C("vector", lambda e: e.tensor_copy(
                k.dfp_sb[:, h * NCH:(h + 1) * NCH],
                ep[:].rearrange("p (c w) -> p c w", w=CH)[:, :, CH - 1]))
            C("vector", lambda e: e.reduce_sum(
                dcs_sb[:, h:h + 1],
                ap_[:].rearrange("p (c w) -> p c w", w=CH)[:, :, CH - 1],
                axis=AX))
            C("vector", lambda e: e.tensor_copy(
                facd[h % 2][:, 0:NCH],
                em[:].rearrange("p (c w) -> p c w", w=CH)[:, :, SC - 1]))
            C("vector", lambda e: e.tensor_copy(
                facd[h % 2][:, NCH:2 * NCH],
                ep[:].rearrange("p (c w) -> p c w", w=CH)[:, :, SC - 1]))
            C("vector", lambda e: e.tensor_copy(
                facd[h % 2][:, 2 * NCH:3 * NCH],
                k.dfp_sb[:, h * NCH:(h + 1) * NCH]))
            C("vector", lambda e: e.tensor_tensor(qt_t[h % 2][:], qrel[:], ep[:],
                                                  ALU.mult))
            C("vector", lambda e: e.tensor_tensor(kt_t[h % 2][:], t2[:], em[:],
                                                  ALU.mult))
            C("vector", lambda e: e.tensor_tensor(
                kh_t[h % 2][:].rearrange("p (c w) -> p c w", w=CH),
                kt_t[h % 2][:].rearrange("p (c w) -> p c w", w=CH),
                k.dfp_sb[:, h * NCH:(h + 1) * NCH].unsqueeze(2
                    ).broadcast_to([128, NCH, CH]),
                ALU.mult))
            C("vector", lambda e: e.tensor_tensor(
                kt_t[h % 2][:].rearrange("p (c w) -> p c w", w=CH)[:, :, SC:CH],
                kt_t[h % 2][:].rearrange("p (c w) -> p c w", w=CH)[:, :, SC:CH],
                facd[h % 2][:, NCH:2 * NCH].unsqueeze(2
                    ).broadcast_to([128, NCH, SC]),
                ALU.mult))
            k.gla_chain_done[h] = s_ch.n
        emit_khtp(NH - 1)
        g.wait_ge(s_st.h, s_st.n)
        dcore = k.sb("gla_dcore", [128, NH], F32)
        C2 = lambda eng, fn: chain_step(k, s_ch, eng, fn)
        C2("scalar", lambda e: e.activation(dcore[:], dcs_sb[:], AF.Exp,
                                            scale=-1.0))
        C2("scalar", lambda e: e.activation(k.junk[:, 512:1024],
                                            k.junk[:, 512:1024], AF.Copy))
        s.wait_ge(s_ch.h, s_ch.n)
        k.dma.inc(s.dma_start(out=td_out[:, D:D + NH], in_=dcore[:]))
        s.wait_ge(k.dma.h, k.dma.n)
        k.barrier()

    with k.scope():
        wib = [k.sb("gla_wib", [128, D], F32) for _ in range(2)]
        wic = [k.sb("gla_wic", [128, D], BF16) for _ in range(2)]
        p_v = [k.ps("gla_pv", [128, D], F32) for _ in range(2)]
        s_wv = k.sem("gla_s_wv")
        s_mmv = k.sem("gla_s_mmv")
        s_evv = k.sem("gla_s_evv")
        dma0 = k.dma.n
        for tp in range(4):
            for dc in range(ND):
                k.dma.inc(k.eng("sync").dma_start(out=wib[dc % 2][:],
                                                  in_=wi_d[dc]))
                if dc >= 1 or tp >= 1:
                    k.eng("sync").wait_ge(s_wv.h, tp * ND + dc)
            for dc in range(ND):
                v2 = k.eng("vector")
                v2.wait_ge(k.dma.h, dma0 + 16 * (tp * ND + dc + 1))
                if tp * ND + dc >= 2:
                    v2.wait_ge(s_mmv.h, tp * ND + dc - 1)
                ins = v2.tensor_copy(wic[dc % 2][:], wib[dc % 2][:])
                s_wv.inc(ins, 1)
                te2 = k.eng("tensor")
                te2.wait_ge(s_wv.h, tp * ND + dc + 1)
                if tp >= 1 and dc == 0:
                    te2.wait_ge(s_evv.h, 2 * tp)
                for ti in range(2):
                    tt = tp * 2 + ti
                    for half in range(4):
                        ins = te2.matmul(
                            p_v[ti][:, half * 512:(half + 1) * 512],
                            xn[:, dc * T + tt * 128:dc * T + (tt + 1) * 128],
                            wic[dc % 2][:, half * 512:(half + 1) * 512],
                            start=(dc == 0), stop=(dc == ND - 1))
                s_mmv.inc(ins, 1)
            for ti in range(2):
                tt = tp * 2 + ti
                v2 = k.eng("vector")
                v2.wait_ge(s_mmv.h, (tp + 1) * ND)
                ins = v2.tensor_copy(k.vt_sb[:, tt * D:(tt + 1) * D],
                                     p_v[ti][:])
                s_evv.inc(ins, 1)
        s2 = k.eng("sync")
        s2.wait_ge(s_evv.h, s_evv.n)
        k.dma.inc(s2.dma_start(out=vt_d[:], in_=k.vt_sb[:]))
        s2.wait_ge(k.dma.h, k.dma.n)
        k.barrier()

    with k.scope():
        S_all = k.sb("gla_S", [128, D], F32)
        p_i = [k.ps("gla_pi", [128, 1024], F32) for _ in range(4)]
        s_i = k.sem("gla_s_i")
        s_u = k.sem("gla_s_u")
        gz = k.sem("gla_gz")
        gz.inc(k.eng("gpsimd").memset(S_all[:], 0.0), 1)
        k.eng("vector").wait_ge(gz.h, 1)
        for ci in range(NCH):
            te3 = k.eng("tensor")
            v3 = k.eng("vector")
            rbase = (ci % 2) * 64
            tt = ci // 2
            for hg in range(2):
                pb = p_i[(2 * ci + hg) % 4]
                if 2 * ci + hg >= 4:
                    te3.wait_ge(s_u.h, 2 * ci + hg - 3)
                for hh in range(8):
                    h = hg * 8 + hh
                    ins = te3.matmul(
                        pb[:, hh * 128:(hh + 1) * 128],
                        k.kh_sb[rbase:rbase + 64,
                                h * T + tt * 128: h * T + (tt + 1) * 128],
                        k.vt_sb[rbase:rbase + 64,
                                tt * D + h * 128: tt * D + (h + 1) * 128],
                        start=True, stop=True)
                s_i.inc(ins, 1)
                v3.wait_ge(s_i.h, 2 * ci + hg + 1)
                for hh in range(8):
                    h = hg * 8 + hh
                    ins = v3.scalar_tensor_tensor(
                        S_all[:, h * 128:(h + 1) * 128],
                        S_all[:, h * 128:(h + 1) * 128],
                        k.dfp_sb[:, h * NCH + ci: h * NCH + ci + 1],
                        pb[:, hh * 128:(hh + 1) * 128],
                        ALU.mult, ALU.add)
                s_u.inc(ins, 1)
        s4 = k.eng("sync")
        s4.wait_ge(s_u.h, s_u.n)
        k.dma.inc(s4.dma_start(out=td_out[:, 0:D], in_=S_all[:]))
        s4.wait_ge(k.dma.h, k.dma.n)
        k.barrier()


def build_launch_gla1():
    k = Kern()
    declare_common_inputs(k)
    x_in = k.dram("x_in", [T, D], F32)
    wq_f = k.dram("wq_f", [NH, 128, D], F32)
    wk_f = k.dram("wk_f", [NH, 128, D], F32)
    wi_d = k.dram("wi_d", [ND, 128, D], F32)
    qt_d = k.dram("qt_d", [NH, 128, T], BF16, kind="ExternalOutput")
    kt_d = k.dram("kt_d", [NH, 128, T], BF16, kind="ExternalOutput")
    kh_d = k.dram("kh_d", [NH, 128, T], BF16, kind="ExternalOutput")
    vt_d = k.dram("vt_d", [128, NT * D], BF16, kind="ExternalOutput")
    fac_d = k.dram("fac_d", [NH, 128, 3 * NCH], F32, kind="ExternalOutput")
    td_out = k.dram("td_out", [128, D + NH], F32, kind="ExternalOutput")
    k.start()
    emit_common(k)
    emit_gla_prep_pass1(k, wq_f, wk_f, wi_d, qt_d, kt_d, kh_d, vt_d, fac_d,
                        td_out, x_in)
    return k.finish()



def emit_gla_pass2(k, qt_d, kt_d, kh_d, vt_d, fac_d, td_all, gsel_in, gw_in,
                   wo_f, x_in, x_out, o_scr, dbg=None):
    s = k.eng("sync")
    g = k.eng("gpsimd")
    v = k.eng("vector")
    a = k.eng("scalar")
    te = k.eng("tensor")

    oT = k.sb("g2_oT", [128, ND * T], BF16)
    with k.scope():
        qt_sb = k.sb("g2_qt", [128, NH * T], BF16)
        kt_sb = k.sb("g2_kt", [128, NH * T], BF16)
        kh_sb = k.sb("g2_kh", [128, NH * T], BF16)
        vt_sb = k.sb("g2_vt", [128, NT * D], BF16)
        fac_sb = k.sb("g2_fac", [128, NH * 3 * NCH], F32)
        gsel = k.sb("g2_gsel", [128, 2 * 8], F32)
        S_all = k.sb("g2_S", [128, D], F32)
        S_bf = k.sb("g2_Sbf", [128, D], BF16)
        ostg = [k.sb("g2_ostg", [64, D], BF16) for _ in range(2)]
        for h in range(NH):
            k.dma.inc(s.dma_start(out=qt_sb[:, h * T:(h + 1) * T], in_=qt_d[h]))
            k.dma.inc(s.dma_start(out=kt_sb[:, h * T:(h + 1) * T], in_=kt_d[h]))
            k.dma.inc(s.dma_start(out=kh_sb[:, h * T:(h + 1) * T], in_=kh_d[h]))
            k.dma.inc(s.dma_start(out=fac_sb[:, h * 3 * NCH:(h + 1) * 3 * NCH],
                                  in_=fac_d[h]))
        k.dma.inc(s.dma_start(out=vt_sb[:], in_=vt_d[:]))
        k.dma.inc(s.dma_start(out=gsel[:], in_=gsel_in[:]))
        s.wait_ge(k.dma.h, k.dma.n)
        k.barrier()

        def fac(h, which, ci=None):
            base = h * 3 * NCH + which * NCH
            if ci is None:
                return fac_sb[:, base:base + NCH]
            return fac_sb[:, base + ci:base + ci + 1]

        sc = k.sem("g2_sc")
        gz = k.sem("g2_gz")
        gz.inc(g.memset(S_all[:], 0.0), 1)
        v.wait_ge(gz.h, 1)
        pfx_scope = k.scope()
        pfx_scope.__enter__()
        td_j = k.sb("g2_td", [128, D + NH], F32)
        tju = k.sb("g2_tju", [128, D], F32)
        dp_j = k.sb("g2_dp", [128, NH], F32)
        for j in range(NCORES - 1):
            d0 = k.dma.n
            k.dma.inc(s.dma_start(out=td_j[:],
                                  in_=td_all[j * 128:(j + 1) * 128, :]))
            v.wait_ge(k.dma.h, d0 + 16)
            v.tensor_scalar(dp_j[:], td_j[:, D:D + NH], gsel[:, j:j + 1],
                            gsel[:, 8 + j:8 + j + 1], ALU.mult, ALU.add)
            ins = v.tensor_scalar(tju[:], td_j[:, 0:D], gsel[:, j:j + 1], None,
                                  ALU.mult)
            sc.inc(ins, 1)
            for h in range(NH):
                ins = v.scalar_tensor_tensor(
                    S_all[:, h * 128:(h + 1) * 128],
                    S_all[:, h * 128:(h + 1) * 128],
                    dp_j[:, h:h + 1],
                    tju[:, h * 128:(h + 1) * 128], ALU.mult, ALU.add)
            sc.inc(ins, 1)
            s.wait_ge(sc.h, sc.n)
        ins = v.tensor_copy(S_bf[:], S_all[:])
        sc.inc(ins, 1)
        k.barrier()
        pfx_scope.__exit__(None, None, None)

        p_P = k.ps("g2_pP", [64, 1024], F32)
        p_o = k.ps("g2_po", [64, 2048], F32)
        p_i = k.ps("g2_pi", [128, 1024], F32)
        P_sb = k.sb("g2_Psb", [128, 1024], BF16)
        qty = k.sb("g2_qty", [128, NH * SC], BF16)
        s_P = k.sem("g2_s_P")
        s_Pe = k.sem("g2_s_Pe")
        s_om = k.sem("g2_s_om")
        s_oe = k.sem("g2_s_oe")
        s_im = k.sem("g2_s_im")
        s_su = k.sem("g2_s_su")
        s_sb2 = k.sem("g2_s_sb2")
        s_od = k.sem("g2_s_od")
        s_y = k.sem("g2_s_y")
        gz.inc(v.memset(p_P[:], 0.0), 1)
        te.wait_ge(gz.h, 2)
        for ci in range(NCH):
            rbase = (ci % 2) * 64
            tt = ci // 2
            Xs = slice(ci * CH, ci * CH + SC)
            Ys = slice(ci * CH + SC, (ci + 1) * CH)
            if ci >= 1:
                te.wait_ge(s_Pe.h, ci)
            for h in range(NH):
                if ci > 0 and h == 0:
                    v.wait_ge(s_P.h, ci)
                ins = v.tensor_scalar(qty[:, h * SC:(h + 1) * SC],
                                      qt_sb[:, h * T:(h + 1) * T][:, Ys],
                                      fac(h, 0, ci), None, ALU.mult)
                s_y.inc(ins, 1)
                te.wait_ge(s_y.h, s_y.n)
                co = h * 64
                te.matmul(p_P[0:32, co:co + 32],
                          kt_sb[:, h * T:(h + 1) * T][:, Xs],
                          qt_sb[:, h * T:(h + 1) * T][:, Xs],
                          start=True, stop=True)
                te.matmul(p_P[0:32, co + 32:co + 64],
                          kt_sb[:, h * T:(h + 1) * T][:, Xs],
                          qt_sb[:, h * T:(h + 1) * T][:, Ys],
                          start=True, stop=True)
                ins = te.matmul(p_P[32:64, co + 32:co + 64],
                                kt_sb[:, h * T:(h + 1) * T][:, Ys],
                                qty[:, h * SC:(h + 1) * SC],
                                start=True, stop=True)
            s_P.inc(ins, 1)
            v.wait_ge(s_P.h, ci + 1)
            ins = v.tensor_tensor(
                P_sb[rbase:rbase + 64, :].rearrange("p (h w) -> p h w", w=64),
                p_P[:].rearrange("p (h w) -> p h w", w=64),
                k.trimask[:].unsqueeze(1).broadcast_to([64, NH, 64]),
                ALU.mult)
            s_Pe.inc(ins, 1)
            te.wait_ge(s_Pe.h, ci + 1)
            if ci >= 1:
                te.wait_ge(s_oe.h, ci)
                te.wait_ge(s_sb2.h, ci)
            for h in range(NH):
                te.matmul(p_o[:, h * 128:(h + 1) * 128],
                          qt_sb[:, h * T + ci * CH: h * T + (ci + 1) * CH],
                          S_bf[:, h * 128:(h + 1) * 128],
                          start=True, stop=False)
                ins = te.matmul(p_o[:, h * 128:(h + 1) * 128],
                                P_sb[rbase:rbase + 64, h * 64:(h + 1) * 64],
                                vt_sb[rbase:rbase + 64,
                                      tt * D + h * 128: tt * D + (h + 1) * 128],
                                start=False, stop=True)
            s_om.inc(ins, 1)
            v.wait_ge(s_om.h, ci + 1)
            if ci >= 2:
                v.wait_ge(s_od.h, 16 * (ci - 1))
            ins = v.tensor_copy(ostg[ci % 2][:], p_o[:])
            s_oe.inc(ins, 1)
            g.wait_ge(s_oe.h, ci + 1)
            s_od.inc(g.dma_start(out=o_scr[ci], in_=ostg[ci % 2][:]))
            for hg in range(2):
                if 2 * ci + hg >= 1:
                    te.wait_ge(s_su.h, 2 * ci + hg)
                for hh in range(8):
                    h = hg * 8 + hh
                    ins = te.matmul(
                        p_i[:, hh * 128:(hh + 1) * 128],
                        kh_sb[rbase:rbase + 64,
                              h * T + tt * 128: h * T + (tt + 1) * 128],
                        vt_sb[rbase:rbase + 64,
                              tt * D + h * 128: tt * D + (h + 1) * 128],
                        start=True, stop=True)
                s_im.inc(ins, 1)
                v.wait_ge(s_im.h, 2 * ci + hg + 1)
                if hg == 0:
                    v.wait_ge(s_om.h, ci + 1)
                for hh in range(8):
                    h = hg * 8 + hh
                    ins = v.scalar_tensor_tensor(
                        S_all[:, h * 128:(h + 1) * 128],
                        S_all[:, h * 128:(h + 1) * 128],
                        fac(h, 2, ci),
                        p_i[:, hh * 128:(hh + 1) * 128],
                        ALU.mult, ALU.add)
                s_su.inc(ins, 1)
            ins = v.tensor_copy(S_bf[:], S_all[:])
            s_sb2.inc(ins, 1)
        g.wait_ge(s_od.h, s_od.n)
        k.barrier()

    with k.scope():
        otile = [k.sb("g2_otile", [128, D], BF16) for _ in range(2)]
        tp = [k.ps("g2_tp", [128, 512], BF16) for _ in range(2)]
        s_tp = k.sem("g2_s_tp")
        s_ev = k.sem("g2_s_ev")
        odma0 = k.dma.n
        for tt in range(NT):
            if tt >= 2:
                s.wait_ge(s_ev.h, 4 * (tt - 1))
            k.dma.inc(s.dma_start(out=otile[tt % 2][0:64, :], in_=o_scr[2 * tt]))
            k.dma.inc(s.dma_start(out=otile[tt % 2][64:128, :],
                                  in_=o_scr[2 * tt + 1]))
        grp = 0
        for tt in range(NT):
            te.wait_ge(k.dma.h, odma0 + 32 * (tt + 1))
            for dg in range(4):
                pb = tp[grp % 2]
                if grp >= 2:
                    te.wait_ge(s_ev.h, grp - 1)
                for j in range(4):
                    dc = dg * 4 + j
                    ins = te.transpose(
                        pb[:, j * 128:(j + 1) * 128],
                        otile[tt % 2][:, dc * 128:(dc + 1) * 128],
                        k.identB[:])
                s_tp.inc(ins, 1)
                v.wait_ge(s_tp.h, grp + 1)
                for j in range(4):
                    dc = dg * 4 + j
                    ins = v.tensor_copy(
                        oT[:, dc * T + tt * 128: dc * T + (tt + 1) * 128],
                        pb[:, j * 128:(j + 1) * 128])
                s_ev.inc(ins, 1)
                grp += 1
        k.barrier()

    st_scope = k.scope()
    st_scope.__enter__()
    if True:
        sqc = [k.sb("g2_sqc", [128, T], BF16) for _ in range(2)]
        p_s2 = k.ps("g2_ps2", [1, T], F32)
        s_sq = k.sem("g2_s_sq")
        s_mm = k.sem("g2_s_mm")
        for dc in range(ND):
            if dc >= 2:
                a.wait_ge(s_mm.h, dc - 1)
            ins = a.activation(sqc[dc % 2][:], oT[:, dc * T:(dc + 1) * T],
                               AF.Square)
            s_sq.inc(ins, 1)
            te.wait_ge(s_sq.h, dc + 1)
            for half in range(2):
                hs = slice(half * 512, (half + 1) * 512)
                ins = te.matmul(p_s2[:, hs], k.ones_col[:], sqc[dc % 2][:, hs],
                                start=(dc == 0), stop=(dc == ND - 1))
            s_mm.inc(ins, 1)
        ms_row = k.sb("g2_ms", [1, T], F32)
        r_row = k.sb("g2_r", [1, T], F32)
        s_st = k.sem("g2_s_st")
        v.wait_ge(s_mm.h, s_mm.n)
        ins = v.tensor_scalar_mul(ms_row[:], p_s2[:], 1.0 / D)
        s_st.inc(ins, 1)
        a.wait_ge(s_st.h, s_st.n)
        a.activation(r_row[:], ms_row[:], AF.Ln, bias=LN_EPS)
        ins = a.activation(r_row[:], r_row[:], AF.Exp, scale=-0.5)
        s_st.inc(ins, 1)
        p_rb = k.ps("g2_prb", [128, 512], F32)
        rb = k.sb("g2_rb", [128, T], F32)
        s_bc = k.sem("g2_s_bc")
        te.wait_ge(s_st.h, s_st.n)
        for half in range(2):
            hs = slice(half * 512, (half + 1) * 512)
            ins = te.matmul(p_rb[:], k.ones_rowF[:], r_row[:, hs],
                            start=True, stop=True)
            s_bc.inc(ins, 1)
            v.wait_ge(s_bc.h, s_bc.n)
            ins = v.tensor_copy(rb[:, hs], p_rb[:])
            s_bc.inc(ins, 1)
            te.wait_ge(s_bc.h, s_bc.n)
        gw = k.sb("g2_gw", [128, ND], F32)
        k.dma.inc(s.dma_start(out=gw[:], in_=gw_in[:]))
        s.wait_ge(k.dma.h, k.dma.n)
        nrm = k.sb("g2_nrm", [128, T], F32)
        s_nm = k.sem("g2_s_nm")
        k.barrier()
        for dc in range(ND):
            v.wait_ge(s_nm.h, s_nm.n)
            ins = v.tensor_tensor(nrm[:], oT[:, dc * T:(dc + 1) * T], rb[:],
                                  ALU.mult)
            s_nm.inc(ins, 1)
            a.wait_ge(s_nm.h, s_nm.n)
            ins = a.activation(oT[:, dc * T:(dc + 1) * T], nrm[:], AF.Copy,
                               scale=gw[:, dc:dc + 1])
            s_nm.inc(ins, 1)
        k.barrier()
        if dbg is not None:
            k.dma.inc(s.dma_start(out=dbg[:], in_=oT[:]))
            s.wait_ge(k.dma.h, k.dma.n)
            k.barrier()

        st_scope.__exit__(None, None, None)
        k.xT = k.sb("xT_g2", [128, ND * T], F32)
        emit_in_transpose(k, x_in)
        wo_scope = k.scope()
        wo_scope.__enter__()
        wob = [k.sb("g2_wob", [128, D], F32) for _ in range(2)]
        woc = [k.sb("g2_woc", [128, D], BF16) for _ in range(2)]
        o_f32 = [k.sb("g2_of32", [128, 512], F32) for _ in range(2)]
        p_w = [k.ps("g2_pw", [128, 512], F32) for _ in range(4)]
        s_wc = k.sem("g2_s_wc")
        s_wm = k.sem("g2_s_wm")
        s_ev2 = k.sem("g2_s_ev2")
        s_add = k.sem("g2_s_add")
        dma0 = k.dma.n
        for fb in range(ND):
            k.dma.inc(s.dma_start(out=wob[fb % 2][:], in_=wo_f[fb]))
            if fb >= 1:
                s.wait_ge(s_wc.h, fb)
        for fb in range(ND):
            v.wait_ge(k.dma.h, dma0 + 16 * (fb + 1))
            if fb >= 2:
                v.wait_ge(s_wm.h, 2 * (fb - 1))
            ins = v.tensor_copy(woc[fb % 2][:], wob[fb % 2][:])
            s_wc.inc(ins, 1)
            te.wait_ge(s_wc.h, fb + 1)
            for half in range(2):
                idx = 2 * fb + half
                pb = p_w[idx % 4]
                if idx >= 4:
                    te.wait_ge(s_add.h, idx - 3)
                for dc in range(ND):
                    ins = te.matmul(
                        pb[:], woc[fb % 2][:, dc * 128:(dc + 1) * 128],
                        oT[:, dc * T + half * 512: dc * T + (half + 1) * 512],
                        start=(dc == 0), stop=(dc == ND - 1))
                s_wm.inc(ins, 1)
                a.wait_ge(s_wm.h, idx + 1)
                if idx >= 2:
                    a.wait_ge(s_add.h, idx - 1)
                ins = a.activation(o_f32[idx % 2][:, 0:512], pb[:], AF.Copy)
                s_ev2.inc(ins, 1)
                v.wait_ge(s_ev2.h, s_ev2.n)
                xsl = slice(fb * T + half * 512, fb * T + (half + 1) * 512)
                ins = v.tensor_tensor(k.xT[:, xsl], k.xT[:, xsl],
                                      o_f32[idx % 2][:, 0:512], ALU.add)
                s_add.inc(ins, 1)
        k.barrier()
        wo_scope.__exit__(None, None, None)
        if x_out is not None:
            emit_out_transpose(k, x_out)


def build_launch_gla2():
    k = Kern()
    declare_common_inputs(k)
    x_in = k.dram("x_in", [T, D], F32)
    qt_d = k.dram("qt_d", [NH, 128, T], BF16)
    kt_d = k.dram("kt_d", [NH, 128, T], BF16)
    kh_d = k.dram("kh_d", [NH, 128, T], BF16)
    vt_d = k.dram("vt_d", [128, NT * D], BF16)
    fac_d = k.dram("fac_d", [NH, 128, 3 * NCH], F32)
    td_all = k.dram("td_all", [NCORES * 128, D + NH], F32)
    gsel_in = k.dram("gsel_in", [128, 16], F32)
    gw_in = k.dram("gw_in", [128, ND], F32)
    wo_f = k.dram("wo_f", [ND, 128, D], F32)
    trimask_in = k.dram("trimask_in", [64, 64], BF16)
    x_out = k.dram("x_out", [T, D], F32, kind="ExternalOutput")
    o_scr = k.dram("o_scr", [NCH, 64, D], BF16, kind="Internal")
    dbg = None
    k.start()
    emit_common(k)
    k.trimask = k.sb("trimask", [64, 64], BF16)
    s = k.eng("sync")
    k.dma.inc(s.dma_start(out=k.trimask[:], in_=trimask_in[:]))
    s.wait_ge(k.dma.h, k.dma.n)
    k.barrier()
    emit_gla_pass2(k, qt_d, kt_d, kh_d, vt_d, fac_d, td_all, gsel_in, gw_in,
                   wo_f, x_in, x_out, o_scr, dbg=dbg)
    return k.finish()




def emit_l1ffn(k, wbase_f, rw_in, a_f, bw_f, bb_in, e16_in):
    s = k.eng("sync")
    v = k.eng("vector")
    a = k.eng("scalar")
    te = k.eng("tensor")
    with k.scope():
        xn = k.sb("l1f_xn", [128, ND * T], BF16)
        emit_ln(k, xn)
        rw = k.sb("l1f_rw", [128, ND * 8], BF16)
        rw_f32 = k.sb("l1f_rwf", [128, ND * 8], F32)
        aw = k.sb("l1f_aw", [128, D], BF16)
        aw_f32 = k.sb("l1f_awf", [128, D], F32)
        e16 = k.sb("l1f_e16", [8, 128], BF16)
        bb = k.sb("l1f_bb", [128, ND], F32)
        bbh = k.sb("l1f_bbh", [128, ND], F32)
        d0 = k.dma.n
        k.dma.inc(s.dma_start(out=rw_f32[:], in_=rw_in[:]))
        k.dma.inc(s.dma_start(out=aw_f32[:], in_=a_f[:]))
        k.dma.inc(s.dma_start(out=e16[:], in_=e16_in[:]))
        k.dma.inc(s.dma_start(out=bb[:], in_=bb_in[:]))
        sprep = k.sem("l1f_sprep")
        v.wait_ge(k.dma.h, k.dma.n)
        v.tensor_copy(rw[:], rw_f32[:])
        v.tensor_copy(aw[:], aw_f32[:])
        ins = v.tensor_scalar_mul(bbh[:], bb[:], 0.5)
        sprep.inc(ins, 1)

        hm = k.sb("l1f_hm", [128, 1024], BF16)
        rt_scope = k.scope()
        rt_scope.__enter__()
        p_r = k.ps("l1f_pr", [128, 64], F32)
        logit = k.sb("l1f_lg", [128, 64], F32)
        rmax = k.sb("l1f_rmax", [128, 1], F32)
        mask = k.sb("l1f_mask", [128, 64], BF16)
        p_mt = k.ps("l1f_pmt", [8, 1024], BF16)
        maskT = k.sb("l1f_maskT", [8, 1024], BF16)
        p_me = k.ps("l1f_pme", [128, 1024], F32)
        mexp = k.sb("l1f_mexp", [128, 1024], BF16)
        s_r = k.sem("l1f_s_r")
        te.wait_ge(sprep.h, 1)
        for tt in range(NT):
            for dc in range(ND):
                ins = te.matmul(p_r[:, tt * 8:(tt + 1) * 8],
                                xn[:, dc * T + tt * 128: dc * T + (tt + 1) * 128],
                                rw[:, dc * 8:(dc + 1) * 8],
                                start=(dc == 0), stop=(dc == ND - 1))
        s_r.inc(ins, 1)
        v.wait_ge(s_r.h, 1)
        ins = v.tensor_copy(logit[:], p_r[:])
        s_r.inc(ins, 1)
        v.wait_ge(s_r.h, 2)
        for tt in range(NT):
            v.reduce_max(rmax[:], logit[:, tt * 8:(tt + 1) * 8], axis=AX)
            v.tensor_copy(k.junk[:, 0:512], k.junk[:, 0:512])
            ins = v.tensor_scalar(mask[:, tt * 8:(tt + 1) * 8],
                                  logit[:, tt * 8:(tt + 1) * 8],
                                  rmax[:], None, ALU.is_ge)
        s_r.inc(ins, 1)
        te.wait_ge(s_r.h, 3)
        for tt in range(NT):
            ins = te.transpose(p_mt[:, tt * 128:(tt + 1) * 128],
                               mask[:, tt * 8:(tt + 1) * 8], k.identB[:])
        s_r.inc(ins, 1)
        v.wait_ge(s_r.h, 4)
        ins = v.tensor_copy(maskT[:], p_mt[:])
        s_r.inc(ins, 1)
        te.wait_ge(s_r.h, 5)
        for half in range(2):
            hs = slice(half * 512, (half + 1) * 512)
            ins = te.matmul(p_me[:, hs], e16[:], maskT[:, hs],
                            start=True, stop=True)
        s_r.inc(ins, 1)
        v.wait_ge(s_r.h, 6)
        ins = v.tensor_copy(mexp[:], p_me[:])
        s_r.inc(ins, 1)

        p_h = k.ps("l1f_ph", [128, 1024], F32)
        te.wait_ge(s_r.h, 7)
        for half in range(2):
            hs = slice(half * 512, (half + 1) * 512)
            for dc in range(ND):
                ins = te.matmul(p_h[:, hs], aw[:, dc * 128:(dc + 1) * 128],
                                xn[:, dc * T + half * 512: dc * T + (half + 1) * 512],
                                start=(dc == 0), stop=(dc == ND - 1))
        s_r.inc(ins, 1)
        v.wait_ge(s_r.h, 8)
        ins = v.tensor_tensor(hm[:], p_h[:], mexp[:], ALU.mult)
        s_r.inc(ins, 1)
        k.barrier()
        rt_scope.__exit__(None, None, None)

        wbb = [k.sb("l1f_wbb", [128, D], F32) for _ in range(2)]
        wbc = [k.sb("l1f_wbc", [128, D], BF16) for _ in range(2)]
        bwb = k.sb("l1f_bwb", [128, ND * 128], F32)
        bwc = k.sb("l1f_bwc", [128, ND * 128], BF16)
        o_f32 = [k.sb("l1f_of32", [128, 512], F32) for _ in range(2)]
        p_w = [k.ps("l1f_pw", [128, 512], F32) for _ in range(4)]
        s_wc = k.sem("l1f_s_wc")
        s_wm = k.sem("l1f_s_wm")
        s_ev = k.sem("l1f_s_ev")
        s_add = k.sem("l1f_s_add")
        d1 = k.dma.n
        for fb in range(ND):
            k.dma.inc(s.dma_start(out=bwb[:, fb * 128:(fb + 1) * 128],
                                  in_=bw_f[fb]))
        v.wait_ge(k.dma.h, k.dma.n)
        ins = v.tensor_copy(bwc[:], bwb[:])
        s_wc.inc(ins, 1)
        dma0 = k.dma.n
        for fb in range(ND):
            k.dma.inc(s.dma_start(out=wbb[fb % 2][:], in_=wbase_f[fb]))
            if fb >= 1:
                s.wait_ge(s_wc.h, fb + 1)
        for fb in range(ND):
            v.wait_ge(k.dma.h, dma0 + 16 * (fb + 1))
            if fb >= 2:
                v.wait_ge(s_wm.h, 2 * (fb - 1))
            ins = v.tensor_copy(wbc[fb % 2][:], wbb[fb % 2][:])
            s_wc.inc(ins, 1)
            te.wait_ge(s_wc.h, fb + 2)
            te.wait_ge(s_r.h, 9)
            for half in range(2):
                idx = 2 * fb + half
                pb = p_w[idx % 4]
                if idx >= 4:
                    te.wait_ge(s_add.h, idx - 3)
                for dc in range(ND):
                    te.matmul(
                        pb[:], wbc[fb % 2][:, dc * 128:(dc + 1) * 128],
                        xn[:, dc * T + half * 512: dc * T + (half + 1) * 512],
                        start=(dc == 0), stop=False)
                ins = te.matmul(
                    pb[:], bwc[:, fb * 128:(fb + 1) * 128],
                    hm[:, half * 512:(half + 1) * 512],
                    start=False, stop=True)
                s_wm.inc(ins, 1)
                a.wait_ge(s_wm.h, idx + 1)
                if idx >= 2:
                    a.wait_ge(s_add.h, idx - 1)
                ins = a.activation(o_f32[idx % 2][:], pb[:], AF.Identity,
                                   bias=bbh[:, fb:fb + 1], scale=0.5)
                s_ev.inc(ins, 1)
                v.wait_ge(s_ev.h, s_ev.n)
                xsl = slice(fb * T + half * 512, fb * T + (half + 1) * 512)
                ins = v.tensor_tensor(k.xT[:, xsl], k.xT[:, xsl],
                                      o_f32[idx % 2][:], ALU.add)
                s_add.inc(ins, 1)
        k.barrier()



NKV = 4
SWA_W = 1024


def emit_swa_proj(k, wq_f, wk_f, wv_d, qt_out, kv_out):
    s = k.eng("sync")
    g = k.eng("gpsimd")
    v = k.eng("vector")
    a = k.eng("scalar")
    te = k.eng("tensor")
    with k.scope():
        xn = k.sb("swp_xn", [128, ND * T], BF16)
        emit_ln(k, xn)
        wb = [k.sb("swp_wb", [128, D], F32) for _ in range(2)]
        wc = [k.sb("swp_wc", [128, D], BF16) for _ in range(2)]
        ot = k.sb("swp_ot", [128, T], BF16)
        p_o = k.ps("swp_po", [128, 1024], F32)
        s_wc = k.sem("swp_s_wc")
        s_mm = k.sem("swp_s_mm")
        s_ev = k.sem("swp_s_ev")
        s_st = k.sem("swp_s_st")
        blocks = [("q", h) for h in range(NH)] + [("k", c) for c in range(NKV)]
        dma0 = k.dma.n
        for bi, (kind, h) in enumerate(blocks):
            src_t = wq_f if kind == "q" else wk_f
            k.dma.inc(s.dma_start(out=wb[bi % 2][:], in_=src_t[h]))
            if bi >= 1:
                s.wait_ge(s_wc.h, bi)
        for bi, (kind, h) in enumerate(blocks):
            v.wait_ge(k.dma.h, dma0 + 16 * (bi + 1))
            if bi >= 2:
                v.wait_ge(s_mm.h, bi - 1)
            ins = v.tensor_copy(wc[bi % 2][:], wb[bi % 2][:])
            s_wc.inc(ins, 1)
            te.wait_ge(s_wc.h, bi + 1)
            if bi >= 1:
                te.wait_ge(s_ev.h, bi)
            for half in range(2):
                for dc in range(ND):
                    ins = te.matmul(
                        p_o[:, half * 512:(half + 1) * 512],
                        wc[bi % 2][:, dc * 128:(dc + 1) * 128],
                        xn[:, dc * T + half * 512: dc * T + (half + 1) * 512],
                        start=(dc == 0), stop=(dc == ND - 1))
            s_mm.inc(ins, 1)
            v.wait_ge(s_mm.h, bi + 1)
            if bi >= 1:
                v.wait_ge(s_st.h, 16 * bi)
            ins = v.tensor_copy(ot[:], p_o[:])
            s_ev.inc(ins, 1)
            g.wait_ge(s_ev.h, bi + 1)
            if kind == "q":
                ins = g.dma_start(out=qt_out[h], in_=ot[:])
            else:
                ins = g.dma_start(out=kv_out[:, h * T:(h + 1) * T], in_=ot[:])
            s_st.inc(ins)
        wvb = [k.sb("swp_wvb", [128, NKV * 128], F32) for _ in range(2)]
        wvc = k.sb("swp_wvc", [128, ND * NKV * 128], BF16)
        p_v = [k.ps("swp_pv", [128, 512], F32) for _ in range(2)]
        vstg = [k.sb("swp_vstg", [128, 512], BF16) for _ in range(2)]
        s_vw = k.sem("swp_s_vw")
        s_vm = k.sem("swp_s_vm")
        s_vev = k.sem("swp_s_vev")
        s_vst = k.sem("swp_s_vst")
        dma1 = k.dma.n
        for dc in range(ND):
            k.dma.inc(s.dma_start(out=wvb[dc % 2][:], in_=wv_d[dc]))
            if dc >= 1:
                s.wait_ge(s_vw.h, dc)
        for dc in range(ND):
            v.wait_ge(k.dma.h, dma1 + 16 * (dc + 1))
            ins = v.tensor_copy(wvc[:, dc * 512:(dc + 1) * 512], wvb[dc % 2][:])
            s_vw.inc(ins, 1)
        te.wait_ge(s_vw.h, ND)
        for tt in range(NT):
            pb = p_v[tt % 2]
            if tt >= 2:
                te.wait_ge(s_vev.h, tt - 1)
            for dc in range(ND):
                ins = te.matmul(pb[:],
                                xn[:, dc * T + tt * 128: dc * T + (tt + 1) * 128],
                                wvc[:, dc * 512:(dc + 1) * 512],
                                start=(dc == 0), stop=(dc == ND - 1))
            s_vm.inc(ins, 1)
            v.wait_ge(s_vm.h, tt + 1)
            if tt >= 2:
                v.wait_ge(s_vst.h, 16 * (tt - 1))
            ins = v.tensor_copy(vstg[tt % 2][:], pb[:])
            s_vev.inc(ins, 1)
            g.wait_ge(s_vev.h, tt + 1)
            ins = g.dma_start(out=kv_out[:, NKV * T + tt * 512:
                                        NKV * T + (tt + 1) * 512],
                              in_=vstg[tt % 2][:])
            s_vst.inc(ins)
        g.wait_ge(s_vst.h, s_vst.n)
        k.barrier()


def emit_swa_core(k, qt_d, kctx_in, vctx_in, ksink_in, vsink_in, dmask_in,
                  fmask_in, smask_in, nsp_in, wo_f, x_in, x_out):
    s = k.eng("sync")
    g = k.eng("gpsimd")
    v = k.eng("vector")
    a = k.eng("scalar")
    te = k.eng("tensor")
    NCT = 16
    oatt = k.sb("swa_o", [128, NH * T], BF16)
    with k.scope():
        qt_sb = k.sb("swa_qt", [128, NH * T], BF16)
        kctx = k.sb("swa_kctx", [128, NKV * 2 * T], BF16)
        vctx = k.sb("swa_vctx", [128, 2 * NT * 512], BF16)
        ksink = k.sb("swa_ksink", [128, NKV * 4], BF16)
        vsink = k.sb("swa_vsink", [4, 512], BF16)
        dmask = k.sb("swa_dmask", [128, 128], BF16)
        fmask = k.sb("swa_fmask", [128, 128], BF16)
        smask = k.sb("swa_smask", [4, T], BF16)
        nsp = k.sb("swa_nsp", [1, T], F32)
        ehat = k.sb("swa_ehat", [1, 128], BF16)
        ehatf = k.sb("swa_ehatf", [1, 128], F32)
        zrow = k.sb("swa_z", [1, T], F32)
        rzb = k.sb("swa_rzb", [128, T], F32)
        P_sb = [k.sb("swa_P", [128, T], BF16) for _ in range(2)]
        Ps_sb = k.sb("swa_Ps", [4, T], BF16)
        for h in range(NH):
            k.dma.inc(s.dma_start(out=qt_sb[:, h * T:(h + 1) * T], in_=qt_d[h]))
        k.dma.inc(s.dma_start(out=kctx[:], in_=kctx_in[:]))
        k.dma.inc(s.dma_start(out=vctx[:], in_=vctx_in[:]))
        k.dma.inc(s.dma_start(out=ksink[:], in_=ksink_in[:]))
        k.dma.inc(s.dma_start(out=vsink[:], in_=vsink_in[:]))
        k.dma.inc(s.dma_start(out=dmask[:], in_=dmask_in[:]))
        k.dma.inc(s.dma_start(out=fmask[:], in_=fmask_in[:]))
        k.dma.inc(s.dma_start(out=smask[:], in_=smask_in[:]))
        k.dma.inc(s.dma_start(out=nsp[:], in_=nsp_in[:]))
        s.wait_ge(k.dma.h, k.dma.n)
        k.barrier()
        sc0 = k.sem("swa_sc0")
        gz = k.sem("swa_gz")
        gz.inc(v.memset(ehat[:], 0.0), 1)
        a.wait_ge(gz.h, 1)
        a.activation(ehat[:], ehat[:], AF.Exp, bias=-SWA_SHIFT,
                     scale=SWA_SCALE)
        v.wait_ge(gz.h, 1)
        gz.inc(v.memset(k.junk[:, 0:512], 0.0), 1)
        ins = a.activation(k.junk[0:1, 512:640], k.junk[0:1, 640:768], AF.Copy)
        sc0.inc(ins, 1)
        v.wait_ge(sc0.h, 1)
        ins = v.tensor_copy(ehatf[:], ehat[:])
        sc0.inc(ins, 1)

        p_s = [k.ps("swa_ps", [128, 1024], F32) for _ in range(2)]
        p_o = k.ps("swa_po", [128, 1024], F32)
        p_z = k.ps("swa_pz", [1, 1024], F32)
        p_rb = p_s[0][:, 0:512]
        s_sm = k.sem("swa_s_sm")
        s_pe = k.sem("swa_s_pe")
        s_mk = k.sem("swa_s_mk")
        s_pv = k.sem("swa_s_pv")
        s_on = k.sem("swa_s_on")
        s_zz = k.sem("swa_s_zz")
        for qh in range(NH):
            kvh = qh // 4
            gs = qh * (NCT + 1)
            if qh >= 1:
                te.wait_ge(s_pe.h, gs)
                te.wait_ge(s_on.h, qh)
                te.wait_ge(s_zz.h, 5 * qh)
            for off in range(0, T, 512):
                ins = te.matmul(p_s[0][0:4, off:off + 512],
                                ksink[:, kvh * 4:(kvh + 1) * 4],
                                qt_sb[:, qh * T + off: qh * T + off + 512],
                                start=True, stop=True)
            s_sm.inc(ins, 1)
            a.wait_ge(s_sm.h, gs + 1)
            if qh >= 1:
                a.wait_ge(s_pv.h, gs)
            ins = a.activation(Ps_sb[:], p_s[0][0:4, :], AF.Exp,
                               bias=-SWA_SHIFT, scale=SWA_SCALE)
            s_pe.inc(ins, 1)
            v.wait_ge(s_pe.h, gs + 1)
            ins = v.tensor_tensor(Ps_sb[:], Ps_sb[:], smask[:], ALU.mult)
            s_mk.inc(ins, 1)
            te.wait_ge(s_mk.h, s_mk.n)
            for off in range(0, T, 512):
                te.matmul(p_o[:, off:off + 512],
                          vsink[:, kvh * 128:(kvh + 1) * 128],
                          Ps_sb[:, off:off + 512], start=True, stop=False)
                ins = te.matmul(p_z[:, off:off + 512], k.ones_col[0:4, :],
                                Ps_sb[:, off:off + 512], start=True, stop=False)
            s_pv.inc(ins, 1)
            for ktl in range(NCT):
                qt0 = max(0, ktl - 8)
                qt1 = min(7, ktl)
                qlo, qhi = qt0 * 128, qt1 * 128 + 128
                n = qhi - qlo
                gi = qh * (NCT + 1) + 1 + ktl
                par = ktl % 2
                ps_b = p_s[par]
                pb_b = P_sb[par]
                if ktl == 0:
                    te.wait_ge(s_pe.h, gs + 1)
                    te.wait_ge(s_zz.h, 5 * qh)
                elif ktl == 1:
                    te.wait_ge(s_pe.h, gs)
                else:
                    te.wait_ge(s_pe.h, gi - 1)
                for seg in range(qlo // 512, (qhi + 511) // 512):
                    lo = max(qlo, seg * 512)
                    hi = min(qhi, (seg + 1) * 512)
                    ins = te.matmul(
                        ps_b[:, lo:hi],
                        kctx[:, kvh * 2 * T + ktl * 128:
                             kvh * 2 * T + (ktl + 1) * 128],
                        qt_sb[:, qh * T + lo: qh * T + hi],
                        start=True, stop=True)
                s_sm.inc(ins, 1)
                a.wait_ge(s_sm.h, gi + 1)
                a.wait_ge(s_pv.h, max(gi - 1, qh * (NCT + 1)))
                ins = a.activation(pb_b[:, qlo:qhi], ps_b[:, qlo:qhi], AF.Exp,
                                   bias=-SWA_SHIFT, scale=SWA_SCALE)
                s_pe.inc(ins, 1)
                if ktl >= 8:
                    v.wait_ge(s_pe.h, gi + 1)
                    ins = v.tensor_tensor(
                        pb_b[:, (ktl - 8) * 128:(ktl - 7) * 128],
                        pb_b[:, (ktl - 8) * 128:(ktl - 7) * 128],
                        dmask[:], ALU.mult)
                    s_mk.inc(ins, 1)
                if ktl <= 7:
                    v.wait_ge(s_pe.h, gi + 1)
                    ins = v.tensor_tensor(
                        pb_b[:, ktl * 128:(ktl + 1) * 128],
                        pb_b[:, ktl * 128:(ktl + 1) * 128],
                        fmask[:], ALU.mult)
                    s_mk.inc(ins, 1)
                te.wait_ge(s_pe.h, gi + 1)
                te.wait_ge(s_mk.h, s_mk.n)
                last = (ktl == NCT - 1)
                for seg in range(qlo // 512, (qhi + 511) // 512):
                    lo = max(qlo, seg * 512)
                    hi = min(qhi, (seg + 1) * 512)
                    te.matmul(p_o[:, lo:hi],
                              vctx[:, ktl * 512 + kvh * 128:
                                   ktl * 512 + (kvh + 1) * 128],
                              pb_b[:, lo:hi], start=False, stop=last)
                    ins = te.matmul(p_z[:, lo:hi], k.ones_col[:],
                                    pb_b[:, lo:hi], start=False, stop=last)
                s_pv.inc(ins, 1)
            v.wait_ge(s_pv.h, s_pv.n)
            v.wait_ge(sc0.h, 2)
            v.tensor_scalar(zrow[:], nsp[:], ehatf[0:1, 0:1], None, ALU.mult)
            v.tensor_copy(k.junk[:, 0:512], k.junk[:, 0:512])
            v.tensor_tensor(zrow[:], p_z[:], zrow[:], ALU.subtract)
            v.tensor_copy(k.junk[:, 0:512], k.junk[:, 0:512])
            ins = v.reciprocal(zrow[:], zrow[:])
            s_zz.inc(ins, 1)
            te.wait_ge(s_zz.h, s_zz.n)
            for half in range(2):
                hs = slice(half * 512, (half + 1) * 512)
                ins = te.matmul(p_rb[:], k.ones_rowF[:], zrow[:, hs],
                                start=True, stop=True)
                s_zz.inc(ins, 1)
                v.wait_ge(s_zz.h, s_zz.n)
                ins = v.tensor_copy(rzb[:, hs], p_rb[:])
                s_zz.inc(ins, 1)
                te.wait_ge(s_zz.h, s_zz.n)
            ins = v.tensor_tensor(oatt[:, qh * T:(qh + 1) * T], p_o[:],
                                  rzb[:], ALU.mult)
            s_on.inc(ins, 1)
            if getattr(k, "swa_zdbg", None) is not None:
                s.wait_ge(s_on.h, qh + 1)
                k.dma.inc(s.dma_start(out=k.swa_zdbg[qh], in_=zrow[:]))
            te.wait_ge(s_on.h, qh + 1)
        k.barrier()
        if getattr(k, "swa_dbg", None) is not None:
            s.wait_ge(s_on.h, s_on.n)
            k.dma.inc(s.dma_start(out=k.swa_dbg[:], in_=oatt[:]))
            s.wait_ge(k.dma.h, k.dma.n)
            k.barrier()

    if True:
        k.xT = k.sb("xT_swa", [128, ND * T], F32)
        emit_in_transpose(k, x_in)
        wo_scope = k.scope()
        wo_scope.__enter__()
        wob = [k.sb("swa_wob", [128, D], F32) for _ in range(2)]
        woc = [k.sb("swa_woc", [128, D], BF16) for _ in range(2)]
        o_f32 = [k.sb("swa_of32", [128, 512], F32) for _ in range(2)]
        p_w = [k.ps("swa_pw", [128, 512], F32) for _ in range(4)]
        s_wc = k.sem("swa_s_wc")
        s_wm = k.sem("swa_s_wm")
        s_ev2 = k.sem("swa_s_ev2")
        s_add = k.sem("swa_s_add")
        dma0 = k.dma.n
        for fb in range(ND):
            k.dma.inc(s.dma_start(out=wob[fb % 2][:], in_=wo_f[fb]))
            if fb >= 1:
                s.wait_ge(s_wc.h, fb)
        for fb in range(ND):
            v.wait_ge(k.dma.h, dma0 + 16 * (fb + 1))
            if fb >= 2:
                v.wait_ge(s_wm.h, 2 * (fb - 1))
            ins = v.tensor_copy(woc[fb % 2][:], wob[fb % 2][:])
            s_wc.inc(ins, 1)
            te.wait_ge(s_wc.h, fb + 1)
            for half in range(2):
                idx = 2 * fb + half
                pb = p_w[idx % 4]
                if idx >= 4:
                    te.wait_ge(s_add.h, idx - 3)
                for dc in range(ND):
                    ins = te.matmul(
                        pb[:], woc[fb % 2][:, dc * 128:(dc + 1) * 128],
                        oatt[:, dc * T + half * 512: dc * T + (half + 1) * 512],
                        start=(dc == 0), stop=(dc == ND - 1))
                s_wm.inc(ins, 1)
                a.wait_ge(s_wm.h, idx + 1)
                if idx >= 2:
                    a.wait_ge(s_add.h, idx - 1)
                ins = a.activation(o_f32[idx % 2][:], pb[:], AF.Copy)
                s_ev2.inc(ins, 1)
                v.wait_ge(s_ev2.h, s_ev2.n)
                xsl = slice(fb * T + half * 512, fb * T + (half + 1) * 512)
                ins = v.tensor_tensor(k.xT[:, xsl], k.xT[:, xsl],
                                      o_f32[idx % 2][:], ALU.add)
                s_add.inc(ins, 1)
        k.barrier()
        wo_scope.__exit__(None, None, None)
        if x_out is not None:
            emit_out_transpose(k, x_out)


def build_launch_ffn0():
    k = Kern()
    declare_common_inputs(k)
    x_in = k.dram("x_in", [T, D], F32)
    wfct = k.dram("wfct", [NHF, 128, D], F32)
    wprojt = k.dram("wprojt", [ND, 128, HF], F32)
    x_out = k.dram("x_out", [T, D], F32, kind="ExternalOutput")
    am_out = k.dram("am_out", [1, 2], F32, kind="ExternalOutput")
    qfc_out = k.dram("qfc_out", [NHF, 128, D], BF16, kind="ExternalOutput")
    qproj_out = k.dram("qproj_out", [ND, 128, HF], BF16, kind="ExternalOutput")
    k.start()
    emit_common(k)
    k.xT = k.sb("xT", [128, ND * T], F32)
    hdram = k.dram("hdram", [NHF, 128, T], BF16, kind="Internal")
    emit_absmean(k, wfct, wprojt, am_out)
    emit_in_transpose(k, x_in)
    emit_ffn0(k, wfct, wprojt, hdram, qfc_out=qfc_out, qproj_out=qproj_out)
    emit_out_transpose(k, x_out)
    return k.finish()


def build_launch_ffn0b():
    k = Kern()
    declare_common_inputs(k)
    x_in = k.dram("x_in", [T, D], F32)
    qfc_in = k.dram("qfc_in", [NHF, 128, D], BF16)
    qproj_in = k.dram("qproj_in", [ND, 128, HF], BF16)
    am_in = k.dram("am_in", [1, 2], F32)
    x_out = k.dram("x_out", [T, D], F32, kind="ExternalOutput")
    k.start()
    emit_common(k)
    k.xT = k.sb("xT_b", [128, ND * T], F32)
    hdram = k.dram("hdram_b", [NHF, 128, T], BF16, kind="Internal")
    emit_ffn_scales(k, am_in)
    emit_in_transpose(k, x_in)
    emit_ffn0_pre(k, qfc_in, qproj_in, hdram)
    emit_out_transpose(k, x_out)
    return k.finish()


def build_launch_gla1():
    k = Kern()
    declare_common_inputs(k)
    x_in = k.dram("x_in", [T, D], F32)
    wq_f = k.dram("wq_f", [NH, 128, D], F32)
    wk_f = k.dram("wk_f", [NH, 128, D], F32)
    wi_d = k.dram("wi_d", [ND, 128, D], F32)
    qt_d = k.dram("qt_d", [NH, 128, T], BF16, kind="ExternalOutput")
    kt_d = k.dram("kt_d", [NH, 128, T], BF16, kind="ExternalOutput")
    kh_d = k.dram("kh_d", [NH, 128, T], BF16, kind="ExternalOutput")
    vt_d = k.dram("vt_d", [128, NT * D], BF16, kind="ExternalOutput")
    fac_d = k.dram("fac_d", [NH, 128, 3 * NCH], F32, kind="ExternalOutput")
    td_out = k.dram("td_out", [128, D + NH], F32, kind="ExternalOutput")
    k.start()
    emit_common(k)
    emit_gla_prep_pass1(k, wq_f, wk_f, wi_d, qt_d, kt_d, kh_d, vt_d, fac_d,
                        td_out, x_in)
    return k.finish()


def build_launch_gla2():
    k = Kern()
    declare_common_inputs(k)
    x_in = k.dram("x_in", [T, D], F32)
    qt_d = k.dram("qt_d", [NH, 128, T], BF16)
    kt_d = k.dram("kt_d", [NH, 128, T], BF16)
    kh_d = k.dram("kh_d", [NH, 128, T], BF16)
    vt_d = k.dram("vt_d", [128, NT * D], BF16)
    fac_d = k.dram("fac_d", [NH, 128, 3 * NCH], F32)
    td_all = k.dram("td_all", [NCORES * 128, D + NH], F32)
    gsel_in = k.dram("gsel_in", [128, 16], F32)
    gw_in = k.dram("gw_in", [128, ND], F32)
    wo_f = k.dram("wo_f", [ND, 128, D], F32)
    trimask_in = k.dram("trimask_in", [64, 64], BF16)
    x_out = k.dram("x_out", [T, D], F32, kind="ExternalOutput")
    o_scr = k.dram("o_scr", [NCH, 64, D], BF16, kind="Internal")
    k.start()
    emit_common(k)
    k.trimask = k.sb("trimask", [64, 64], BF16)
    s = k.eng("sync")
    k.dma.inc(s.dma_start(out=k.trimask[:], in_=trimask_in[:]))
    s.wait_ge(k.dma.h, k.dma.n)
    k.barrier()
    emit_gla_pass2(k, qt_d, kt_d, kh_d, vt_d, fac_d, td_all, gsel_in, gw_in,
                   wo_f, x_in, x_out, o_scr)
    return k.finish()


def build_launch_l1ffn():
    k = Kern()
    declare_common_inputs(k)
    x_in = k.dram("x_in", [T, D], F32)
    wbase_f = k.dram("wbase_f", [ND, 128, D], F32)
    rw_in = k.dram("rw_in", [128, ND * 8], F32)
    a_f = k.dram("a_f", [128, D], F32)
    bw_f = k.dram("bw_f", [ND, 128, 128], F32)
    bb_in = k.dram("bb_in", [128, ND], F32)
    e16_in = k.dram("e16_in", [8, 128], BF16)
    x_out = k.dram("x_out", [T, D], F32, kind="ExternalOutput")
    k.start()
    emit_common(k)
    k.xT = k.sb("xT_l1f", [128, ND * T], F32)
    emit_in_transpose(k, x_in)
    emit_l1ffn(k, wbase_f, rw_in, a_f, bw_f, bb_in, e16_in)
    emit_out_transpose(k, x_out)
    return k.finish()


def build_launch_swaproj():
    k = Kern()
    declare_common_inputs(k)
    x_in = k.dram("x_in", [T, D], F32)
    wq_f = k.dram("wq_f", [NH, 128, D], F32)
    wk_f = k.dram("wk_f", [NKV, 128, D], F32)
    wv_d = k.dram("wv_d", [ND, 128, NKV * 128], F32)
    qt_out = k.dram("qt_out", [NH, 128, T], BF16, kind="ExternalOutput")
    kv_out = k.dram("kv_out", [128, NKV * T + NT * 512], BF16,
                    kind="ExternalOutput")
    k.start()
    emit_common(k)
    k.xT = k.sb("xT_swp", [128, ND * T], F32)
    emit_in_transpose(k, x_in)
    emit_swa_proj(k, wq_f, wk_f, wv_d, qt_out, kv_out)
    return k.finish()


def build_launch_swacore():
    k = Kern()
    declare_common_inputs(k)
    x_in = k.dram("x_in", [T, D], F32)
    qt_d = k.dram("qt_d", [NH, 128, T], BF16)
    kctx_in = k.dram("kctx_in", [128, NKV * 2 * T], BF16)
    vctx_in = k.dram("vctx_in", [128, 2 * NT * 512], BF16)
    ksink_in = k.dram("ksink_in", [128, NKV * 4], BF16)
    vsink_in = k.dram("vsink_in", [4, 512], BF16)
    dmask_in = k.dram("dmask_in", [128, 128], BF16)
    fmask_in = k.dram("fmask_in", [128, 128], BF16)
    smask_in = k.dram("smask_in", [4, T], BF16)
    nsp_in = k.dram("nsp_in", [1, T], F32)
    wo_f = k.dram("wo_f", [ND, 128, D], F32)
    x_out = k.dram("x_out", [T, D], F32, kind="ExternalOutput")
    k.start()
    emit_common(k)
    k.swa_dbg = None
    k.swa_zdbg = None
    emit_swa_core(k, qt_d, kctx_in, vctx_in, ksink_in, vsink_in, dmask_in,
                  fmask_in, smask_in, nsp_in, wo_f, x_in, x_out)
    return k.finish()





def build_launch_A():
    k = Kern()
    declare_common_inputs(k)
    x_in = k.dram("x_in", [T, D], F32)
    wfct = k.dram("wfct", [NHF, 128, D], F32)
    wprojt = k.dram("wprojt", [ND, 128, HF], F32)
    wq_f = k.dram("wq_f", [NH, 128, D], F32)
    wk_f = k.dram("wk_f", [NH, 128, D], F32)
    wi_d = k.dram("wi_d", [ND, 128, D], F32)
    x_out = k.dram("x_out", [T, D], F32, kind="ExternalOutput")
    am_out = k.dram("am_out", [1, 2], F32, kind="ExternalOutput")
    qfc_out = k.dram("qfc_out", [NHF, 128, D], BF16, kind="ExternalOutput")
    qproj_out = k.dram("qproj_out", [ND, 128, HF], BF16, kind="ExternalOutput")
    qt_d = k.dram("qt_d", [NH, 128, T], BF16, kind="ExternalOutput")
    kt_d = k.dram("kt_d", [NH, 128, T], BF16, kind="ExternalOutput")
    kh_d = k.dram("kh_d", [NH, 128, T], BF16, kind="ExternalOutput")
    vt_d = k.dram("vt_d", [128, NT * D], BF16, kind="ExternalOutput")
    fac_d = k.dram("fac_d", [NH, 128, 3 * NCH], F32, kind="ExternalOutput")
    td_out = k.dram("td_out", [128, D + NH], F32, kind="ExternalOutput")
    hdram = k.dram("hdramA", [NHF, 128, T], BF16, kind="Internal")
    k.start()
    emit_common(k)
    with k.scope():
        k.xT = k.sb("xT_A", [128, ND * T], F32)
        emit_absmean(k, wfct, wprojt, am_out)
        emit_in_transpose(k, x_in)
        emit_ffn0(k, wfct, wprojt, hdram, qfc_out=qfc_out, qproj_out=qproj_out)
        emit_out_transpose(k, x_out)
    emit_gla_prep_pass1(k, wq_f, wk_f, wi_d, qt_d, kt_d, kh_d, vt_d, fac_d,
                        td_out, x_out)
    return k.finish()


def build_launch_B():
    k = Kern()
    declare_common_inputs(k)
    x_in = k.dram("x_in", [T, D], F32)
    qt_d = k.dram("qt_d", [NH, 128, T], BF16)
    kt_d = k.dram("kt_d", [NH, 128, T], BF16)
    kh_d = k.dram("kh_d", [NH, 128, T], BF16)
    vt_d = k.dram("vt_d", [128, NT * D], BF16)
    fac_d = k.dram("fac_d", [NH, 128, 3 * NCH], F32)
    td_all = k.dram("td_all", [NCORES * 128, D + NH], F32)
    gsel_in = k.dram("gsel_in", [128, 16], F32)
    gw_in = k.dram("gw_in", [128, ND], F32)
    wo_f = k.dram("wo_f", [ND, 128, D], F32)
    trimask_in = k.dram("trimask_in", [64, 64], BF16)
    am_in = k.dram("am_in", [1, 2], F32)
    qfc_in = k.dram("qfc_in", [NHF, 128, D], BF16)
    qproj_in = k.dram("qproj_in", [ND, 128, HF], BF16)
    wbase_f = k.dram("wbase_f", [ND, 128, D], F32)
    rw_in = k.dram("rw_in", [128, ND * 8], F32)
    a_f = k.dram("a_f", [128, D], F32)
    bw_f = k.dram("bw_f", [ND, 128, 128], F32)
    bb_in = k.dram("bb_in", [128, ND], F32)
    e16_in = k.dram("e16_in", [8, 128], BF16)
    swq_f = k.dram("swq_f", [NH, 128, D], F32)
    swk_f = k.dram("swk_f", [NKV, 128, D], F32)
    swv_d = k.dram("swv_d", [ND, 128, NKV * 128], F32)
    x_out = k.dram("x_out", [T, D], F32, kind="ExternalOutput")
    qt_out = k.dram("qt_out", [NH, 128, T], BF16, kind="ExternalOutput")
    kv_out = k.dram("kv_out", [128, NKV * T + NT * 512], BF16,
                    kind="ExternalOutput")
    o_scr = k.dram("o_scrB", [NCH, 64, D], BF16, kind="Internal")
    hdram = k.dram("hdramB", [NHF, 128, T], BF16, kind="Internal")
    k.start()
    emit_common(k)
    k.trimask = k.sb("trimask", [64, 64], BF16)
    s = k.eng("sync")
    k.dma.inc(s.dma_start(out=k.trimask[:], in_=trimask_in[:]))
    s.wait_ge(k.dma.h, k.dma.n)
    k.barrier()
    emit_gla_pass2(k, qt_d, kt_d, kh_d, vt_d, fac_d, td_all, gsel_in, gw_in,
                   wo_f, x_in, None, o_scr)
    emit_ffn_scales(k, am_in)
    emit_ffn0_pre(k, qfc_in, qproj_in, hdram)
    emit_l1ffn(k, wbase_f, rw_in, a_f, bw_f, bb_in, e16_in)
    emit_out_transpose(k, x_out)
    emit_swa_proj(k, swq_f, swk_f, swv_d, qt_out, kv_out)
    return k.finish()


def build_launch_C():
    k = Kern()
    declare_common_inputs(k)
    x_in = k.dram("x_in", [T, D], F32)
    qt_d = k.dram("qt_d", [NH, 128, T], BF16)
    kctx_in = k.dram("kctx_in", [128, NKV * 2 * T], BF16)
    vctx_in = k.dram("vctx_in", [128, 2 * NT * 512], BF16)
    ksink_in = k.dram("ksink_in", [128, NKV * 4], BF16)
    vsink_in = k.dram("vsink_in", [4, 512], BF16)
    dmask_in = k.dram("dmask_in", [128, 128], BF16)
    fmask_in = k.dram("fmask_in", [128, 128], BF16)
    smask_in = k.dram("smask_in", [4, T], BF16)
    nsp_in = k.dram("nsp_in", [1, T], F32)
    wo_f = k.dram("wo_f", [ND, 128, D], F32)
    wbase_f = k.dram("wbase_f", [ND, 128, D], F32)
    rw_in = k.dram("rw_in", [128, ND * 8], F32)
    a_f = k.dram("a_f", [128, D], F32)
    bw_f = k.dram("bw_f", [ND, 128, 128], F32)
    bb_in = k.dram("bb_in", [128, ND], F32)
    e16_in = k.dram("e16_in", [8, 128], BF16)
    x_out = k.dram("x_out", [T, D], F32, kind="ExternalOutput")
    k.start()
    emit_common(k)
    k.swa_dbg = None
    k.swa_zdbg = None
    emit_swa_core(k, qt_d, kctx_in, vctx_in, ksink_in, vsink_in, dmask_in,
                  fmask_in, smask_in, nsp_in, wo_f, x_in, None)
    emit_l1ffn(k, wbase_f, rw_in, a_f, bw_f, bb_in, e16_in)
    emit_out_transpose(k, x_out)
    return k.finish()


_BUILD_CACHE = {}


def _get(name, builder):
    if name not in _BUILD_CACHE:
        _BUILD_CACHE[name] = builder()
    return _BUILD_CACHE[name]


def _run(nc, in_maps):
    res = run_bass_kernel_spmd(nc, in_maps, core_ids=list(range(NCORES)))
    return res.results


def kernel(L0_attn_hidden_states, L0_attn_Wq, L0_attn_Wk, L0_attn_Wi,
           L0_attn_Wo, L0_attn_g_weight, L0_ffn_w_fc, L0_ffn_w_proj,
           L1_attn_Wq, L1_attn_Wk, L1_attn_Wv, L1_attn_Wo, L1_ffn_W_base,
           L1_ffn_b_base, L1_ffn_router_W, L1_ffn_A, L1_ffn_Bw):
    import ml_dtypes
    common = host_common_inputs()
    BD = ml_dtypes.bfloat16
    C = np.ascontiguousarray

    x = np.asarray(L0_attn_hidden_states)[0]
    xs = [C(x[c * T:(c + 1) * T]) for c in range(NCORES)]

    nc_A = _get("A", build_launch_A)
    wfct = prep_w_fmaj(C(np.asarray(L0_ffn_w_fc).T))
    wprojt = prep_w_fmaj(C(np.asarray(L0_ffn_w_proj).T))
    wq_f = prep_w_fmaj(np.asarray(L0_attn_Wq))
    wk_f = prep_w_fmaj(np.asarray(L0_attn_Wk))
    wi_d = prep_w_dmaj(np.asarray(L0_attn_Wi))
    maps = [dict(common, x_in=xs[c], wfct=wfct, wprojt=wprojt,
                 wq_f=wq_f, wk_f=wk_f, wi_d=wi_d) for c in range(NCORES)]
    rA = _run(nc_A, maps)
    xs = [rA[c]["x_out"] for c in range(NCORES)]

    nc_B = _get("B", build_launch_B)
    td_all = np.concatenate([rA[c]["td_out"] for c in range(NCORES)], axis=0)
    wo_f = prep_w_fmaj(np.asarray(L0_attn_Wo))
    gw_in = C(np.asarray(L0_attn_g_weight).reshape(ND, 128).T)
    trimask = np.triu(np.ones((64, 64))).astype(BD)
    wbase_f = prep_w_fmaj(C(np.asarray(L1_ffn_W_base).T))
    rw_in = C(np.asarray(L1_ffn_router_W).T.reshape(ND, 128, 8)
              .transpose(1, 0, 2).reshape(128, ND * 8))
    A2d = np.asarray(L1_ffn_A).reshape(128, D)
    a_f = prep_w_fmaj(C(A2d.T))[0]
    BwT = C(np.asarray(L1_ffn_Bw).transpose(0, 2, 1).reshape(128, D))
    bw_f = prep_w_fmaj(BwT)
    bb_in = C(np.asarray(L1_ffn_b_base).reshape(ND, 128).T)
    e16 = np.zeros((8, 128), np.float32)
    for e in range(8):
        e16[e, e * 16:(e + 1) * 16] = 2.0
    e16 = e16.astype(BD)
    l1f_w = dict(wbase_f=wbase_f, rw_in=rw_in, a_f=a_f, bw_f=bw_f,
                 bb_in=bb_in, e16_in=e16)
    swq_f = prep_w_fmaj(C(np.asarray(L1_attn_Wq).T))
    swk_f = prep_w_fmaj(C(np.asarray(L1_attn_Wk).T))
    swv_d = prep_w_dmaj(C(np.asarray(L1_attn_Wv).T))
    maps = []
    for c in range(NCORES):
        gsel = np.zeros((128, 16), np.float32)
        gsel[:, 0:8] = (np.arange(8)[None, :] < c)
        gsel[:, 8:16] = 1.0 - gsel[:, 0:8]
        maps.append(dict(common, x_in=xs[c], qt_d=rA[c]["qt_d"],
                         kt_d=rA[c]["kt_d"], kh_d=rA[c]["kh_d"],
                         vt_d=rA[c]["vt_d"], fac_d=rA[c]["fac_d"],
                         td_all=td_all, gsel_in=gsel, gw_in=gw_in,
                         wo_f=wo_f, trimask_in=trimask,
                         am_in=rA[c]["am_out"], qfc_in=rA[c]["qfc_out"],
                         qproj_in=rA[c]["qproj_out"],
                         swq_f=swq_f, swk_f=swk_f, swv_d=swv_d, **l1f_w))
    rB = _run(nc_B, maps)
    xs = [rB[c]["x_out"] for c in range(NCORES)]

    nc_C = _get("C", build_launch_C)
    swo_f = prep_w_fmaj(C(np.asarray(L1_attn_Wo).T))
    dmask = np.triu(np.ones((128, 128))).astype(BD)
    fmask = np.tril(np.ones((128, 128)), -1).astype(BD)
    kv0 = rB[0]["kv_out"]
    ksink = C(kv0[:, [kvh * T + j for kvh in range(NKV) for j in range(4)]])
    vsink = C(kv0[0:4, NKV * T:NKV * T + 512])
    zkv = np.zeros_like(kv0)
    maps = []
    for c in range(NCORES):
        prev = rB[c - 1]["kv_out"] if c >= 1 else zkv
        own = rB[c]["kv_out"]
        kctx = np.concatenate(
            [np.concatenate([prev[:, kvh * T:(kvh + 1) * T],
                             own[:, kvh * T:(kvh + 1) * T]], axis=1)
             for kvh in range(NKV)], axis=1)
        vctx = np.concatenate([prev[:, NKV * T:], own[:, NKV * T:]], axis=1)
        smask = np.zeros((4, T), np.float32)
        if c == 1:
            smask[:, :] = (np.arange(T)[None, :] >= np.arange(4)[:, None])
        elif c >= 2:
            smask[:, :] = 1.0
        nsp = np.zeros((1, T), np.float32)
        if c == 0:
            nsp[0, :] = 1023.0 - np.arange(T)
        maps.append(dict(common, x_in=xs[c], qt_d=rB[c]["qt_out"],
                         kctx_in=C(kctx), vctx_in=C(vctx), ksink_in=ksink,
                         vsink_in=vsink, dmask_in=dmask, fmask_in=fmask,
                         smask_in=smask.astype(BD), nsp_in=nsp, wo_f=swo_f,
                         **l1f_w))
    rC = _run(nc_C, maps)
    xs = [rC[c]["x_out"] for c in range(NCORES)]

    return np.concatenate(xs, axis=0)[None, :, :]
